# version 13
# speedup vs baseline: 1.3596x; 1.1527x over previous
import numpy as np
import ml_dtypes
import concourse.bass as bass
import concourse.mybir as mybir
from concourse.bass_utils import run_bass_kernel_spmd
from contextlib import ExitStack

F32 = mybir.dt.float32
BF16 = mybir.dt.bfloat16
AL = mybir.AluOpType
ACT = mybir.ActivationFunctionType
AX = mybir.AxisListType

L = 4096
DM = 1024
DC = 8
S = 16
R = 64
H = 4096
HC = 32
TC = 1024
NTC = L // TC
NTI = L // 128
TPC = TC // 128
QP = 127.0
EPS = 1e-5
RND = float(1.5 * 2.0 ** 23)

DEBUG = False

ENGINES = ("sync", "scalar", "vector", "tensor", "gpsimd")


class Buf:
    __slots__ = ("writer", "readers", "name")

    def __init__(self, name=""):
        self.writer = None
        self.readers = []
        self.name = name


class Item:
    __slots__ = ("eng", "fn", "deps", "marked", "ord", "is_dma", "idx", "dmacnt", "nodrain",
                 "stream", "stream_ord")

    def __init__(self, eng, fn, deps, is_dma):
        self.eng = eng
        self.fn = fn
        self.deps = deps
        self.marked = False
        self.ord = None
        self.is_dma = is_dma
        self.idx = None
        self.dmacnt = None
        self.nodrain = False
        self.stream = None
        self.stream_ord = None


class Prog:
    def __init__(self):
        self.items = {e: [] for e in ENGINES}
        self.bar = {e: None for e in ENGINES}
        self.dmac = {e: 0 for e in ENGINES}

    STREAMS = ("wgt", "hrw", "act", "hst", "out")

    def emit(self, eng, fn, reads=(), writes=(), deps=(), dma=False, nodrain=False, stream=None):
        dd = []
        for b in reads:
            if b.writer is not None:
                dd.append(b.writer)
        for b in writes:
            if b.writer is not None:
                dd.append(b.writer)
            dd.extend(b.readers)
        dd.extend(deps)
        if self.bar[eng] is not None:
            dd.append(self.bar[eng])
        it = Item(eng, fn, dd, dma)
        it.nodrain = nodrain
        it.idx = len(self.items[eng])
        it.dmacnt = dict(self.dmac)
        if dma:
            if stream is not None:
                it.stream = stream
                cnt = getattr(self, "_scnt", None)
                if cnt is None:
                    cnt = self._scnt = {k: 0 for k in Prog.STREAMS}
                cnt[stream] += 1
                it.stream_ord = cnt[stream]
            else:
                self.dmac[eng] += 1
        self.items[eng].append(it)
        for b in reads:
            b.readers.append(it)
        for b in writes:
            b.writer = it
            b.readers = []
        for d in dd:
            if d.eng != eng or d.is_dma:
                d.marked = True
        return it

    def barrier(self):
        lasts = [self.items[e][-1] for e in ENGINES if self.items[e]]
        newbar = {}
        for e in ENGINES:
            it = Item(e, lambda eng: eng.nop(), list(lasts), False)
            it.idx = len(self.items[e])
            it.dmacnt = dict(self.dmac)
            self.items[e].append(it)
            newbar[e] = it
        for d in lasts:
            d.marked = True
        for e in ENGINES:
            self.bar[e] = newbar[e]

    def finalize(self):
        for e in ENGINES:
            o = 0
            do = 0
            for it in self.items[e]:
                if it.is_dma:
                    do += 1
                    it.ord = do
                    it.marked = True
                elif it.marked:
                    o += 1
                    it.ord = o
        self.ninstr = {e: len(self.items[e]) for e in ENGINES}

    FENCE = 12

    def replay(self, eng_name, eng_obj, esem, dsem, ssem):
        waited = {}
        ndma = 0
        last_drain = -1
        need_drain_engines = eng_name in ("vector", "gpsimd")
        for it in self.items[eng_name]:
            if need_drain_engines and not it.nodrain:
                for d in it.deps:
                    if (not d.is_dma) and d.eng == eng_name and d.idx > last_drain:
                        eng_obj.drain()
                        last_drain = it.idx - 1
                        break
            for d in it.deps:
                if d.is_dma:
                    if d.stream is not None:
                        sem = ssem[d.stream]
                        val = 16 * d.stream_ord
                    else:
                        sem = dsem[d.eng]
                        val = 16 * it.dmacnt[d.eng]
                elif d.eng == eng_name:
                    continue
                else:
                    if d.ord is None:
                        continue
                    sem = esem[d.eng]
                    val = d.ord
                key = id(sem)
                if waited.get(key, -1) < val:
                    eng_obj.wait_ge(sem, val)
                    waited[key] = val
            if it.is_dma and it.stream is None:
                ndma += 1
                if ndma > Prog.FENCE:
                    val = 16 * (ndma - Prog.FENCE)
                    key = id(dsem[eng_name])
                    if waited.get(key, -1) < val:
                        eng_obj.wait_ge(dsem[eng_name], val)
                        waited[key] = val
            inst = it.fn(eng_obj)
            if it.is_dma:
                if it.stream is not None:
                    inst.then_inc(ssem[it.stream], 16)
                else:
                    inst.then_inc(dsem[it.eng], 16)
            elif it.marked:
                inst.then_inc(esem[it.eng], 1)


def build_nc():
    nc = bass.Bass("TRN2", target_bir_lowering=False, debug=False)

    dx = nc.dram_tensor("x_in", [L, DM], F32, kind="ExternalInput").ap()
    dAlog = nc.dram_tensor("A_log", [DM, S], F32, kind="ExternalInput").ap()
    dD = nc.dram_tensor("D_vec", [DM], F32, kind="ExternalInput").ap()
    dWx = nc.dram_tensor("Wx", [96, DM], F32, kind="ExternalInput").ap()
    ddtW = nc.dram_tensor("dt_W", [DM, R], F32, kind="ExternalInput").ap()
    ddtb = nc.dram_tensor("dt_b", [DM], F32, kind="ExternalInput").ap()
    dw1 = nc.dram_tensor("w1", [H, DM], F32, kind="ExternalInput").ap()
    db1 = nc.dram_tensor("b1", [H], F32, kind="ExternalInput").ap()
    dw2 = nc.dram_tensor("w2", [DM, H], F32, kind="ExternalInput").ap()
    db2 = nc.dram_tensor("b2", [DM], F32, kind="ExternalInput").ap()
    didf = nc.dram_tensor("ident_f", [128, 128], F32, kind="ExternalInput").ap()
    didb = nc.dram_tensor("ident_b", [128, 128], BF16, kind="ExternalInput").ap()
    dones = nc.dram_tensor("ones_b", [1, 128], BF16, kind="ExternalInput").ap()
    donesf = nc.dram_tensor("ones_f", [1, 128], F32, kind="ExternalInput").ap()
    doh32 = nc.dram_tensor("onehot32", [32, 32 * 128], BF16, kind="ExternalInput").ap()

    dout = nc.dram_tensor("out", [L, DM], F32, kind="ExternalOutput").ap()
    dout_view = dout.rearrange("(n s p) d -> n p s d", s=4, p=128)

    dxa = nc.dram_tensor("xa_hbm", [L, DM], F32)
    dx1 = nc.dram_tensor("x1_hbm", [L, DM], F32)
    dh = nc.dram_tensor("h_hbm", [8, HC, 128, 512], BF16)
    dw1qT = nc.dram_tensor("w1qT_hbm", [HC, 128, DC, 128], BF16)
    dw2qT = nc.dram_tensor("w2qT_hbm", [DC, HC, 128, 128], BF16)
    cc1_in = nc.dram_tensor("cc1_in", [1, 8], F32)
    cc1_out = nc.dram_tensor("cc1_out", [1, 8], F32)
    cc2_in = nc.dram_tensor("cc2_in", [1, 8], F32)
    cc2_out = nc.dram_tensor("cc2_out", [1, 8], F32)

    dbg = {}
    if DEBUG:
        dbg["d_x1"] = nc.dram_tensor("dbg_x1", [L, DM], F32, kind="ExternalOutput").ap()
        dbg["d_scal"] = nc.dram_tensor("dbg_scal", [1, 16], F32, kind="ExternalOutput").ap()
        dbg["d_zq"] = nc.dram_tensor("dbg_zq", [128, DC * 512], F32, kind="ExternalOutput").ap()
        dbg["d_xa"] = nc.dram_tensor("dbg_xa", [L, DM], F32, kind="ExternalOutput").ap()
        dbg["d_dt"] = nc.dram_tensor("dbg_dt", [128, DC * TC], F32, kind="ExternalOutput").ap()
        dbg["d_yb"] = nc.dram_tensor("dbg_yb", [128, DC * TC], F32, kind="ExternalOutput").ap()
        dbg["d_Bb"] = nc.dram_tensor("dbg_Bb", [128, S * TC], F32, kind="ExternalOutput").ap()
        dbg["d_z0"] = nc.dram_tensor("dbg_z0", [128, DM], F32, kind="ExternalOutput").ap()
        dbg["d_small"] = nc.dram_tensor("dbg_small", [128, 8], F32, kind="ExternalOutput").ap()
        dbg["d_scal2"] = nc.dram_tensor("dbg_scal2", [1, 24], F32, kind="ExternalOutput").ap()

    es = ExitStack()
    sb = lambda n_, s_, d_: es.enter_context(nc.sbuf_tensor(n_, s_, d_))
    pst = lambda n_, s_, d_: es.enter_context(nc.psum_tensor(n_, s_, d_))

    identf = sb("identf", [128, 128], F32)
    identb = sb("identb", [128, 128], BF16)
    ones1 = sb("ones1", [1, 128], BF16)
    ones1f = sb("ones1f", [1, 128], F32)
    oh32 = sb("oh32", [32, 32 * 128], BF16)
    A_neg = sb("A_neg", [128, DC * S], F32)
    dtb_c = sb("dtb_c", [128, DC], F32)
    Dcol_c = sb("Dcol_c", [128, DC], F32)
    b1_c = sb("b1_c", [128, HC], F32)
    b2_c = sb("b2_c", [128, DC], F32)
    b1s_c = sb("b1s_c", [128, HC], F32)
    b2s_c = sb("b2s_c", [128, DC], F32)
    state = sb("state", [128, DC * S], F32)
    negm2_a = sb("negm2_a", [128, NTI], F32)
    r2_a = sb("r2_a", [128, NTI], F32)
    rq_a = sb("rq_a", [128, NTI], F32)
    bq_a = sb("bq_a", [128, NTI], F32)
    maxz_r = sb("maxz_r", [128, 2], F32)
    maxh_r = sb("maxh_r", [128, 2], F32)
    scal = sb("scal", [1, 24], F32)
    cc_sb = sb("cc_sb", [1, 8], F32)
    sc_bc = sb("sc_bc", [128, 8], F32)
    WxT = sb("WxT", [128, DC * 96], BF16)
    dtWT = sb("dtWT", [64, DC * 128], BF16)

    z0Tb = sb("z0Tb", [128, DC * TC], BF16)
    dt_t = sb("dt_t", [128, DC * TC], BF16)
    dtx_t = sb("dtx_t", [128, DC * TC], BF16)
    Bb = sb("Bb", [128, S * TC], BF16)
    Cb = sb("Cb", [128, S * TC], BF16)
    dtr_b = sb("dtr_b", [64, TC], BF16)
    BCb = sb("BCb", [32, TC], BF16)
    yb = sb("yb", [128, DC * TC], BF16)
    dA_b = sb("dA_b", [128, 2 * TC], F32)
    u_b = sb("u_b", [128, 2 * TC], BF16)
    h_b = sb("h_b", [128, 2 * TC], F32)
    hc_b = sb("hc_b", [128, 2 * TC], BF16)
    xt_b = sb("xt_b", [128, 2 * DM], F32)
    z0_b = sb("z0_b", [128, 2 * DM], F32)
    scrA = sb("scrA", [128, DM], F32)
    scrB = sb("scrB", [128, DM], F32)
    x1t = sb("x1t", [128, DM], F32)
    spt = scrA
    spa = scrB
    spe_ = scrB
    spl = x1t
    small = sb("small", [128, 8], F32)

    wchunk = z0_b
    wq_t = dt_t
    zq_nt = z0Tb
    w1s = dtx_t
    hq_nt_a = Bb
    hq_nt_b = Cb
    hin_b = dtx_t
    w2s = dt_t
    hsb = dt_t
    x2f = z0_b
    x1q = z0_b
    outt = x1t

    pt_a = pst("pt_a", [128, 512], F32)
    pt_b = pst("pt_b", [128, 512], F32)
    pt_c = pst("pt_c", [128, 512], F32)
    pt_d = pst("pt_d", [128, 512], F32)
    y_ps = pst("y_ps", [128, 1024], F32)
    pt_ab = pst("pt_ab", [128, 1024], BF16)
    pt_bb = pst("pt_bb", [128, 1024], BF16)

    esem = {}
    dsem = {}
    for e in ENGINES:
        esem[e] = es.enter_context(nc.semaphore("sem_" + e))
        dsem[e] = es.enter_context(nc.semaphore("dma_" + e))
    ssem = {k: es.enter_context(nc.semaphore("sdma_" + k)) for k in Prog.STREAMS}

    P = Prog()
    nb = lambda n: Buf(n)

    B_identf = nb("identf"); B_identb = nb("identb"); B_ones = nb("ones")
    B_Aneg = nb("A_neg"); B_dtb = nb("dtb"); B_b1 = nb("b1"); B_b2 = nb("b2")
    B_b1s = nb("b1s"); B_b2s = nb("b2s")
    B_Dbc = nb("Dbc"); B_Dcol = nb("Dcol"); B_Ddiag = nb("Ddiag")
    B_state = [nb(f"st{i}") for i in range(DC * S)]
    B_negm2 = nb("negm2"); B_r2 = nb("r2"); B_rq = nb("rq"); B_bq = nb("bq")
    B_maxz = nb("maxz"); B_maxh = nb("maxh")
    B_scal = nb("scal"); B_ccsb = nb("ccsb"); B_scbc = nb("scbc")
    B_WxT = nb("WxT"); B_dtWT = nb("dtWT")
    B_z0Tb = [nb(f"z0Tb{d}") for d in range(DC)]
    B_dt = [nb(f"dt{d}") for d in range(DC)]
    B_dtx = [nb(f"dtx{d}") for d in range(DC)]
    B_Bb = [nb(f"Bb{s}") for s in range(S)]
    B_Cb = [nb(f"Cb{s}") for s in range(S)]
    B_dtr = nb("dtr"); B_BCb = nb("BCb")
    B_yb = [nb(f"yb{d}") for d in range(DC)]
    B_dA = [nb("dA0"), nb("dA1")]
    B_u = [nb("u0"), nb("u1")]
    B_h = [nb("h0"), nb("h1")]
    B_hc = [nb("hc0"), nb("hc1")]
    B_xt = [nb("xt0"), nb("xt1")]
    B_z0 = [nb(f"z0_{i}") for i in range(2)]
    B_scrA = nb("scrA"); B_scrB = nb("scrB")
    B_x1t = nb("x1t")
    B_spt = [B_scrA, B_scrA]; B_spa = B_scrB; B_spa2 = B_scrB
    B_spe = nb('spe0'); B_spe2 = nb('spe1'); B_spl2 = nb('spl0'); B_spl3 = nb('spl1')
    B_small = nb("small")
    B_pa = nb("pt_a"); B_pb = nb("pt_b"); B_pc = nb("pt_c"); B_pd = nb("pt_d")
    B_yps = nb("yps"); B_pab = nb("pt_ab"); B_pbb = nb("pt_bb")
    B_wch = [nb("wch0"), nb("wch1")]; B_wqt = nb("wq_t")
    B_zqnt = nb("zq_nt"); B_w1s = [nb(f"w1s{i}") for i in range(4)]
    B_hqnt = [nb("hq0"), nb("hq1")]; B_hin = [nb("hin0"), nb("hin1")]
    B_w2s = [nb("w2s0"), nb("w2s1")]; B_hsb = [nb("hsb0"), nb("hsb1")]
    B_x2f = [nb("x2f0"), nb("x2f1")]; B_x1q = [nb("x1q0"), nb("x1q1")]
    B_outt = [nb("outt0"), nb("outt1")]
    B_dxa = [nb(f"dxa{i}") for i in range(NTI)]
    B_dx1 = [nb(f"dx1{i}") for i in range(NTI)]
    B_dh = [[nb(f"dh{a}_{b}") for b in range(HC)] for a in range(8)]
    B_dw1qT = [nb(f"dw1qT{a}") for a in range(HC)]
    B_dw2qT = [nb(f"dw2qT{a}") for a in range(DC)]
    B_cc1i = nb("cc1i"); B_cc1o = nb("cc1o"); B_cc2i = nb("cc2i"); B_cc2o = nb("cc2o")

    E = P.emit

    def _load_combined_table(g):
        from concourse.hw_specs import get_activation_tables
        for idx, (nm, funcs) in enumerate(get_activation_tables(nc.m.arch).items()):
            if ACT.Exp in funcs and ACT.Ln in funcs:
                return g.add_instruction(mybir.InstLoadActFuncSet(
                    name=nc.get_next_instruction_name(), act_func_set_id=idx, ins=[], outs=[]))
        raise ValueError("no combined exp+ln table")
    E("scalar", _load_combined_table)
    E("sync", lambda g: g.dma_start(out=identf[:], in_=didf), writes=[B_identf], dma=True)
    E("sync", lambda g: g.dma_start(out=identb[:], in_=didb), writes=[B_identb], dma=True)
    E("sync", lambda g: g.dma_start(out=ones1[:], in_=dones), writes=[B_ones], dma=True)
    E("sync", lambda g: g.dma_start(out=ones1f[:], in_=donesf), writes=[B_ones], dma=True)
    E("sync", lambda g: g.dma_start(out=oh32[:], in_=doh32), writes=[B_ones], dma=True)
    for dc in range(DC):
        E("sync", lambda g, dc=dc: g.dma_start(out=scrB[:, dc * S:(dc + 1) * S],
                                               in_=dAlog[dc * 128:(dc + 1) * 128, :]),
          writes=[B_scrB], dma=True)
    E("scalar", lambda g: g.activation(out=scrB[:, 128:256], in_=scrB[:, 0:128], func=ACT.Exp),
      reads=[B_scrB], writes=[B_scrB])
    E("vector", lambda g: g.tensor_scalar_mul(A_neg[:], scrB[:, 128:256], -1.0),
      reads=[B_scrB], writes=[B_Aneg])
    for dc in range(DC):
        E("sync", lambda g, dc=dc: g.dma_start(out=dtb_c[:, dc:dc + 1],
                                               in_=ddtb[dc * 128:(dc + 1) * 128].rearrange("(p o) -> p o", o=1)),
          writes=[B_dtb], dma=True)
        E("sync", lambda g, dc=dc: g.dma_start(out=b2_c[:, dc:dc + 1],
                                               in_=db2[dc * 128:(dc + 1) * 128].rearrange("(p o) -> p o", o=1)),
          writes=[B_b2], dma=True)
    for hcc in range(HC):
        E("sync", lambda g, hcc=hcc: g.dma_start(out=b1_c[:, hcc:hcc + 1],
                                                 in_=db1[hcc * 128:(hcc + 1) * 128].rearrange("(p o) -> p o", o=1)),
          writes=[B_b1], dma=True)
    for dc in range(DC):
        E("sync", lambda g, dc=dc: g.dma_start(out=Dcol_c[:, dc:dc + 1],
                                               in_=dD[dc * 128:(dc + 1) * 128].rearrange("(p o) -> p o", o=1)),
          writes=[B_Dcol], dma=True)

    E("vector", lambda g: g.memset(state[:], 0.0), writes=B_state)
    E("vector", lambda g: g.memset(maxz_r[:], 0.0), writes=[B_maxz])
    E("vector", lambda g: g.memset(maxh_r[:], 0.0), writes=[B_maxh])
    E("sync", lambda g: g.dma_start(out=wchunk[0:96, 0:DM], in_=dWx), writes=[B_wch[0]], dma=True)
    E("vector", lambda g: g.tensor_copy(wq_t[0:96, 0:DM], wchunk[0:96, 0:DM]),
      reads=[B_wch[0]], writes=[B_wqt])
    for dc in range(DC):
        E("tensor", lambda g, dc=dc: g.transpose(pt_ab[0:128, 0:96],
                                                 wq_t[0:96, dc * 128:(dc + 1) * 128],
                                                 identb[0:96, 0:96]),
          reads=[B_wqt, B_identb], writes=[B_pab])
        E("scalar", lambda g, dc=dc: g.activation(out=WxT[:, dc * 96:(dc + 1) * 96],
                                                  in_=pt_ab[0:128, 0:96], func=ACT.Copy),
          reads=[B_pab], writes=[B_WxT])
    for dc in range(DC):
        E("sync", lambda g, dc=dc: g.dma_start(out=wchunk[:, 0:R],
                                               in_=ddtW[dc * 128:(dc + 1) * 128, :]),
          writes=[B_wch[0]], dma=True)
        E("vector", lambda g, dc=dc: g.tensor_copy(wq_t[:, 0:R], wchunk[:, 0:R]),
          reads=[B_wch[0]], writes=[B_wqt])
        E("tensor", lambda g, dc=dc: g.transpose(pt_ab[0:R, 0:128], wq_t[:, 0:R], identb[:]),
          reads=[B_wqt, B_identb], writes=[B_pab])
        E("scalar", lambda g, dc=dc: g.activation(out=dtWT[:, dc * 128:(dc + 1) * 128],
                                                  in_=pt_ab[0:R, 0:128], func=ACT.Copy),
          reads=[B_pab], writes=[B_dtWT])

    def finalize_max(run_tile, run_buf, idx):
        E("tensor", lambda g: g.transpose(pt_a[0:1, 0:128], run_tile, identf[:]),
          reads=[run_buf, B_identf], writes=[B_pa])
        E("vector", lambda g: g.tensor_copy(scrB[0:1, 0:128], pt_a[0:1, 0:128]),
          reads=[B_pa], writes=[B_scrB])
        E("vector", lambda g, idx=idx: g.reduce_max(scal[0:1, idx:idx + 1], scrB[0:1, 0:128], axis=AX.X),
          reads=[B_scrB], writes=[B_scal])

    E("vector", lambda g: g.memset(small[:, 0:2], 0.0), writes=[B_small])
    for ch in range(HC):
        bw = B_wch[ch % 2]
        pp_ = ch % 2
        E("sync", lambda g, ch=ch: g.dma_start(out=wchunk[:, (ch % 2) * DM:(ch % 2 + 1) * DM],
                                               in_=dw1[ch * 128:(ch + 1) * 128, :]),
          writes=[bw], dma=True)
        E("vector", lambda g, ch=ch: g.reduce_max(small[:, 4:5],
                                                  wchunk[:, (ch % 2) * DM:(ch % 2 + 1) * DM],
                                                  axis=AX.X, apply_absolute_value=True),
          reads=[bw], writes=[B_small])
        E("vector", lambda g, pp_=pp_: g.tensor_max(small[:, 1 - pp_:2 - pp_],
                                                    small[:, pp_:pp_ + 1], small[:, 4:5]),
          reads=[B_small], writes=[B_small])
    finalize_max(small[:, (HC % 2):(HC % 2) + 1], B_small, 0)
    E("vector", lambda g: g.memset(small[:, 2:3], 0.0), writes=[B_small])
    for ch in range(DC):
        bw = B_wch[ch % 2]
        for half in range(2):
            E("sync", lambda g, ch=ch, half=half: g.dma_start(
                out=wchunk[:, (ch % 2) * DM:(ch % 2) * DM + DM],
                in_=dw2[ch * 128:(ch + 1) * 128, half * 2048:half * 2048 + DM]),
              writes=[bw], dma=True)
            E("vector", lambda g, ch=ch: g.reduce_max(small[:, 3:4],
                                                      wchunk[:, (ch % 2) * DM:(ch % 2 + 1) * DM],
                                                      axis=AX.X, apply_absolute_value=True),
              reads=[bw], writes=[B_small])
            E("vector", lambda g: g.tensor_max(small[:, 2:3], small[:, 2:3], small[:, 3:4]),
              reads=[B_small], writes=[B_small])
            E("sync", lambda g, ch=ch, half=half: g.dma_start(
                out=wchunk[:, (ch % 2) * DM:(ch % 2) * DM + DM],
                in_=dw2[ch * 128:(ch + 1) * 128, half * 2048 + DM:half * 2048 + 2 * DM]),
              writes=[bw], dma=True)
            E("vector", lambda g, ch=ch: g.reduce_max(small[:, 3:4],
                                                      wchunk[:, (ch % 2) * DM:(ch % 2 + 1) * DM],
                                                      axis=AX.X, apply_absolute_value=True),
              reads=[bw], writes=[B_small])
            E("vector", lambda g: g.tensor_max(small[:, 2:3], small[:, 2:3], small[:, 3:4]),
              reads=[B_small], writes=[B_small])
    finalize_max(small[:, 2:3], B_small, 1)

    def make_scale(src_idx, s_idx, inv_idx):
        E("vector", lambda g, src_idx=src_idx, s_idx=s_idx: g.tensor_scalar(
            scal[0:1, s_idx:s_idx + 1], scal[0:1, src_idx:src_idx + 1],
            1e-6, 1.0 / QP, AL.max, AL.mult),
          reads=[B_scal], writes=[B_scal])
        E("scalar", lambda g, s_idx=s_idx: g.activation(out=scal[0:1, 16:17],
                                                        in_=scal[0:1, s_idx:s_idx + 1], func=ACT.Ln),
          reads=[B_scal], writes=[B_scal])
        E("scalar", lambda g, inv_idx=inv_idx: g.activation(out=scal[0:1, inv_idx:inv_idx + 1],
                                                            in_=scal[0:1, 16:17], func=ACT.Exp,
                                                            scale=-1.0),
          reads=[B_scal], writes=[B_scal])

    make_scale(0, 2, 4)
    make_scale(1, 3, 5)

    def bcast_scalar(idx, dst_col):
        E("tensor", lambda g, idx=idx: g.matmul(pt_a[:, 0:1], ones1f[:], scal[0:1, idx:idx + 1],
                                                start=True, stop=True),
          reads=[B_scal, B_ones], writes=[B_pa])
        E("scalar", lambda g, dst_col=dst_col: g.activation(out=sc_bc[:, dst_col:dst_col + 1],
                                                            in_=pt_a[:, 0:1], func=ACT.Copy),
          reads=[B_pa], writes=[B_scbc])

    bcast_scalar(4, 0)
    bcast_scalar(5, 1)

    for ch in range(HC):
        bw = B_wch[ch % 2]
        E("sync", lambda g, ch=ch: g.dma_start(out=wchunk[:, (ch % 2) * DM:(ch % 2 + 1) * DM],
                                               in_=dw1[ch * 128:(ch + 1) * 128, :]),
          writes=[bw], dma=True)
        E("scalar", lambda g, ch=ch: g.activation(out=scrB[:], in_=wchunk[:, (ch % 2) * DM:(ch % 2 + 1) * DM],
                                                  func=ACT.Identity, scale=sc_bc[:, 0:1]),
          reads=[bw, B_scbc], writes=[B_scrB])
        E("vector", lambda g: g.tensor_scalar(wq_t[:, 0:DM], scrB[:], RND, RND, AL.add, AL.subtract),
          reads=[B_scrB], writes=[B_wqt])
        for dc in range(DC):
            E("tensor", lambda g, dc=dc: g.transpose(pt_ab[:, (dc % 8) * 128:(dc % 8 + 1) * 128],
                                                     wq_t[:, dc * 128:(dc + 1) * 128], identb[:]),
              reads=[B_wqt, B_identb], writes=[B_pab])
            E("scalar", lambda g, dc=dc: g.activation(out=zq_nt[:, dc * 128:(dc + 1) * 128],
                                                      in_=pt_ab[:, (dc % 8) * 128:(dc % 8 + 1) * 128],
                                                      func=ACT.Copy),
              reads=[B_pab], writes=[B_zqnt])
        E("gpsimd", lambda g, ch=ch: g.dma_start(out=dw1qT.ap()[ch], in_=zq_nt[:, 0:DC * 128].rearrange("p (d c) -> p d c", d=DC)),
          reads=[B_zqnt], writes=[B_dw1qT[ch]], dma=True)
    for ch in range(DC):
        for q in range(4):
            bw = B_wch[(ch * 4 + q) % 2]
            E("sync", lambda g, ch=ch, q=q: g.dma_start(
                out=wchunk[:, ((ch * 4 + q) % 2) * DM:((ch * 4 + q) % 2 + 1) * DM],
                in_=dw2[ch * 128:(ch + 1) * 128, q * DM:(q + 1) * DM]),
              writes=[bw], dma=True)
            E("scalar", lambda g, ch=ch, q=q: g.activation(
                out=scrB[:], in_=wchunk[:, ((ch * 4 + q) % 2) * DM:((ch * 4 + q) % 2 + 1) * DM],
                func=ACT.Identity, scale=sc_bc[:, 1:2]),
              reads=[bw, B_scbc], writes=[B_scrB])
            E("vector", lambda g: g.tensor_scalar(wq_t[:, 0:DM], scrB[:], RND, RND, AL.add, AL.subtract),
              reads=[B_scrB], writes=[B_wqt])
            for k in range(8):
                kc = q * 8 + k
                E("tensor", lambda g, k=k: g.transpose(pt_ab[:, k * 128:(k + 1) * 128],
                                                       wq_t[:, k * 128:(k + 1) * 128], identb[:]),
                  reads=[B_wqt, B_identb], writes=[B_pab])
                E("scalar", lambda g, k=k: g.activation(out=zq_nt[:, k * 128:(k + 1) * 128],
                                                        in_=pt_ab[:, k * 128:(k + 1) * 128],
                                                        func=ACT.Copy),
                  reads=[B_pab], writes=[B_zqnt])
            E("gpsimd", lambda g, ch=ch, q=q: g.dma_start(
                out=dw2qT.ap()[ch, q * 8:(q + 1) * 8].rearrange("k p c -> p k c"),
                in_=zq_nt[:, 0:8 * 128].rearrange("p (k c) -> p k c", k=8)),
              reads=[B_zqnt], writes=[B_dw2qT[ch]], dma=True)

    if DEBUG:
        E("gpsimd", lambda g: g.dma_start(out=dbg["d_scal2"], in_=scal[:]),
          reads=[B_scal], dma=True)
    P.barrier()

    def emit_A_proj(tc, E_):
            for tib in range(4):
                for tis in range(2):
                    ti = tc * TPC + tib * 2 + tis
                    bx = B_xt[tis % 2]
                    bz = B_z0[tis % 2]
                    E_("sync", lambda g, ti=ti, tis=tis: g.dma_start(
                        out=xt_b[:, (tis % 2) * DM:(tis % 2 + 1) * DM],
                        in_=dx[ti * 128:(ti + 1) * 128, :]),
                      writes=[bx], dma=True)
                    E_("scalar", lambda g, tis=tis: g.activation(
                        out=scrA[:], in_=xt_b[:, (tis % 2) * DM:(tis % 2 + 1) * DM],
                        func=ACT.Copy, accum_out=small[:, 0:1]),
                      reads=[bx], writes=[B_scrA, B_small])
                    E_("scalar", lambda g, tis=tis: g.activation(
                        out=scrA[:], in_=xt_b[:, (tis % 2) * DM:(tis % 2 + 1) * DM],
                        func=ACT.Square, accum_out=small[:, 1:2]),
                      reads=[bx], writes=[B_scrA, B_small])
                    E_("vector", lambda g: g.tensor_scalar_mul(small[:, 2:3], small[:, 0:1], -1.0 / DM),
                      reads=[B_small], writes=[B_small])
                    E_("scalar", lambda g: g.activation(out=small[:, 3:4], in_=small[:, 2:3],
                                                       func=ACT.Square),
                      reads=[B_small], writes=[B_small])
                    E_("vector", lambda g: g.tensor_scalar(small[:, 4:5], small[:, 1:2], 1.0 / DM, EPS,
                                                          AL.mult, AL.add),
                      reads=[B_small], writes=[B_small])
                    E_("vector", lambda g: g.tensor_sub(small[:, 5:6], small[:, 4:5], small[:, 3:4]),
                      reads=[B_small], writes=[B_small])
                    E_("scalar", lambda g: g.activation(out=small[:, 6:7], in_=small[:, 5:6], func=ACT.Ln),
                      reads=[B_small], writes=[B_small])
                    E_("scalar", lambda g: g.activation(out=small[:, 7:8], in_=small[:, 6:7], func=ACT.Exp,
                                                       scale=-0.5),
                      reads=[B_small], writes=[B_small])
                    E_("vector", lambda g, tis=tis: g.tensor_scalar(
                        z0_b[:, tis * DM:(tis + 1) * DM], xt_b[:, (tis % 2) * DM:(tis % 2 + 1) * DM],
                        small[:, 2:3], small[:, 7:8], AL.add, AL.mult),
                      reads=[bx, B_small], writes=[bz])
                    if DEBUG and ti == 0 and tis == 0:
                        E_("gpsimd", lambda g: g.dma_start(out=dbg["d_z0"], in_=z0_b[:, 0:DM]),
                          reads=[bz], dma=True)
                        E_("gpsimd", lambda g: g.dma_start(out=dbg["d_small"], in_=small[:]),
                          reads=[B_small], dma=True)
                for dc in range(DC):
                    for tis in range(2):
                        E_("tensor", lambda g, dc=dc, tis=tis: g.transpose(
                            pt_b[:, tis * 128:(tis + 1) * 128],
                            z0_b[:, tis * DM + dc * 128:tis * DM + (dc + 1) * 128], identf[:]),
                          reads=[B_z0[tis], B_identf], writes=[B_pb])
                    E_("scalar", lambda g, dc=dc, tib=tib: g.activation(
                        out=z0Tb[:, dc * TC + tib * 256:dc * TC + (tib + 1) * 256],
                        in_=pt_b[:, 0:256], func=ACT.Copy),
                      reads=[B_pb], writes=[B_z0Tb[dc]])
            for ntb in range(2):
                for dc in range(DC):
                    E_("tensor", lambda g, ntb=ntb, dc=dc: g.matmul(
                        pt_b[0:96, 0:512], WxT[:, dc * 96:(dc + 1) * 96],
                        z0Tb[:, dc * TC + ntb * 512:dc * TC + (ntb + 1) * 512],
                        start=(dc == 0), stop=(dc == DC - 1)),
                      reads=[B_WxT, B_z0Tb[dc]], writes=[B_pb])
                E_("scalar", lambda g, ntb=ntb: g.activation(out=dtr_b[:, ntb * 512:(ntb + 1) * 512],
                                                            in_=pt_b[0:64, 0:512], func=ACT.Copy),
                  reads=[B_pb], writes=[B_dtr])
                E_("scalar", lambda g, ntb=ntb: g.activation(out=BCb[:, ntb * 512:(ntb + 1) * 512],
                                                            in_=pt_b[64:96, 0:512], func=ACT.Copy),
                  reads=[B_pb], writes=[B_BCb])


    emit_A_proj(0, E)
    for tc in range(NTC):
        for dc in range(DC):
            for ntb in range(2):
                pdt = (pt_c, pt_b)[ntb % 2]
                bpdt = (B_pc, B_pb)[ntb % 2]
                E("tensor", lambda g, dc=dc, ntb=ntb, pdt=pdt: g.matmul(
                    pdt[:, 0:512], dtWT[:, dc * 128:(dc + 1) * 128],
                    dtr_b[:, ntb * 512:(ntb + 1) * 512], start=True, stop=True),
                  reads=[B_dtWT, B_dtr], writes=[bpdt])
                p_ = ntb % 2
                bspt = B_spt[p_]
                bspa = (B_spa, B_spa2)[p_]
                bspe = (B_spe, B_spe2)[p_]
                bspl = (B_spl2, B_spl3)[p_]
                tspa = scrB[:, p_ * 512:(p_ + 1) * 512]
                tspe = z0_b[:, p_ * 512:(p_ + 1) * 512]
                tspl = z0_b[:, 1024 + p_ * 512:1024 + (p_ + 1) * 512]
                pdt = (pt_c, pt_b)[p_]
                bpdt = (B_pc, B_pb)[p_]
                E("scalar", lambda g, dc=dc, p_=p_, pdt=pdt: g.activation(
                    out=spt[:, p_ * 512:(p_ + 1) * 512], in_=pdt[:, 0:512],
                    func=ACT.Identity, bias=dtb_c[:, dc:dc + 1]),
                  reads=[bpdt, B_dtb], writes=[bspt])
                E("vector", lambda g, p_=p_, tspa=tspa: g.scalar_tensor_tensor(
                    tspa, spt[:, p_ * 512:(p_ + 1) * 512], -1.0,
                    spt[:, p_ * 512:(p_ + 1) * 512], AL.mult, AL.max),
                  reads=[bspt], writes=[bspa])
                E("scalar", lambda g, tspa=tspa, tspe=tspe: g.activation(
                    out=tspe, in_=tspa, func=ACT.Exp, scale=-1.0),
                  reads=[bspa], writes=[bspe])
                E("scalar", lambda g, tspe=tspe, tspl=tspl: g.activation(
                    out=tspl, in_=tspe, func=ACT.Ln, bias=1.0),
                  reads=[bspe], writes=[bspl])
                E("vector", lambda g, dc=dc, ntb=ntb, p_=p_, tspl=tspl: g.scalar_tensor_tensor(
                    dt_t[:, dc * TC + ntb * 512:dc * TC + (ntb + 1) * 512],
                    spt[:, p_ * 512:(p_ + 1) * 512], 0.0, tspl, AL.max, AL.add),
                  reads=[bspt, bspl], writes=[B_dt[dc]])
            E("vector", lambda g, dc=dc: g.tensor_mul(dtx_t[:, dc * TC:(dc + 1) * TC],
                                                      dt_t[:, dc * TC:(dc + 1) * TC],
                                                      z0Tb[:, dc * TC:(dc + 1) * TC]),
              reads=[B_dt[dc], B_z0Tb[dc]], writes=[B_dtx[dc]])
        for s in range(S):
            for ntb in range(2):
                pbc = (pt_c, pt_d)[ntb]
                bbc = (B_pc, B_pd)[ntb]
                E("tensor", lambda g, s=s, ntb=ntb, pbc=pbc: g.matmul(
                    pbc[:, 0:512], oh32[:, s * 128:(s + 1) * 128],
                    BCb[:, ntb * 512:(ntb + 1) * 512], start=True, stop=True),
                  reads=[B_ones, B_BCb], writes=[bbc])
                E("scalar", lambda g, s=s, ntb=ntb, pbc=pbc: g.activation(
                    out=Bb[:, s * TC + ntb * 512:s * TC + (ntb + 1) * 512],
                    in_=pbc[:, 0:512], func=ACT.Copy),
                  reads=[bbc], writes=[B_Bb[s]])
            for ntb in range(2):
                pbc = (pt_c, pt_d)[ntb]
                bbc = (B_pc, B_pd)[ntb]
                E("tensor", lambda g, s=s, ntb=ntb, pbc=pbc: g.matmul(
                    pbc[:, 0:512], oh32[:, (16 + s) * 128:(17 + s) * 128],
                    BCb[:, ntb * 512:(ntb + 1) * 512], start=True, stop=True),
                  reads=[B_ones, B_BCb], writes=[bbc])
                E("scalar", lambda g, s=s, ntb=ntb, pbc=pbc: g.activation(
                    out=Cb[:, s * TC + ntb * 512:s * TC + (ntb + 1) * 512],
                    in_=pbc[:, 0:512], func=ACT.Copy),
                  reads=[bbc], writes=[B_Cb[s]])
        thunks = []
        if tc + 1 < NTC:
            emit_A_proj(tc + 1, lambda *a, **k: thunks.append((a, k)))
        tpos = [0]

        def drip(n):
            for _ in range(n):
                if tpos[0] < len(thunks):
                    a_, k_ = thunks[tpos[0]]
                    E(*a_, **k_)
                    tpos[0] += 1

        for dc in range(DC):
            E("vector", lambda g, dc=dc: g.tensor_scalar(
                yb[:, dc * TC:(dc + 1) * TC], z0Tb[:, dc * TC:(dc + 1) * TC],
                Dcol_c[:, dc:dc + 1], None, AL.mult),
              reads=[B_z0Tb[dc], B_Dcol], writes=[B_yb[dc]])
        for dc in range(DC):
            yps = y_ps
            byps = B_yps
            for s in range(S):
                k = s % 2
                st_idx = dc * S + s
                E("scalar", lambda g, dc=dc, s=s, k=k: g.activation(
                    out=dA_b[:, k * TC:(k + 1) * TC], in_=dt_t[:, dc * TC:(dc + 1) * TC],
                    func=ACT.Exp, scale=A_neg[:, dc * S + s:dc * S + s + 1]),
                  reads=[B_dt[dc], B_Aneg], writes=[B_dA[k]])
                E("vector", lambda g, dc=dc, s=s, k=k: g.tensor_mul(
                    u_b[:, k * TC:(k + 1) * TC], dtx_t[:, dc * TC:(dc + 1) * TC],
                    Bb[:, s * TC:(s + 1) * TC]),
                  reads=[B_dtx[dc], B_Bb[s]], writes=[B_u[k]])
                E("vector", lambda g, k=k, st_idx=st_idx: g.tensor_tensor_scan(
                    h_b[:, k * TC:(k + 1) * TC], dA_b[:, k * TC:(k + 1) * TC],
                    u_b[:, k * TC:(k + 1) * TC], state[:, st_idx:st_idx + 1],
                    AL.mult, AL.add),
                  reads=[B_dA[k], B_u[k], B_state[st_idx]], writes=[B_h[k]])
                E("vector", lambda g, k=k, st_idx=st_idx: g.tensor_copy(
                    state[:, st_idx:st_idx + 1], h_b[:, (k + 1) * TC - 1:(k + 1) * TC]),
                  reads=[B_h[k]], writes=[B_state[st_idx]])
                E("vector", lambda g, k=k, s=s: g.tensor_mul(
                    hc_b[:, k * TC:(k + 1) * TC], h_b[:, k * TC:(k + 1) * TC],
                    Cb[:, s * TC:(s + 1) * TC]),
                  reads=[B_h[k], B_Cb[s]], writes=[B_hc[k]])
                for ntb in range(2):
                    E("tensor", lambda g, k=k, ntb=ntb, s=s, yps=yps: g.matmul(
                        yps[:, ntb * 512:(ntb + 1) * 512], identb[:],
                        hc_b[:, k * TC + ntb * 512:k * TC + (ntb + 1) * 512],
                        start=(s == 0), stop=(s == S - 1)),
                      reads=[B_identb, B_hc[k]], writes=[byps])
            drip(2 + len(thunks) // DC)
            for ntb in range(2):
                E("vector", lambda g, dc=dc, ntb=ntb, yps=yps: g.tensor_add(
                    yb[:, dc * TC + ntb * 512:dc * TC + (ntb + 1) * 512],
                    yps[:, ntb * 512:(ntb + 1) * 512],
                    yb[:, dc * TC + ntb * 512:dc * TC + (ntb + 1) * 512]),
                  reads=[byps, B_yb[dc]], writes=[B_yb[dc]])
        drip(len(thunks))
        if DEBUG and tc == 0:
            E("gpsimd", lambda g: g.dma_start(out=dbg["d_dt"], in_=dt_t[:]),
              reads=[B_dt[7]], deps=[P.items["vector"][-1]], dma=True)
            E("gpsimd", lambda g: g.dma_start(out=dbg["d_Bb"], in_=Bb[:]),
              reads=[B_Bb[15]], dma=True)
            E("gpsimd", lambda g: g.dma_start(out=dbg["d_yb"], in_=yb[:]),
              reads=[B_yb[7]], deps=[P.items["scalar"][-1]], dma=True)
        for tloc in range(TPC):
            ti = tc * TPC + tloc
            bx = B_xt[tloc % 2]
            E("sync", lambda g, ti=ti, tloc=tloc: g.dma_start(
                out=xt_b[:, (tloc % 2) * DM:(tloc % 2 + 1) * DM],
                in_=dx[ti * 128:(ti + 1) * 128, :]),
              writes=[bx], dma=True)
            for dc in range(DC):
                pp = (pt_ab, pt_bb)[dc // 4]
                bpp = (B_pab, B_pbb)[dc // 4]
                E("tensor", lambda g, dc=dc, tloc=tloc, pp=pp: g.transpose(
                    pp[:, (dc % 4) * 128:(dc % 4 + 1) * 128],
                    yb[:, dc * TC + tloc * 128:dc * TC + (tloc + 1) * 128], identb[:]),
                  reads=[B_yb[dc], B_identb], writes=[bpp])
            E("vector", lambda g, tloc=tloc: g.tensor_add(
                x1t[:, 0:512], pt_ab[:, 0:512], xt_b[:, (tloc % 2) * DM:(tloc % 2) * DM + 512]),
              reads=[B_pab, bx], writes=[B_x1t])
            E("vector", lambda g, tloc=tloc: g.tensor_add(
                x1t[:, 512:1024], pt_bb[:, 0:512],
                xt_b[:, (tloc % 2) * DM + 512:(tloc % 2 + 1) * DM]),
              reads=[B_pbb, bx], writes=[B_x1t])
            E("scalar", lambda g: g.activation(out=scrA[:], in_=x1t[:], func=ACT.Copy,
                                               accum_out=small[:, 0:1]),
              reads=[B_x1t], writes=[B_scrA, B_small])
            E("scalar", lambda g: g.activation(out=scrA[:], in_=x1t[:], func=ACT.Square,
                                               accum_out=small[:, 1:2]),
              reads=[B_x1t], writes=[B_scrA, B_small])
            E("vector", lambda g, ti=ti: g.tensor_scalar_mul(negm2_a[:, ti:ti + 1], small[:, 0:1],
                                                             -1.0 / DM),
              reads=[B_small], writes=[B_negm2])
            E("scalar", lambda g, ti=ti: g.activation(out=small[:, 3:4], in_=negm2_a[:, ti:ti + 1],
                                                      func=ACT.Square),
              reads=[B_negm2], writes=[B_small])
            E("vector", lambda g: g.tensor_scalar(small[:, 4:5], small[:, 1:2], 1.0 / DM, EPS,
                                                  AL.mult, AL.add),
              reads=[B_small], writes=[B_small])
            E("vector", lambda g: g.tensor_sub(small[:, 5:6], small[:, 4:5], small[:, 3:4]),
              reads=[B_small], writes=[B_small])
            E("scalar", lambda g: g.activation(out=small[:, 6:7], in_=small[:, 5:6], func=ACT.Ln),
              reads=[B_small], writes=[B_small])
            E("scalar", lambda g, ti=ti: g.activation(out=r2_a[:, ti:ti + 1], in_=small[:, 6:7],
                                                      func=ACT.Exp, scale=-0.5),
              reads=[B_small], writes=[B_r2])
            E("scalar", lambda g, ti=ti: g.activation(out=scrA[:], in_=x1t[:], func=ACT.Identity,
                                                      bias=negm2_a[:, ti:ti + 1]),
              reads=[B_x1t, B_negm2], writes=[B_scrA])
            E("vector", lambda g: g.reduce_max(small[:, 3:4], scrA[:], axis=AX.X,
                                               apply_absolute_value=True),
              reads=[B_scrA], writes=[B_small])
            E("vector", lambda g, ti=ti: g.tensor_mul(small[:, 4:5], small[:, 3:4], r2_a[:, ti:ti + 1]),
              reads=[B_small, B_r2], writes=[B_small])
            ppz = ti % 2
            E("vector", lambda g, ppz=ppz: g.tensor_max(maxz_r[:, 1 - ppz:2 - ppz],
                                                        maxz_r[:, ppz:ppz + 1], small[:, 4:5]),
              reads=[B_small, B_maxz], writes=[B_maxz])
            E("gpsimd", lambda g, ti=ti: g.dma_start(out=dx1.ap()[ti * 128:(ti + 1) * 128, :],
                                                     in_=x1t[:]),
              reads=[B_x1t], writes=[B_dx1[ti]], dma=True)

    finalize_max(maxz_r[:, (NTI % 2):(NTI % 2) + 1], B_maxz, 6)
    E("vector", lambda g: g.tensor_copy(cc_sb[:], scal[0:1, 6:7].broadcast_to([1, 8])),
      reads=[B_scal], writes=[B_ccsb])
    E("gpsimd", lambda g: g.dma_start(out=cc1_in.ap(), in_=cc_sb[:]),
      reads=[B_ccsb], writes=[B_cc1i], dma=True)
    E("gpsimd", lambda g: g.collective_compute(
        "AllReduce", AL.max, replica_groups=[list(range(8))],
        ins=[cc1_in.ap().opt()], outs=[cc1_out.ap().opt()]),
      reads=[B_cc1i], writes=[B_cc1o])
    E("sync", lambda g: g.dma_start(out=cc_sb[:], in_=cc1_out.ap()),
      reads=[B_cc1o], writes=[B_ccsb], dma=True)
    E("vector", lambda g: g.tensor_copy(scal[0:1, 7:8], cc_sb[0:1, 0:1]),
      reads=[B_ccsb], writes=[B_scal])
    make_scale(7, 8, 9)
    bcast_scalar(9, 2)
    E("vector", lambda g: g.tensor_mul(scal[0:1, 10:11], scal[0:1, 8:9], scal[0:1, 2:3]),
      reads=[B_scal], writes=[B_scal])
    bcast_scalar(10, 3)
    E("vector", lambda g: g.tensor_scalar(b1s_c[:], b1_c[:], sc_bc[:, 3:4], None, AL.mult),
      reads=[B_b1, B_scbc], writes=[B_b1s])
    E("vector", lambda g: g.tensor_scalar(rq_a[:], r2_a[:], sc_bc[:, 2:3], None, AL.mult),
      reads=[B_r2, B_scbc], writes=[B_rq])
    E("vector", lambda g: g.tensor_mul(bq_a[:], negm2_a[:], rq_a[:]),
      reads=[B_negm2, B_rq], writes=[B_bq])

    P.barrier()

    E("scalar", lambda g: g.preload_act_table(ACT.Gelu))

    B_zqx = [nb("zq0"), nb("zq1")]

    wqs = [dt_t, dt_t]
    wqo = [0, 2048]
    B_wq2 = [nb("wq0"), nb("wq1")]

    def pc_quant_prep(nt, tl):
        ti = nt * 4 + tl
        bx = B_xt[tl % 2]
        E("sync", lambda g, ti=ti, tl=tl: g.dma_start(
            out=xt_b[:, (tl % 2) * DM:(tl % 2 + 1) * DM],
            in_=dx1.ap()[ti * 128:(ti + 1) * 128, :]),
          reads=[B_dx1[ti]], writes=[bx], dma=True, stream="act")
        bsc = (B_scrA, B_scrB)[tl % 2]
        tsc = (scrA, scrB)[tl % 2]
        E("scalar", lambda g, ti=ti, tl=tl, tsc=tsc: g.activation(
            out=tsc[:], in_=xt_b[:, (tl % 2) * DM:(tl % 2 + 1) * DM],
            func=ACT.Identity, scale=rq_a[:, ti:ti + 1], bias=bq_a[:, ti:ti + 1]),
          reads=[bx, B_rq, B_bq], writes=[bsc])
        o = wqo[tl % 2]
        E("vector", lambda g, tsc=tsc, o=o: g.tensor_scalar(dt_t[:, o:o + DM], tsc[:], RND, RND,
                                                            AL.add, AL.subtract),
          reads=[bsc], writes=[B_wq2[tl % 2]])

    def pc_quant_emit(nt, tl):
        zb = nt % 2
        bzq = B_zqx[zb]
        o = wqo[tl % 2]
        for dcb in range(2):
            hb = dcb % 2
            bpab = (B_pab, B_pbb)[hb]
            tpab = (pt_ab, pt_bb)[hb]
            for dcs in range(4):
                dc = dcb * 4 + dcs
                E("tensor", lambda g, dc=dc, dcs=dcs, tpab=tpab, o=o: g.transpose(
                    tpab[:, dcs * 128:(dcs + 1) * 128], dt_t[:, o + dc * 128:o + (dc + 1) * 128],
                    identb[:]),
                  reads=[B_wq2[tl % 2], B_identb], writes=[bpab])
            E("vector", lambda g, dcb=dcb, tl=tl, zb=zb, tpab=tpab: g.tensor_copy(
                zq_nt[:, zb * 4096:(zb + 1) * 4096].rearrange("p (d t) -> p d t", d=DC)[
                    :, dcb * 4:(dcb + 1) * 4, tl * 128:(tl + 1) * 128],
                tpab[:, 0:512].rearrange("p (d t) -> p d t", d=4)),
              reads=[bpab], writes=[bzq])

    def pc_quant_tl(nt, tl):
        pc_quant_prep(nt, tl)
        pc_quant_emit(nt, tl)

    W1PF = 4
    for tl in range(4):
        pc_quant_tl(0, tl)
    if DEBUG:
        E("gpsimd", lambda g: g.dma_start(out=dbg["d_zq"], in_=zq_nt[:, 0:DC * 512]),
          reads=[B_zqx[0]], dma=True)
    mm_banks = [(pt_a, B_pa), (pt_b, B_pb), (pt_c, B_pc), (pt_d, B_pd)]
    for nt in range(8):
        zb = nt % 2
        for pf in range(W1PF - 1):
            E("sync", lambda g, pf=pf: g.dma_start(
                out=w1s[:, pf * DC * 128:(pf + 1) * DC * 128],
                in_=dw1qT.ap()[pf].rearrange("p d c -> p (d c)")),
              reads=[B_dw1qT[pf]], writes=[B_w1s[pf]], dma=True)
        for hcc in range(HC):
            k = hcc % W1PF
            pp, bpp = mm_banks[hcc % 4]
            for dc in range(DC):
                E("tensor", lambda g, hcc=hcc, dc=dc, k=k, pp=pp, zb=zb: g.matmul(
                    pp[:, 0:512], w1s[:, k * DC * 128 + dc * 128:k * DC * 128 + (dc + 1) * 128],
                    zq_nt[:, zb * 4096 + dc * 512:zb * 4096 + (dc + 1) * 512],
                    start=(dc == 0), stop=(dc == DC - 1)),
                  reads=[B_w1s[k], B_zqx[zb]], writes=[bpp])
            if hcc + W1PF - 1 < HC:
                kpf = (hcc + W1PF - 1) % W1PF
                E("sync", lambda g, hcc=hcc, kpf=kpf: g.dma_start(
                    out=w1s[:, kpf * DC * 128:(kpf + 1) * DC * 128],
                    in_=dw1qT.ap()[hcc + W1PF - 1].rearrange("p d c -> p (d c)")),
                  reads=[B_dw1qT[hcc + W1PF - 1]], writes=[B_w1s[kpf]], dma=True, stream="wgt")
            bh = B_hsb[hcc % 2]
            E("scalar", lambda g, hcc=hcc, pp=pp: g.activation(
                out=hsb[:, 1024 + (hcc % 2) * 512:1024 + (hcc % 2 + 1) * 512], in_=pp[:, 0:512],
                func=ACT.Gelu, scale=sc_bc[:, 3:4], bias=b1s_c[:, hcc:hcc + 1]),
              reads=[bpp, B_scbc, B_b1s], writes=[bh])
            E("gpsimd", lambda g, hcc=hcc, nt=nt: g.dma_start(
                out=dh.ap()[nt, hcc], in_=hsb[:, 1024 + (hcc % 2) * 512:1024 + (hcc % 2 + 1) * 512]),
              reads=[bh], writes=[B_dh[nt][hcc]], dma=True, stream="hst")
            E("vector", lambda g, hcc=hcc: g.reduce_max(
                small[:, 7:8], hsb[:, 1024 + (hcc % 2) * 512:1024 + (hcc % 2 + 1) * 512], axis=AX.X,
                apply_absolute_value=True),
              reads=[bh], writes=[B_small])
            pph = (nt * HC + hcc) % 2
            E("vector", lambda g, pph=pph: g.tensor_max(maxh_r[:, 1 - pph:2 - pph],
                                                        maxh_r[:, pph:pph + 1], small[:, 7:8]),
              reads=[B_small, B_maxh], writes=[B_maxh])
            if nt + 1 < 8 and hcc % 8 == 2:
                pc_quant_prep(nt + 1, hcc // 8)
            if nt + 1 < 8 and hcc % 8 == 7:
                pc_quant_emit(nt + 1, hcc // 8)

    finalize_max(maxh_r[:, 0:1], B_maxh, 11)
    E("vector", lambda g: g.tensor_copy(cc_sb[:], scal[0:1, 11:12].broadcast_to([1, 8])),
      reads=[B_scal], writes=[B_ccsb])
    E("gpsimd", lambda g: g.dma_start(out=cc2_in.ap(), in_=cc_sb[:]),
      reads=[B_ccsb], writes=[B_cc2i], dma=True)
    E("gpsimd", lambda g: g.collective_compute(
        "AllReduce", AL.max, replica_groups=[list(range(8))],
        ins=[cc2_in.ap().opt()], outs=[cc2_out.ap().opt()]),
      reads=[B_cc2i], writes=[B_cc2o])
    E("sync", lambda g: g.dma_start(out=cc_sb[:], in_=cc2_out.ap()),
      reads=[B_cc2o], writes=[B_ccsb], dma=True)
    E("vector", lambda g: g.tensor_copy(scal[0:1, 12:13], cc_sb[0:1, 0:1]),
      reads=[B_ccsb], writes=[B_scal])
    E("scalar", _load_combined_table)
    make_scale(12, 13, 14)
    bcast_scalar(14, 4)
    E("vector", lambda g: g.tensor_mul(scal[0:1, 15:16], scal[0:1, 13:14], scal[0:1, 3:4]),
      reads=[B_scal], writes=[B_scal])
    bcast_scalar(15, 5)
    E("vector", lambda g: g.tensor_scalar(b2s_c[:], b2_c[:], sc_bc[:, 5:6], None, AL.mult),
      reads=[B_b2, B_scbc], writes=[B_b2s])
    if DEBUG:
        E("vector", lambda g: g.tensor_copy(scrB[0:1, 0:16], scal[0:1, 0:16]),
          reads=[B_scal], writes=[B_scrB])
        E("gpsimd", lambda g: g.dma_start(out=dbg["d_scal"], in_=scrB[0:1, 0:16]),
          reads=[B_scrB], dma=True)

    P.barrier()

    B_hraw = nb("hraw")

    def pd_readback(nt):
        E("sync", lambda g, nt=nt: g.dma_start(
            out=yb[:, 0:8192].rearrange("p (k c) -> p k c", k=16),
            in_=dh.ap()[nt, 0:16].rearrange("k p c -> p k c")),
          reads=[B_dh[nt][k_] for k_ in range(16)], writes=[B_hraw], dma=True, stream="hrw")
        E("sync", lambda g, nt=nt: g.dma_start(
            out=dtx_t[:, 0:8192].rearrange("p (k c) -> p k c", k=16),
            in_=dh.ap()[nt, 16:32].rearrange("k p c -> p k c")),
          reads=[B_dh[nt][k_] for k_ in range(16, 32)], writes=[B_hraw], dma=True, stream="hrw")

    def pd_quant_chunk(nt, chunk):
        kb = nt % 2
        hqt = (hq_nt_a, hq_nt_b)[kb]
        for kc in range(chunk * 4, chunk * 4 + 4):
            src = yb if kc < 16 else dtx_t
            off = (kc if kc < 16 else kc - 16) * 512
            bsc = (B_scrA, B_scrB)[kc % 2]
            tsc = (scrA, scrB)[kc % 2]
            E("scalar", lambda g, src=src, off=off, tsc=tsc: g.activation(
                out=tsc[:, 0:512], in_=src[:, off:off + 512],
                func=ACT.Identity, scale=sc_bc[:, 4:5]),
              reads=[B_hraw, B_scbc], writes=[bsc])
            E("vector", lambda g, kc=kc, hqt=hqt, tsc=tsc: g.tensor_scalar(
                hqt[:, kc * 512:(kc + 1) * 512],
                tsc[:, 0:512], RND, RND, AL.add, AL.subtract),
              reads=[bsc], writes=[B_hqnt[kb]])

    def pd_quant(nt):
        for chunk in range(8):
            pd_quant_chunk(nt, chunk)

    pd_readback(0)
    pd_quant(0)
    for nt in range(8):
        kb = nt % 2
        if nt + 1 < 8:
            pd_readback(nt + 1)
        E("sync", lambda g: g.dma_start(
            out=w2s[:, 0:HC * 128].rearrange("p (k c) -> p k c", k=HC),
            in_=dw2qT.ap()[0].rearrange("k p c -> p k c")),
          reads=[B_dw2qT[0]], writes=[B_w2s[0]], dma=True, stream="wgt")
        for dout in range(DC):
            k2 = dout % 2
            pp = (pt_c, pt_d)[dout % 2]
            bpp = (B_pc, B_pd)[dout % 2]
            hqt = (hq_nt_a, hq_nt_b)[kb]
            for kc in range(HC):
                E("tensor", lambda g, kc=kc, k2=k2, hqt=hqt, pp=pp: g.matmul(
                    pp[:, 0:512], w2s[:, k2 * HC * 128 + kc * 128:k2 * HC * 128 + (kc + 1) * 128],
                    hqt[:, kc * 512:(kc + 1) * 512],
                    start=(kc == 0), stop=(kc == HC - 1)),
                  reads=[B_w2s[k2], B_hqnt[kb]], writes=[bpp])
            if dout + 1 < DC:
                E("sync", lambda g, dout=dout, k2=k2: g.dma_start(
                    out=w2s[:, (1 - k2) * HC * 128:(2 - k2) * HC * 128].rearrange("p (k c) -> p k c", k=HC),
                    in_=dw2qT.ap()[dout + 1].rearrange("k p c -> p k c")),
                  reads=[B_dw2qT[dout + 1]], writes=[B_w2s[1 - k2]], dma=True, stream="wgt")
            bx2 = B_x2f[dout % 2]
            E("scalar", lambda g, dout=dout, pp=pp: g.activation(
                out=x2f[:, (dout % 2) * 512:(dout % 2) * 512 + 512], in_=pp[:, 0:512],
                func=ACT.Identity, scale=sc_bc[:, 5:6], bias=b2s_c[:, dout:dout + 1]),
              reads=[bpp, B_scbc, B_b2s], writes=[bx2])
            ppt = (pt_a, pt_b)[dout % 2]
            bppt = (B_pa, B_pb)[dout % 2]
            for sub in range(4):
                E("tensor", lambda g, sub=sub, dout=dout, ppt=ppt: g.transpose(
                    ppt[:, sub * 128:(sub + 1) * 128],
                    x2f[:, (dout % 2) * 512 + sub * 128:(dout % 2) * 512 + (sub + 1) * 128],
                    identf[:]),
                  reads=[bx2, B_identf], writes=[bppt])
            bq1 = B_x1q[dout % 2]
            E("sync", lambda g, nt=nt, dout=dout: g.dma_start(
                out=x1q[:, 1024 + (dout % 2) * 512:1024 + (dout % 2) * 512 + 512].rearrange("p (s c) -> p s c", s=4),
                in_=dx1.ap().rearrange("(n s p) d -> n p s d", s=4, p=128)[nt, :, :,
                                                                          dout * 128:(dout + 1) * 128]),
              reads=[B_dx1[nt * 4], B_dx1[nt * 4 + 1], B_dx1[nt * 4 + 2], B_dx1[nt * 4 + 3]],
              writes=[bq1], dma=True, stream="act")
            bo = B_outt[dout % 2]
            E("vector", lambda g, dout=dout, ppt=ppt: g.tensor_add(
                outt[:, (dout % 2) * 512:(dout % 2) * 512 + 512], ppt[:, 0:512],
                x1q[:, 1024 + (dout % 2) * 512:1024 + (dout % 2) * 512 + 512]),
              reads=[bppt, bq1], writes=[bo])
            E("gpsimd", lambda g, nt=nt, dout=dout: g.dma_start(
                out=dout_view[nt, :, :, dout * 128:(dout + 1) * 128],
                in_=outt[:, (dout % 2) * 512:(dout % 2) * 512 + 512].rearrange("p (s c) -> p s c", s=4)),
              reads=[bo], dma=True, stream="out")
            if nt + 1 < 8:
                pd_quant_chunk(nt + 1, dout)

    if DEBUG:
        for ti in range(NTI):
            E("sync", lambda g, ti=ti: g.dma_start(out=xt_b[:, 0:DM],
                                                   in_=dx1.ap()[ti * 128:(ti + 1) * 128, :]),
              reads=[B_dx1[ti]], writes=[B_xt[0]], dma=True)
            E("gpsimd", lambda g, ti=ti: g.dma_start(out=dbg["d_x1"][ti * 128:(ti + 1) * 128, :],
                                                     in_=xt_b[:, 0:DM]),
              reads=[B_xt[0]], dma=True)
            E("sync", lambda g, ti=ti: g.dma_start(out=xt_b[:, DM:2 * DM],
                                                   in_=dxa.ap()[ti * 128:(ti + 1) * 128, :]),
              reads=[B_dxa[ti]], writes=[B_xt[1]], dma=True)
            E("gpsimd", lambda g, ti=ti: g.dma_start(out=dbg["d_xa"][ti * 128:(ti + 1) * 128, :],
                                                     in_=xt_b[:, DM:2 * DM]),
              reads=[B_xt[1]], dma=True)

    P.barrier()

    P.finalize()

    with es, nc.Block() as block:
        @block.sync
        def _(eng):
            P.replay("sync", eng, esem, dsem, ssem)
            for e in ENGINES:
                n = sum(1 for it in P.items[e] if it.is_dma and it.stream is None)
                if n:
                    eng.wait_ge(dsem[e], 16 * n)
            for k in Prog.STREAMS:
                n = sum(1 for e in ENGINES for it in P.items[e]
                        if it.is_dma and it.stream == k)
                if n:
                    eng.wait_ge(ssem[k], 16 * n)

        @block.scalar
        def _(eng):
            P.replay("scalar", eng, esem, dsem, ssem)

        @block.vector
        def _(eng):
            P.replay("vector", eng, esem, dsem, ssem)

        @block.tensor
        def _(eng):
            P.replay("tensor", eng, esem, dsem, ssem)

        @block.gpsimd
        def _(eng):
            P.replay("gpsimd", eng, esem, dsem, ssem)
            n = sum(1 for it in P.items["gpsimd"] if it.is_dma and it.stream is None)
            eng.wait_ge(dsem["gpsimd"], 16 * n)

    return nc, P


def kernel(L0_attn_x, L0_attn_A_log, L0_attn_D, L0_attn_Wx, L0_attn_dt_W,
           L0_attn_dt_b, L0_ffn_w1, L0_ffn_b1, L0_ffn_w2, L0_ffn_b2):
    bf = ml_dtypes.bfloat16
    x = np.ascontiguousarray(np.asarray(L0_attn_x, dtype=np.float32))
    ident_f = np.eye(128, dtype=np.float32)
    ident_b = np.eye(128).astype(bf)
    ones_b = np.ones((1, 128), dtype=np.float32).astype(bf)
    ones_f = np.ones((1, 128), dtype=np.float32)
    onehot32 = np.zeros((32, 32 * 128), dtype=np.float32)
    for s_ in range(32):
        onehot32[s_, s_ * 128:(s_ + 1) * 128] = 1.0
    onehot32 = onehot32.astype(bf)

    nc, P = build_nc()
    common = dict(
        A_log=np.ascontiguousarray(np.asarray(L0_attn_A_log, dtype=np.float32)),
        D_vec=np.ascontiguousarray(np.asarray(L0_attn_D, dtype=np.float32)),
        Wx=np.ascontiguousarray(np.asarray(L0_attn_Wx, dtype=np.float32)),
        dt_W=np.ascontiguousarray(np.asarray(L0_attn_dt_W, dtype=np.float32)),
        dt_b=np.ascontiguousarray(np.asarray(L0_attn_dt_b, dtype=np.float32)),
        w1=np.ascontiguousarray(np.asarray(L0_ffn_w1, dtype=np.float32)),
        b1=np.ascontiguousarray(np.asarray(L0_ffn_b1, dtype=np.float32)),
        w2=np.ascontiguousarray(np.asarray(L0_ffn_w2, dtype=np.float32)),
        b2=np.ascontiguousarray(np.asarray(L0_ffn_b2, dtype=np.float32)),
        ident_f=ident_f, ident_b=ident_b, ones_b=ones_b,
        ones_f=ones_f, onehot32=onehot32,
    )
    in_maps = [dict(x_in=np.ascontiguousarray(x[i]), **common) for i in range(8)]
    res = run_bass_kernel_spmd(nc, in_maps, core_ids=list(range(8)))
    out = np.stack([np.asarray(res.results[i]["out"], dtype=np.float32) for i in range(8)], axis=0)
    if DEBUG:
        kernel._dbg = res
    return out



# version 14
# speedup vs baseline: 1.3766x; 1.0125x over previous
import numpy as np
import ml_dtypes
import concourse.bass as bass
import concourse.mybir as mybir
from concourse.bass_utils import run_bass_kernel_spmd
from contextlib import ExitStack

F32 = mybir.dt.float32
BF16 = mybir.dt.bfloat16
AL = mybir.AluOpType
ACT = mybir.ActivationFunctionType
AX = mybir.AxisListType

L = 4096
DM = 1024
DC = 8
S = 16
R = 64
H = 4096
HC = 32
TC = 1024
NTC = L // TC
NTI = L // 128
TPC = TC // 128
QP = 127.0
EPS = 1e-5
RND = float(1.5 * 2.0 ** 23)

DEBUG = False

ENGINES = ("sync", "scalar", "vector", "tensor", "gpsimd")


class Buf:
    __slots__ = ("writer", "readers", "name")

    def __init__(self, name=""):
        self.writer = None
        self.readers = []
        self.name = name


class Item:
    __slots__ = ("eng", "fn", "deps", "marked", "ord", "is_dma", "idx", "dmacnt", "nodrain",
                 "stream", "stream_ord")

    def __init__(self, eng, fn, deps, is_dma):
        self.eng = eng
        self.fn = fn
        self.deps = deps
        self.marked = False
        self.ord = None
        self.is_dma = is_dma
        self.idx = None
        self.dmacnt = None
        self.nodrain = False
        self.stream = None
        self.stream_ord = None


class Prog:
    def __init__(self):
        self.items = {e: [] for e in ENGINES}
        self.bar = {e: None for e in ENGINES}
        self.dmac = {e: 0 for e in ENGINES}

    STREAMS = ("wgt", "hrw", "act", "hst", "out")

    def emit(self, eng, fn, reads=(), writes=(), deps=(), dma=False, nodrain=False, stream=None):
        dd = []
        for b in reads:
            if b.writer is not None:
                dd.append(b.writer)
        for b in writes:
            if b.writer is not None:
                dd.append(b.writer)
            dd.extend(b.readers)
        dd.extend(deps)
        if self.bar[eng] is not None:
            dd.append(self.bar[eng])
        it = Item(eng, fn, dd, dma)
        it.nodrain = nodrain
        it.idx = len(self.items[eng])
        it.dmacnt = dict(self.dmac)
        if dma:
            if stream is not None:
                it.stream = stream
                cnt = getattr(self, "_scnt", None)
                if cnt is None:
                    cnt = self._scnt = {k: 0 for k in Prog.STREAMS}
                cnt[stream] += 1
                it.stream_ord = cnt[stream]
            else:
                self.dmac[eng] += 1
        self.items[eng].append(it)
        for b in reads:
            b.readers.append(it)
        for b in writes:
            b.writer = it
            b.readers = []
        for d in dd:
            if d.eng != eng or d.is_dma:
                d.marked = True
        return it

    def barrier(self):
        lasts = [self.items[e][-1] for e in ENGINES if self.items[e]]
        newbar = {}
        for e in ENGINES:
            it = Item(e, lambda eng: eng.nop(), list(lasts), False)
            it.idx = len(self.items[e])
            it.dmacnt = dict(self.dmac)
            self.items[e].append(it)
            newbar[e] = it
        for d in lasts:
            d.marked = True
        for e in ENGINES:
            self.bar[e] = newbar[e]

    def finalize(self):
        for e in ENGINES:
            o = 0
            do = 0
            for it in self.items[e]:
                if it.is_dma:
                    do += 1
                    it.ord = do
                    it.marked = True
                elif it.marked:
                    o += 1
                    it.ord = o
        self.ninstr = {e: len(self.items[e]) for e in ENGINES}

    FENCE = 12

    def replay(self, eng_name, eng_obj, esem, dsem, ssem):
        waited = {}
        ndma = 0
        last_drain = -1
        need_drain_engines = eng_name in ("vector", "gpsimd")
        for it in self.items[eng_name]:
            if need_drain_engines and not it.nodrain:
                for d in it.deps:
                    if (not d.is_dma) and d.eng == eng_name and d.idx > last_drain:
                        eng_obj.drain()
                        last_drain = it.idx - 1
                        break
            for d in it.deps:
                if d.is_dma:
                    if d.stream is not None:
                        sem = ssem[d.stream]
                        val = 16 * d.stream_ord
                    else:
                        sem = dsem[d.eng]
                        val = 16 * it.dmacnt[d.eng]
                elif d.eng == eng_name:
                    continue
                else:
                    if d.ord is None:
                        continue
                    sem = esem[d.eng]
                    val = d.ord
                key = id(sem)
                if waited.get(key, -1) < val:
                    eng_obj.wait_ge(sem, val)
                    waited[key] = val
            if it.is_dma and it.stream is None:
                ndma += 1
                if ndma > Prog.FENCE:
                    val = 16 * (ndma - Prog.FENCE)
                    key = id(dsem[eng_name])
                    if waited.get(key, -1) < val:
                        eng_obj.wait_ge(dsem[eng_name], val)
                        waited[key] = val
            inst = it.fn(eng_obj)
            if it.is_dma:
                if it.stream is not None:
                    inst.then_inc(ssem[it.stream], 16)
                else:
                    inst.then_inc(dsem[it.eng], 16)
            elif it.marked:
                inst.then_inc(esem[it.eng], 1)


def build_nc():
    nc = bass.Bass("TRN2", target_bir_lowering=False, debug=False)

    dx = nc.dram_tensor("x_in", [L, DM], F32, kind="ExternalInput").ap()
    dAlog = nc.dram_tensor("A_log", [DM, S], F32, kind="ExternalInput").ap()
    dD = nc.dram_tensor("D_vec", [DM], F32, kind="ExternalInput").ap()
    dWx = nc.dram_tensor("Wx", [96, DM], F32, kind="ExternalInput").ap()
    ddtW = nc.dram_tensor("dt_W", [DM, R], F32, kind="ExternalInput").ap()
    ddtb = nc.dram_tensor("dt_b", [DM], F32, kind="ExternalInput").ap()
    dw1 = nc.dram_tensor("w1", [H, DM], F32, kind="ExternalInput").ap()
    db1 = nc.dram_tensor("b1", [H], F32, kind="ExternalInput").ap()
    dw2 = nc.dram_tensor("w2", [DM, H], F32, kind="ExternalInput").ap()
    db2 = nc.dram_tensor("b2", [DM], F32, kind="ExternalInput").ap()
    didf = nc.dram_tensor("ident_f", [128, 128], F32, kind="ExternalInput").ap()
    didb = nc.dram_tensor("ident_b", [128, 128], BF16, kind="ExternalInput").ap()
    dones = nc.dram_tensor("ones_b", [1, 128], BF16, kind="ExternalInput").ap()
    donesf = nc.dram_tensor("ones_f", [1, 128], F32, kind="ExternalInput").ap()
    doh32 = nc.dram_tensor("onehot32", [32, 32 * 128], BF16, kind="ExternalInput").ap()

    dout = nc.dram_tensor("out", [L, DM], F32, kind="ExternalOutput").ap()
    dout_view = dout.rearrange("(n s p) d -> n p s d", s=4, p=128)

    dxa = nc.dram_tensor("xa_hbm", [L, DM], F32)
    dx1 = nc.dram_tensor("x1_hbm", [L, DM], F32)
    dh = nc.dram_tensor("h_hbm", [8, HC, 128, 512], BF16)
    dw1qT = nc.dram_tensor("w1qT_hbm", [HC, 128, DC, 128], BF16)
    dw2qT = nc.dram_tensor("w2qT_hbm", [DC, HC, 128, 128], BF16)
    cc1_in = nc.dram_tensor("cc1_in", [1, 8], F32)
    cc1_out = nc.dram_tensor("cc1_out", [1, 8], F32)
    cc2_in = nc.dram_tensor("cc2_in", [1, 8], F32)
    cc2_out = nc.dram_tensor("cc2_out", [1, 8], F32)

    dbg = {}
    if DEBUG:
        dbg["d_x1"] = nc.dram_tensor("dbg_x1", [L, DM], F32, kind="ExternalOutput").ap()
        dbg["d_scal"] = nc.dram_tensor("dbg_scal", [1, 16], F32, kind="ExternalOutput").ap()
        dbg["d_zq"] = nc.dram_tensor("dbg_zq", [128, DC * 512], F32, kind="ExternalOutput").ap()
        dbg["d_xa"] = nc.dram_tensor("dbg_xa", [L, DM], F32, kind="ExternalOutput").ap()
        dbg["d_dt"] = nc.dram_tensor("dbg_dt", [128, DC * TC], F32, kind="ExternalOutput").ap()
        dbg["d_yb"] = nc.dram_tensor("dbg_yb", [128, DC * TC], F32, kind="ExternalOutput").ap()
        dbg["d_Bb"] = nc.dram_tensor("dbg_Bb", [128, S * TC], F32, kind="ExternalOutput").ap()
        dbg["d_z0"] = nc.dram_tensor("dbg_z0", [128, DM], F32, kind="ExternalOutput").ap()
        dbg["d_small"] = nc.dram_tensor("dbg_small", [128, 8], F32, kind="ExternalOutput").ap()
        dbg["d_scal2"] = nc.dram_tensor("dbg_scal2", [1, 24], F32, kind="ExternalOutput").ap()

    es = ExitStack()
    sb = lambda n_, s_, d_: es.enter_context(nc.sbuf_tensor(n_, s_, d_))
    pst = lambda n_, s_, d_: es.enter_context(nc.psum_tensor(n_, s_, d_))

    identf = sb("identf", [128, 128], F32)
    identb = sb("identb", [128, 128], BF16)
    ones1 = sb("ones1", [1, 128], BF16)
    ones1f = sb("ones1f", [1, 128], F32)
    oh32 = sb("oh32", [32, 32 * 128], BF16)
    A_neg = sb("A_neg", [128, DC * S], F32)
    dtb_c = sb("dtb_c", [128, DC], F32)
    Dcol_c = sb("Dcol_c", [128, DC], F32)
    b1_c = sb("b1_c", [128, HC], F32)
    b2_c = sb("b2_c", [128, DC], F32)
    b1s_c = sb("b1s_c", [128, HC], F32)
    b2s_c = sb("b2s_c", [128, DC], F32)
    state = sb("state", [128, DC * S], F32)
    negm2_a = sb("negm2_a", [128, NTI], F32)
    r2_a = sb("r2_a", [128, NTI], F32)
    rq_a = sb("rq_a", [128, NTI], F32)
    bq_a = sb("bq_a", [128, NTI], F32)
    maxz_r = sb("maxz_r", [128, 2], F32)
    maxh_r = sb("maxh_r", [128, 2], F32)
    scal = sb("scal", [1, 24], F32)
    cc_sb = sb("cc_sb", [1, 8], F32)
    sc_bc = sb("sc_bc", [128, 8], F32)
    WxT = sb("WxT", [128, DC * 96], BF16)
    dtWT = sb("dtWT", [64, DC * 128], BF16)

    z0Tb = sb("z0Tb", [128, DC * TC], BF16)
    dt_t = sb("dt_t", [128, DC * TC], BF16)
    dtx_t = sb("dtx_t", [128, DC * TC], BF16)
    Bb = sb("Bb", [128, S * TC], BF16)
    Cb = sb("Cb", [128, S * TC], BF16)
    dtr_b = sb("dtr_b", [64, TC], BF16)
    BCb = sb("BCb", [32, TC], BF16)
    yb = sb("yb", [128, DC * TC], BF16)
    dA_b = sb("dA_b", [128, 2 * TC], F32)
    u_b = sb("u_b", [128, 2 * TC], BF16)
    h_b = sb("h_b", [128, 2 * TC], F32)
    hc_b = sb("hc_b", [128, 2 * TC], BF16)
    xt_b = sb("xt_b", [128, 2 * DM], F32)
    z0_b = sb("z0_b", [128, 2 * DM], F32)
    scrA = sb("scrA", [128, DM], F32)
    scrB = sb("scrB", [128, DM], F32)
    x1t = sb("x1t", [128, DM], F32)
    spt = scrA
    spa = scrB
    spe_ = scrB
    spl = x1t
    small = sb("small", [128, 8], F32)

    wchunk = z0_b
    wq_t = dt_t
    zq_nt = z0Tb
    w1s = dtx_t
    hq_nt_a = Bb
    hq_nt_b = Cb
    hin_b = dtx_t
    w2s = dt_t
    hsb = dt_t
    x2f = z0_b
    x1q = z0_b
    outt = x1t

    pt_a = pst("pt_a", [128, 512], F32)
    pt_b = pst("pt_b", [128, 512], F32)
    pt_c = pst("pt_c", [128, 512], F32)
    pt_d = pst("pt_d", [128, 512], F32)
    y_ps = pst("y_ps", [128, 1024], F32)
    pt_ab = pst("pt_ab", [128, 1024], BF16)
    pt_bb = pst("pt_bb", [128, 1024], BF16)

    esem = {}
    dsem = {}
    for e in ENGINES:
        esem[e] = es.enter_context(nc.semaphore("sem_" + e))
        dsem[e] = es.enter_context(nc.semaphore("dma_" + e))
    ssem = {k: es.enter_context(nc.semaphore("sdma_" + k)) for k in Prog.STREAMS}

    P = Prog()
    nb = lambda n: Buf(n)

    B_identf = nb("identf"); B_identb = nb("identb"); B_ones = nb("ones")
    B_Aneg = nb("A_neg"); B_dtb = nb("dtb"); B_b1 = nb("b1"); B_b2 = nb("b2")
    B_b1s = nb("b1s"); B_b2s = nb("b2s")
    B_Dbc = nb("Dbc"); B_Dcol = nb("Dcol"); B_Ddiag = nb("Ddiag")
    B_state = [nb(f"st{i}") for i in range(DC * S)]
    B_negm2 = nb("negm2"); B_r2 = nb("r2"); B_rq = nb("rq"); B_bq = nb("bq")
    B_maxz = nb("maxz"); B_maxh = nb("maxh")
    B_scal = nb("scal"); B_ccsb = nb("ccsb"); B_scbc = nb("scbc")
    B_WxT = nb("WxT"); B_dtWT = nb("dtWT")
    B_z0Tb = [nb(f"z0Tb{d}") for d in range(DC)]
    B_dt = [nb(f"dt{d}") for d in range(DC)]
    B_dtx = [nb(f"dtx{d}") for d in range(DC)]
    B_Bb = [nb(f"Bb{s}") for s in range(S)]
    B_Cb = [nb(f"Cb{s}") for s in range(S)]
    B_dtr = nb("dtr"); B_BCb = nb("BCb")
    B_yb = [nb(f"yb{d}") for d in range(DC)]
    B_dA = [nb("dA0"), nb("dA1")]
    B_u = [nb("u0"), nb("u1")]
    B_h = [nb("h0"), nb("h1")]
    B_hc = [nb("hc0"), nb("hc1")]
    B_xt = [nb("xt0"), nb("xt1")]
    B_z0 = [nb(f"z0_{i}") for i in range(2)]
    B_scrA = nb("scrA"); B_scrB = nb("scrB")
    B_x1t = nb("x1t")
    B_spt = [B_scrA, B_scrA]; B_spa = B_scrB; B_spa2 = B_scrB
    B_spe = nb('spe0'); B_spe2 = nb('spe1'); B_spl2 = nb('spl0'); B_spl3 = nb('spl1')
    B_small = nb("small")
    B_pa = nb("pt_a"); B_pb = nb("pt_b"); B_pc = nb("pt_c"); B_pd = nb("pt_d")
    B_yps = nb("yps"); B_pab = nb("pt_ab"); B_pbb = nb("pt_bb")
    B_wch = [nb("wch0"), nb("wch1")]; B_wqt = nb("wq_t")
    B_zqnt = nb("zq_nt"); B_w1s = [nb(f"w1s{i}") for i in range(4)]
    B_hqnt = [[nb(f"hq{b_}_{c_}") for c_ in range(8)] for b_ in range(2)]
    B_hin = [nb("hin0"), nb("hin1")]
    B_w2s = [nb("w2s0"), nb("w2s1")]; B_hsb = [nb("hsb0"), nb("hsb1")]
    B_x2f = [nb("x2f0"), nb("x2f1")]; B_x1q = [nb("x1q0"), nb("x1q1")]
    B_outt = [nb("outt0"), nb("outt1")]
    B_dxa = [nb(f"dxa{i}") for i in range(NTI)]
    B_dx1 = [nb(f"dx1{i}") for i in range(NTI)]
    B_dh = [[nb(f"dh{a}_{b}") for b in range(HC)] for a in range(8)]
    B_dw1qT = [nb(f"dw1qT{a}") for a in range(HC)]
    B_dw2qT = [nb(f"dw2qT{a}") for a in range(DC)]
    B_cc1i = nb("cc1i"); B_cc1o = nb("cc1o"); B_cc2i = nb("cc2i"); B_cc2o = nb("cc2o")

    E = P.emit

    def _load_combined_table(g):
        from concourse.hw_specs import get_activation_tables
        for idx, (nm, funcs) in enumerate(get_activation_tables(nc.m.arch).items()):
            if ACT.Exp in funcs and ACT.Ln in funcs:
                return g.add_instruction(mybir.InstLoadActFuncSet(
                    name=nc.get_next_instruction_name(), act_func_set_id=idx, ins=[], outs=[]))
        raise ValueError("no combined exp+ln table")
    E("scalar", _load_combined_table)
    E("sync", lambda g: g.dma_start(out=identf[:], in_=didf), writes=[B_identf], dma=True)
    E("sync", lambda g: g.dma_start(out=identb[:], in_=didb), writes=[B_identb], dma=True)
    E("sync", lambda g: g.dma_start(out=ones1[:], in_=dones), writes=[B_ones], dma=True)
    E("sync", lambda g: g.dma_start(out=ones1f[:], in_=donesf), writes=[B_ones], dma=True)
    E("sync", lambda g: g.dma_start(out=oh32[:], in_=doh32), writes=[B_ones], dma=True)
    for dc in range(DC):
        E("sync", lambda g, dc=dc: g.dma_start(out=scrB[:, dc * S:(dc + 1) * S],
                                               in_=dAlog[dc * 128:(dc + 1) * 128, :]),
          writes=[B_scrB], dma=True)
    E("scalar", lambda g: g.activation(out=scrB[:, 128:256], in_=scrB[:, 0:128], func=ACT.Exp),
      reads=[B_scrB], writes=[B_scrB])
    E("vector", lambda g: g.tensor_scalar_mul(A_neg[:], scrB[:, 128:256], -1.0),
      reads=[B_scrB], writes=[B_Aneg])
    for dc in range(DC):
        E("sync", lambda g, dc=dc: g.dma_start(out=dtb_c[:, dc:dc + 1],
                                               in_=ddtb[dc * 128:(dc + 1) * 128].rearrange("(p o) -> p o", o=1)),
          writes=[B_dtb], dma=True)
        E("sync", lambda g, dc=dc: g.dma_start(out=b2_c[:, dc:dc + 1],
                                               in_=db2[dc * 128:(dc + 1) * 128].rearrange("(p o) -> p o", o=1)),
          writes=[B_b2], dma=True)
    for hcc in range(HC):
        E("sync", lambda g, hcc=hcc: g.dma_start(out=b1_c[:, hcc:hcc + 1],
                                                 in_=db1[hcc * 128:(hcc + 1) * 128].rearrange("(p o) -> p o", o=1)),
          writes=[B_b1], dma=True)
    for dc in range(DC):
        E("sync", lambda g, dc=dc: g.dma_start(out=Dcol_c[:, dc:dc + 1],
                                               in_=dD[dc * 128:(dc + 1) * 128].rearrange("(p o) -> p o", o=1)),
          writes=[B_Dcol], dma=True)

    E("vector", lambda g: g.memset(state[:], 0.0), writes=B_state)
    E("vector", lambda g: g.memset(maxz_r[:], 0.0), writes=[B_maxz])
    E("vector", lambda g: g.memset(maxh_r[:], 0.0), writes=[B_maxh])
    E("sync", lambda g: g.dma_start(out=wchunk[0:96, 0:DM], in_=dWx), writes=[B_wch[0]], dma=True)
    E("vector", lambda g: g.tensor_copy(wq_t[0:96, 0:DM], wchunk[0:96, 0:DM]),
      reads=[B_wch[0]], writes=[B_wqt])
    for dc in range(DC):
        E("tensor", lambda g, dc=dc: g.transpose(pt_ab[0:128, 0:96],
                                                 wq_t[0:96, dc * 128:(dc + 1) * 128],
                                                 identb[0:96, 0:96]),
          reads=[B_wqt, B_identb], writes=[B_pab])
        E("scalar", lambda g, dc=dc: g.activation(out=WxT[:, dc * 96:(dc + 1) * 96],
                                                  in_=pt_ab[0:128, 0:96], func=ACT.Copy),
          reads=[B_pab], writes=[B_WxT])
    for dc in range(DC):
        E("sync", lambda g, dc=dc: g.dma_start(out=wchunk[:, 0:R],
                                               in_=ddtW[dc * 128:(dc + 1) * 128, :]),
          writes=[B_wch[0]], dma=True)
        E("vector", lambda g, dc=dc: g.tensor_copy(wq_t[:, 0:R], wchunk[:, 0:R]),
          reads=[B_wch[0]], writes=[B_wqt])
        E("tensor", lambda g, dc=dc: g.transpose(pt_ab[0:R, 0:128], wq_t[:, 0:R], identb[:]),
          reads=[B_wqt, B_identb], writes=[B_pab])
        E("scalar", lambda g, dc=dc: g.activation(out=dtWT[:, dc * 128:(dc + 1) * 128],
                                                  in_=pt_ab[0:R, 0:128], func=ACT.Copy),
          reads=[B_pab], writes=[B_dtWT])

    def finalize_max(run_tile, run_buf, idx):
        E("tensor", lambda g: g.transpose(pt_a[0:1, 0:128], run_tile, identf[:]),
          reads=[run_buf, B_identf], writes=[B_pa])
        E("vector", lambda g: g.tensor_copy(scrB[0:1, 0:128], pt_a[0:1, 0:128]),
          reads=[B_pa], writes=[B_scrB])
        E("vector", lambda g, idx=idx: g.reduce_max(scal[0:1, idx:idx + 1], scrB[0:1, 0:128], axis=AX.X),
          reads=[B_scrB], writes=[B_scal])

    E("vector", lambda g: g.memset(small[:, 0:2], 0.0), writes=[B_small])
    for ch in range(HC):
        bw = B_wch[ch % 2]
        pp_ = ch % 2
        E("sync", lambda g, ch=ch: g.dma_start(out=wchunk[:, (ch % 2) * DM:(ch % 2 + 1) * DM],
                                               in_=dw1[ch * 128:(ch + 1) * 128, :]),
          writes=[bw], dma=True)
        E("vector", lambda g, ch=ch: g.reduce_max(small[:, 4:5],
                                                  wchunk[:, (ch % 2) * DM:(ch % 2 + 1) * DM],
                                                  axis=AX.X, apply_absolute_value=True),
          reads=[bw], writes=[B_small])
        E("vector", lambda g, pp_=pp_: g.tensor_max(small[:, 1 - pp_:2 - pp_],
                                                    small[:, pp_:pp_ + 1], small[:, 4:5]),
          reads=[B_small], writes=[B_small])
    finalize_max(small[:, (HC % 2):(HC % 2) + 1], B_small, 0)
    E("vector", lambda g: g.memset(small[:, 2:3], 0.0), writes=[B_small])
    for ch in range(DC):
        bw = B_wch[ch % 2]
        for half in range(2):
            E("sync", lambda g, ch=ch, half=half: g.dma_start(
                out=wchunk[:, (ch % 2) * DM:(ch % 2) * DM + DM],
                in_=dw2[ch * 128:(ch + 1) * 128, half * 2048:half * 2048 + DM]),
              writes=[bw], dma=True)
            E("vector", lambda g, ch=ch: g.reduce_max(small[:, 3:4],
                                                      wchunk[:, (ch % 2) * DM:(ch % 2 + 1) * DM],
                                                      axis=AX.X, apply_absolute_value=True),
              reads=[bw], writes=[B_small])
            E("vector", lambda g: g.tensor_max(small[:, 2:3], small[:, 2:3], small[:, 3:4]),
              reads=[B_small], writes=[B_small])
            E("sync", lambda g, ch=ch, half=half: g.dma_start(
                out=wchunk[:, (ch % 2) * DM:(ch % 2) * DM + DM],
                in_=dw2[ch * 128:(ch + 1) * 128, half * 2048 + DM:half * 2048 + 2 * DM]),
              writes=[bw], dma=True)
            E("vector", lambda g, ch=ch: g.reduce_max(small[:, 3:4],
                                                      wchunk[:, (ch % 2) * DM:(ch % 2 + 1) * DM],
                                                      axis=AX.X, apply_absolute_value=True),
              reads=[bw], writes=[B_small])
            E("vector", lambda g: g.tensor_max(small[:, 2:3], small[:, 2:3], small[:, 3:4]),
              reads=[B_small], writes=[B_small])
    finalize_max(small[:, 2:3], B_small, 1)

    def make_scale(src_idx, s_idx, inv_idx):
        E("vector", lambda g, src_idx=src_idx, s_idx=s_idx: g.tensor_scalar(
            scal[0:1, s_idx:s_idx + 1], scal[0:1, src_idx:src_idx + 1],
            1e-6, 1.0 / QP, AL.max, AL.mult),
          reads=[B_scal], writes=[B_scal])
        E("scalar", lambda g, s_idx=s_idx: g.activation(out=scal[0:1, 16:17],
                                                        in_=scal[0:1, s_idx:s_idx + 1], func=ACT.Ln),
          reads=[B_scal], writes=[B_scal])
        E("scalar", lambda g, inv_idx=inv_idx: g.activation(out=scal[0:1, inv_idx:inv_idx + 1],
                                                            in_=scal[0:1, 16:17], func=ACT.Exp,
                                                            scale=-1.0),
          reads=[B_scal], writes=[B_scal])

    make_scale(0, 2, 4)
    make_scale(1, 3, 5)

    def bcast_scalar(idx, dst_col):
        E("tensor", lambda g, idx=idx: g.matmul(pt_a[:, 0:1], ones1f[:], scal[0:1, idx:idx + 1],
                                                start=True, stop=True),
          reads=[B_scal, B_ones], writes=[B_pa])
        E("scalar", lambda g, dst_col=dst_col: g.activation(out=sc_bc[:, dst_col:dst_col + 1],
                                                            in_=pt_a[:, 0:1], func=ACT.Copy),
          reads=[B_pa], writes=[B_scbc])

    bcast_scalar(4, 0)
    bcast_scalar(5, 1)

    for ch in range(HC):
        bw = B_wch[ch % 2]
        E("sync", lambda g, ch=ch: g.dma_start(out=wchunk[:, (ch % 2) * DM:(ch % 2 + 1) * DM],
                                               in_=dw1[ch * 128:(ch + 1) * 128, :]),
          writes=[bw], dma=True)
        E("scalar", lambda g, ch=ch: g.activation(out=scrB[:], in_=wchunk[:, (ch % 2) * DM:(ch % 2 + 1) * DM],
                                                  func=ACT.Identity, scale=sc_bc[:, 0:1]),
          reads=[bw, B_scbc], writes=[B_scrB])
        E("vector", lambda g: g.tensor_scalar(wq_t[:, 0:DM], scrB[:], RND, RND, AL.add, AL.subtract),
          reads=[B_scrB], writes=[B_wqt])
        for dc in range(DC):
            E("tensor", lambda g, dc=dc: g.transpose(pt_ab[:, (dc % 8) * 128:(dc % 8 + 1) * 128],
                                                     wq_t[:, dc * 128:(dc + 1) * 128], identb[:]),
              reads=[B_wqt, B_identb], writes=[B_pab])
            E("scalar", lambda g, dc=dc: g.activation(out=zq_nt[:, dc * 128:(dc + 1) * 128],
                                                      in_=pt_ab[:, (dc % 8) * 128:(dc % 8 + 1) * 128],
                                                      func=ACT.Copy),
              reads=[B_pab], writes=[B_zqnt])
        E("gpsimd", lambda g, ch=ch: g.dma_start(out=dw1qT.ap()[ch], in_=zq_nt[:, 0:DC * 128].rearrange("p (d c) -> p d c", d=DC)),
          reads=[B_zqnt], writes=[B_dw1qT[ch]], dma=True)
    for ch in range(DC):
        for q in range(4):
            bw = B_wch[(ch * 4 + q) % 2]
            E("sync", lambda g, ch=ch, q=q: g.dma_start(
                out=wchunk[:, ((ch * 4 + q) % 2) * DM:((ch * 4 + q) % 2 + 1) * DM],
                in_=dw2[ch * 128:(ch + 1) * 128, q * DM:(q + 1) * DM]),
              writes=[bw], dma=True)
            E("scalar", lambda g, ch=ch, q=q: g.activation(
                out=scrB[:], in_=wchunk[:, ((ch * 4 + q) % 2) * DM:((ch * 4 + q) % 2 + 1) * DM],
                func=ACT.Identity, scale=sc_bc[:, 1:2]),
              reads=[bw, B_scbc], writes=[B_scrB])
            E("vector", lambda g: g.tensor_scalar(wq_t[:, 0:DM], scrB[:], RND, RND, AL.add, AL.subtract),
              reads=[B_scrB], writes=[B_wqt])
            for k in range(8):
                kc = q * 8 + k
                E("tensor", lambda g, k=k: g.transpose(pt_ab[:, k * 128:(k + 1) * 128],
                                                       wq_t[:, k * 128:(k + 1) * 128], identb[:]),
                  reads=[B_wqt, B_identb], writes=[B_pab])
                E("scalar", lambda g, k=k: g.activation(out=zq_nt[:, k * 128:(k + 1) * 128],
                                                        in_=pt_ab[:, k * 128:(k + 1) * 128],
                                                        func=ACT.Copy),
                  reads=[B_pab], writes=[B_zqnt])
            E("gpsimd", lambda g, ch=ch, q=q: g.dma_start(
                out=dw2qT.ap()[ch, q * 8:(q + 1) * 8].rearrange("k p c -> p k c"),
                in_=zq_nt[:, 0:8 * 128].rearrange("p (k c) -> p k c", k=8)),
              reads=[B_zqnt], writes=[B_dw2qT[ch]], dma=True)

    if DEBUG:
        E("gpsimd", lambda g: g.dma_start(out=dbg["d_scal2"], in_=scal[:]),
          reads=[B_scal], dma=True)
    P.barrier()

    def emit_A_proj(tc, E_):
            for tib in range(4):
                for tis in range(2):
                    ti = tc * TPC + tib * 2 + tis
                    bx = B_xt[tis % 2]
                    bz = B_z0[tis % 2]
                    E_("sync", lambda g, ti=ti, tis=tis: g.dma_start(
                        out=xt_b[:, (tis % 2) * DM:(tis % 2 + 1) * DM],
                        in_=dx[ti * 128:(ti + 1) * 128, :]),
                      writes=[bx], dma=True)
                    E_("scalar", lambda g, tis=tis: g.activation(
                        out=scrA[:], in_=xt_b[:, (tis % 2) * DM:(tis % 2 + 1) * DM],
                        func=ACT.Copy, accum_out=small[:, 0:1]),
                      reads=[bx], writes=[B_scrA, B_small])
                    E_("scalar", lambda g, tis=tis: g.activation(
                        out=scrA[:], in_=xt_b[:, (tis % 2) * DM:(tis % 2 + 1) * DM],
                        func=ACT.Square, accum_out=small[:, 1:2]),
                      reads=[bx], writes=[B_scrA, B_small])
                    E_("vector", lambda g: g.tensor_scalar_mul(small[:, 2:3], small[:, 0:1], -1.0 / DM),
                      reads=[B_small], writes=[B_small])
                    E_("scalar", lambda g: g.activation(out=small[:, 3:4], in_=small[:, 2:3],
                                                       func=ACT.Square),
                      reads=[B_small], writes=[B_small])
                    E_("vector", lambda g: g.tensor_scalar(small[:, 4:5], small[:, 1:2], 1.0 / DM, EPS,
                                                          AL.mult, AL.add),
                      reads=[B_small], writes=[B_small])
                    E_("vector", lambda g: g.tensor_sub(small[:, 5:6], small[:, 4:5], small[:, 3:4]),
                      reads=[B_small], writes=[B_small])
                    E_("scalar", lambda g: g.activation(out=small[:, 6:7], in_=small[:, 5:6], func=ACT.Ln),
                      reads=[B_small], writes=[B_small])
                    E_("scalar", lambda g: g.activation(out=small[:, 7:8], in_=small[:, 6:7], func=ACT.Exp,
                                                       scale=-0.5),
                      reads=[B_small], writes=[B_small])
                    E_("vector", lambda g, tis=tis: g.tensor_scalar(
                        z0_b[:, tis * DM:(tis + 1) * DM], xt_b[:, (tis % 2) * DM:(tis % 2 + 1) * DM],
                        small[:, 2:3], small[:, 7:8], AL.add, AL.mult),
                      reads=[bx, B_small], writes=[bz])
                    if DEBUG and ti == 0 and tis == 0:
                        E_("gpsimd", lambda g: g.dma_start(out=dbg["d_z0"], in_=z0_b[:, 0:DM]),
                          reads=[bz], dma=True)
                        E_("gpsimd", lambda g: g.dma_start(out=dbg["d_small"], in_=small[:]),
                          reads=[B_small], dma=True)
                for dc in range(DC):
                    for tis in range(2):
                        E_("tensor", lambda g, dc=dc, tis=tis: g.transpose(
                            pt_b[:, tis * 128:(tis + 1) * 128],
                            z0_b[:, tis * DM + dc * 128:tis * DM + (dc + 1) * 128], identf[:]),
                          reads=[B_z0[tis], B_identf], writes=[B_pb])
                    E_("scalar", lambda g, dc=dc, tib=tib: g.activation(
                        out=z0Tb[:, dc * TC + tib * 256:dc * TC + (tib + 1) * 256],
                        in_=pt_b[:, 0:256], func=ACT.Copy),
                      reads=[B_pb], writes=[B_z0Tb[dc]])
            for ntb in range(2):
                for dc in range(DC):
                    E_("tensor", lambda g, ntb=ntb, dc=dc: g.matmul(
                        pt_b[0:96, 0:512], WxT[:, dc * 96:(dc + 1) * 96],
                        z0Tb[:, dc * TC + ntb * 512:dc * TC + (ntb + 1) * 512],
                        start=(dc == 0), stop=(dc == DC - 1)),
                      reads=[B_WxT, B_z0Tb[dc]], writes=[B_pb])
                E_("scalar", lambda g, ntb=ntb: g.activation(out=dtr_b[:, ntb * 512:(ntb + 1) * 512],
                                                            in_=pt_b[0:64, 0:512], func=ACT.Copy),
                  reads=[B_pb], writes=[B_dtr])
                E_("scalar", lambda g, ntb=ntb: g.activation(out=BCb[:, ntb * 512:(ntb + 1) * 512],
                                                            in_=pt_b[64:96, 0:512], func=ACT.Copy),
                  reads=[B_pb], writes=[B_BCb])


    emit_A_proj(0, E)
    for tc in range(NTC):
        for dc in range(DC):
            for ntb in range(2):
                pdt = (pt_c, pt_b)[ntb % 2]
                bpdt = (B_pc, B_pb)[ntb % 2]
                E("tensor", lambda g, dc=dc, ntb=ntb, pdt=pdt: g.matmul(
                    pdt[:, 0:512], dtWT[:, dc * 128:(dc + 1) * 128],
                    dtr_b[:, ntb * 512:(ntb + 1) * 512], start=True, stop=True),
                  reads=[B_dtWT, B_dtr], writes=[bpdt])
                p_ = ntb % 2
                bspt = B_spt[p_]
                bspa = (B_spa, B_spa2)[p_]
                bspe = (B_spe, B_spe2)[p_]
                bspl = (B_spl2, B_spl3)[p_]
                tspa = scrB[:, p_ * 512:(p_ + 1) * 512]
                tspe = z0_b[:, p_ * 512:(p_ + 1) * 512]
                tspl = z0_b[:, 1024 + p_ * 512:1024 + (p_ + 1) * 512]
                pdt = (pt_c, pt_b)[p_]
                bpdt = (B_pc, B_pb)[p_]
                E("scalar", lambda g, dc=dc, p_=p_, pdt=pdt: g.activation(
                    out=spt[:, p_ * 512:(p_ + 1) * 512], in_=pdt[:, 0:512],
                    func=ACT.Identity, bias=dtb_c[:, dc:dc + 1]),
                  reads=[bpdt, B_dtb], writes=[bspt])
                E("vector", lambda g, p_=p_, tspa=tspa: g.scalar_tensor_tensor(
                    tspa, spt[:, p_ * 512:(p_ + 1) * 512], -1.0,
                    spt[:, p_ * 512:(p_ + 1) * 512], AL.mult, AL.max),
                  reads=[bspt], writes=[bspa])
                E("scalar", lambda g, tspa=tspa, tspe=tspe: g.activation(
                    out=tspe, in_=tspa, func=ACT.Exp, scale=-1.0),
                  reads=[bspa], writes=[bspe])
                E("scalar", lambda g, tspe=tspe, tspl=tspl: g.activation(
                    out=tspl, in_=tspe, func=ACT.Ln, bias=1.0),
                  reads=[bspe], writes=[bspl])
                E("vector", lambda g, dc=dc, ntb=ntb, p_=p_, tspl=tspl: g.scalar_tensor_tensor(
                    dt_t[:, dc * TC + ntb * 512:dc * TC + (ntb + 1) * 512],
                    spt[:, p_ * 512:(p_ + 1) * 512], 0.0, tspl, AL.max, AL.add),
                  reads=[bspt, bspl], writes=[B_dt[dc]])
            E("vector", lambda g, dc=dc: g.tensor_mul(dtx_t[:, dc * TC:(dc + 1) * TC],
                                                      dt_t[:, dc * TC:(dc + 1) * TC],
                                                      z0Tb[:, dc * TC:(dc + 1) * TC]),
              reads=[B_dt[dc], B_z0Tb[dc]], writes=[B_dtx[dc]])
        for s in range(S):
            for ntb in range(2):
                pbc = (pt_c, pt_d)[ntb]
                bbc = (B_pc, B_pd)[ntb]
                E("tensor", lambda g, s=s, ntb=ntb, pbc=pbc: g.matmul(
                    pbc[:, 0:512], oh32[:, s * 128:(s + 1) * 128],
                    BCb[:, ntb * 512:(ntb + 1) * 512], start=True, stop=True),
                  reads=[B_ones, B_BCb], writes=[bbc])
                E("scalar", lambda g, s=s, ntb=ntb, pbc=pbc: g.activation(
                    out=Bb[:, s * TC + ntb * 512:s * TC + (ntb + 1) * 512],
                    in_=pbc[:, 0:512], func=ACT.Copy),
                  reads=[bbc], writes=[B_Bb[s]])
            for ntb in range(2):
                pbc = (pt_c, pt_d)[ntb]
                bbc = (B_pc, B_pd)[ntb]
                E("tensor", lambda g, s=s, ntb=ntb, pbc=pbc: g.matmul(
                    pbc[:, 0:512], oh32[:, (16 + s) * 128:(17 + s) * 128],
                    BCb[:, ntb * 512:(ntb + 1) * 512], start=True, stop=True),
                  reads=[B_ones, B_BCb], writes=[bbc])
                E("scalar", lambda g, s=s, ntb=ntb, pbc=pbc: g.activation(
                    out=Cb[:, s * TC + ntb * 512:s * TC + (ntb + 1) * 512],
                    in_=pbc[:, 0:512], func=ACT.Copy),
                  reads=[bbc], writes=[B_Cb[s]])
        thunks = []
        if tc + 1 < NTC:
            emit_A_proj(tc + 1, lambda *a, **k: thunks.append((a, k)))
        tpos = [0]

        def drip(n):
            for _ in range(n):
                if tpos[0] < len(thunks):
                    a_, k_ = thunks[tpos[0]]
                    E(*a_, **k_)
                    tpos[0] += 1

        for dc in range(DC):
            E("vector", lambda g, dc=dc: g.tensor_scalar(
                yb[:, dc * TC:(dc + 1) * TC], z0Tb[:, dc * TC:(dc + 1) * TC],
                Dcol_c[:, dc:dc + 1], None, AL.mult),
              reads=[B_z0Tb[dc], B_Dcol], writes=[B_yb[dc]])
        for dc in range(DC):
            yps = y_ps
            byps = B_yps
            for s in range(S):
                k = s % 2
                st_idx = dc * S + s
                E("scalar", lambda g, dc=dc, s=s, k=k: g.activation(
                    out=dA_b[:, k * TC:(k + 1) * TC], in_=dt_t[:, dc * TC:(dc + 1) * TC],
                    func=ACT.Exp, scale=A_neg[:, dc * S + s:dc * S + s + 1]),
                  reads=[B_dt[dc], B_Aneg], writes=[B_dA[k]])
                E("vector", lambda g, dc=dc, s=s, k=k: g.tensor_mul(
                    u_b[:, k * TC:(k + 1) * TC], dtx_t[:, dc * TC:(dc + 1) * TC],
                    Bb[:, s * TC:(s + 1) * TC]),
                  reads=[B_dtx[dc], B_Bb[s]], writes=[B_u[k]])
                E("vector", lambda g, k=k, st_idx=st_idx: g.tensor_tensor_scan(
                    h_b[:, k * TC:(k + 1) * TC], dA_b[:, k * TC:(k + 1) * TC],
                    u_b[:, k * TC:(k + 1) * TC], state[:, st_idx:st_idx + 1],
                    AL.mult, AL.add),
                  reads=[B_dA[k], B_u[k], B_state[st_idx]], writes=[B_h[k]])
                E("vector", lambda g, k=k, st_idx=st_idx: g.tensor_copy(
                    state[:, st_idx:st_idx + 1], h_b[:, (k + 1) * TC - 1:(k + 1) * TC]),
                  reads=[B_h[k]], writes=[B_state[st_idx]])
                E("vector", lambda g, k=k, s=s: g.tensor_mul(
                    hc_b[:, k * TC:(k + 1) * TC], h_b[:, k * TC:(k + 1) * TC],
                    Cb[:, s * TC:(s + 1) * TC]),
                  reads=[B_h[k], B_Cb[s]], writes=[B_hc[k]])
                for ntb in range(2):
                    E("tensor", lambda g, k=k, ntb=ntb, s=s, yps=yps: g.matmul(
                        yps[:, ntb * 512:(ntb + 1) * 512], identb[:],
                        hc_b[:, k * TC + ntb * 512:k * TC + (ntb + 1) * 512],
                        start=(s == 0), stop=(s == S - 1)),
                      reads=[B_identb, B_hc[k]], writes=[byps])
            drip(2 + len(thunks) // DC)
            for ntb in range(2):
                E("vector", lambda g, dc=dc, ntb=ntb, yps=yps: g.tensor_add(
                    yb[:, dc * TC + ntb * 512:dc * TC + (ntb + 1) * 512],
                    yps[:, ntb * 512:(ntb + 1) * 512],
                    yb[:, dc * TC + ntb * 512:dc * TC + (ntb + 1) * 512]),
                  reads=[byps, B_yb[dc]], writes=[B_yb[dc]])
        drip(len(thunks))
        if DEBUG and tc == 0:
            E("gpsimd", lambda g: g.dma_start(out=dbg["d_dt"], in_=dt_t[:]),
              reads=[B_dt[7]], deps=[P.items["vector"][-1]], dma=True)
            E("gpsimd", lambda g: g.dma_start(out=dbg["d_Bb"], in_=Bb[:]),
              reads=[B_Bb[15]], dma=True)
            E("gpsimd", lambda g: g.dma_start(out=dbg["d_yb"], in_=yb[:]),
              reads=[B_yb[7]], deps=[P.items["scalar"][-1]], dma=True)
        for tloc in range(TPC):
            ti = tc * TPC + tloc
            bx = B_xt[tloc % 2]
            E("sync", lambda g, ti=ti, tloc=tloc: g.dma_start(
                out=xt_b[:, (tloc % 2) * DM:(tloc % 2 + 1) * DM],
                in_=dx[ti * 128:(ti + 1) * 128, :]),
              writes=[bx], dma=True)
            for dc in range(DC):
                pp = (pt_ab, pt_bb)[dc // 4]
                bpp = (B_pab, B_pbb)[dc // 4]
                E("tensor", lambda g, dc=dc, tloc=tloc, pp=pp: g.transpose(
                    pp[:, (dc % 4) * 128:(dc % 4 + 1) * 128],
                    yb[:, dc * TC + tloc * 128:dc * TC + (tloc + 1) * 128], identb[:]),
                  reads=[B_yb[dc], B_identb], writes=[bpp])
            E("vector", lambda g, tloc=tloc: g.tensor_add(
                x1t[:, 0:512], pt_ab[:, 0:512], xt_b[:, (tloc % 2) * DM:(tloc % 2) * DM + 512]),
              reads=[B_pab, bx], writes=[B_x1t])
            E("vector", lambda g, tloc=tloc: g.tensor_add(
                x1t[:, 512:1024], pt_bb[:, 0:512],
                xt_b[:, (tloc % 2) * DM + 512:(tloc % 2 + 1) * DM]),
              reads=[B_pbb, bx], writes=[B_x1t])
            E("scalar", lambda g: g.activation(out=scrA[:], in_=x1t[:], func=ACT.Copy,
                                               accum_out=small[:, 0:1]),
              reads=[B_x1t], writes=[B_scrA, B_small])
            E("scalar", lambda g: g.activation(out=scrA[:], in_=x1t[:], func=ACT.Square,
                                               accum_out=small[:, 1:2]),
              reads=[B_x1t], writes=[B_scrA, B_small])
            E("vector", lambda g, ti=ti: g.tensor_scalar_mul(negm2_a[:, ti:ti + 1], small[:, 0:1],
                                                             -1.0 / DM),
              reads=[B_small], writes=[B_negm2])
            E("scalar", lambda g, ti=ti: g.activation(out=small[:, 3:4], in_=negm2_a[:, ti:ti + 1],
                                                      func=ACT.Square),
              reads=[B_negm2], writes=[B_small])
            E("vector", lambda g: g.tensor_scalar(small[:, 4:5], small[:, 1:2], 1.0 / DM, EPS,
                                                  AL.mult, AL.add),
              reads=[B_small], writes=[B_small])
            E("vector", lambda g: g.tensor_sub(small[:, 5:6], small[:, 4:5], small[:, 3:4]),
              reads=[B_small], writes=[B_small])
            E("scalar", lambda g: g.activation(out=small[:, 6:7], in_=small[:, 5:6], func=ACT.Ln),
              reads=[B_small], writes=[B_small])
            E("scalar", lambda g, ti=ti: g.activation(out=r2_a[:, ti:ti + 1], in_=small[:, 6:7],
                                                      func=ACT.Exp, scale=-0.5),
              reads=[B_small], writes=[B_r2])
            E("scalar", lambda g, ti=ti: g.activation(out=scrA[:], in_=x1t[:], func=ACT.Identity,
                                                      bias=negm2_a[:, ti:ti + 1]),
              reads=[B_x1t, B_negm2], writes=[B_scrA])
            E("vector", lambda g: g.reduce_max(small[:, 3:4], scrA[:], axis=AX.X,
                                               apply_absolute_value=True),
              reads=[B_scrA], writes=[B_small])
            E("vector", lambda g, ti=ti: g.tensor_mul(small[:, 4:5], small[:, 3:4], r2_a[:, ti:ti + 1]),
              reads=[B_small, B_r2], writes=[B_small])
            ppz = ti % 2
            E("vector", lambda g, ppz=ppz: g.tensor_max(maxz_r[:, 1 - ppz:2 - ppz],
                                                        maxz_r[:, ppz:ppz + 1], small[:, 4:5]),
              reads=[B_small, B_maxz], writes=[B_maxz])
            E("gpsimd", lambda g, ti=ti: g.dma_start(out=dx1.ap()[ti * 128:(ti + 1) * 128, :],
                                                     in_=x1t[:]),
              reads=[B_x1t], writes=[B_dx1[ti]], dma=True)

    finalize_max(maxz_r[:, (NTI % 2):(NTI % 2) + 1], B_maxz, 6)
    E("vector", lambda g: g.tensor_copy(cc_sb[:], scal[0:1, 6:7].broadcast_to([1, 8])),
      reads=[B_scal], writes=[B_ccsb])
    E("gpsimd", lambda g: g.dma_start(out=cc1_in.ap(), in_=cc_sb[:]),
      reads=[B_ccsb], writes=[B_cc1i], dma=True)
    E("gpsimd", lambda g: g.collective_compute(
        "AllReduce", AL.max, replica_groups=[list(range(8))],
        ins=[cc1_in.ap().opt()], outs=[cc1_out.ap().opt()]),
      reads=[B_cc1i], writes=[B_cc1o])
    E("sync", lambda g: g.dma_start(out=cc_sb[:], in_=cc1_out.ap()),
      reads=[B_cc1o], writes=[B_ccsb], dma=True)
    E("vector", lambda g: g.tensor_copy(scal[0:1, 7:8], cc_sb[0:1, 0:1]),
      reads=[B_ccsb], writes=[B_scal])
    make_scale(7, 8, 9)
    bcast_scalar(9, 2)
    E("vector", lambda g: g.tensor_mul(scal[0:1, 10:11], scal[0:1, 8:9], scal[0:1, 2:3]),
      reads=[B_scal], writes=[B_scal])
    bcast_scalar(10, 3)
    E("vector", lambda g: g.tensor_scalar(b1s_c[:], b1_c[:], sc_bc[:, 3:4], None, AL.mult),
      reads=[B_b1, B_scbc], writes=[B_b1s])
    E("vector", lambda g: g.tensor_scalar(rq_a[:], r2_a[:], sc_bc[:, 2:3], None, AL.mult),
      reads=[B_r2, B_scbc], writes=[B_rq])
    E("vector", lambda g: g.tensor_mul(bq_a[:], negm2_a[:], rq_a[:]),
      reads=[B_negm2, B_rq], writes=[B_bq])

    P.barrier()

    E("scalar", lambda g: g.preload_act_table(ACT.Gelu))

    B_zqx = [nb("zq0"), nb("zq1")]

    wqs = [dt_t, dt_t]
    wqo = [0, 2048]
    B_wq2 = [nb("wq0"), nb("wq1")]

    def pc_quant_prep(nt, tl):
        ti = nt * 4 + tl
        bx = B_xt[tl % 2]
        E("sync", lambda g, ti=ti, tl=tl: g.dma_start(
            out=xt_b[:, (tl % 2) * DM:(tl % 2 + 1) * DM],
            in_=dx1.ap()[ti * 128:(ti + 1) * 128, :]),
          reads=[B_dx1[ti]], writes=[bx], dma=True, stream="act")
        bsc = (B_scrA, B_scrB)[tl % 2]
        tsc = (scrA, scrB)[tl % 2]
        E("scalar", lambda g, ti=ti, tl=tl, tsc=tsc: g.activation(
            out=tsc[:], in_=xt_b[:, (tl % 2) * DM:(tl % 2 + 1) * DM],
            func=ACT.Identity, scale=rq_a[:, ti:ti + 1], bias=bq_a[:, ti:ti + 1]),
          reads=[bx, B_rq, B_bq], writes=[bsc])
        o = wqo[tl % 2]
        E("vector", lambda g, tsc=tsc, o=o: g.tensor_scalar(dt_t[:, o:o + DM], tsc[:], RND, RND,
                                                            AL.add, AL.subtract),
          reads=[bsc], writes=[B_wq2[tl % 2]])

    def pc_quant_emit(nt, tl):
        zb = nt % 2
        bzq = B_zqx[zb]
        o = wqo[tl % 2]
        for dcb in range(2):
            hb = dcb % 2
            bpab = (B_pab, B_pbb)[hb]
            tpab = (pt_ab, pt_bb)[hb]
            for dcs in range(4):
                dc = dcb * 4 + dcs
                E("tensor", lambda g, dc=dc, dcs=dcs, tpab=tpab, o=o: g.transpose(
                    tpab[:, dcs * 128:(dcs + 1) * 128], dt_t[:, o + dc * 128:o + (dc + 1) * 128],
                    identb[:]),
                  reads=[B_wq2[tl % 2], B_identb], writes=[bpab])
            E("vector", lambda g, dcb=dcb, tl=tl, zb=zb, tpab=tpab: g.tensor_copy(
                zq_nt[:, zb * 4096:(zb + 1) * 4096].rearrange("p (d t) -> p d t", d=DC)[
                    :, dcb * 4:(dcb + 1) * 4, tl * 128:(tl + 1) * 128],
                tpab[:, 0:512].rearrange("p (d t) -> p d t", d=4)),
              reads=[bpab], writes=[bzq])

    def pc_quant_tl(nt, tl):
        pc_quant_prep(nt, tl)
        pc_quant_emit(nt, tl)

    W1PF = 4
    for tl in range(4):
        pc_quant_tl(0, tl)
    if DEBUG:
        E("gpsimd", lambda g: g.dma_start(out=dbg["d_zq"], in_=zq_nt[:, 0:DC * 512]),
          reads=[B_zqx[0]], dma=True)
    mm_banks = [(pt_a, B_pa), (pt_b, B_pb), (pt_c, B_pc), (pt_d, B_pd)]
    for nt in range(8):
        zb = nt % 2
        for pf in range(W1PF - 1):
            E("sync", lambda g, pf=pf: g.dma_start(
                out=w1s[:, pf * DC * 128:(pf + 1) * DC * 128],
                in_=dw1qT.ap()[pf].rearrange("p d c -> p (d c)")),
              reads=[B_dw1qT[pf]], writes=[B_w1s[pf]], dma=True)
        for hcc in range(HC):
            k = hcc % W1PF
            pp, bpp = mm_banks[hcc % 4]
            for dc in range(DC):
                E("tensor", lambda g, hcc=hcc, dc=dc, k=k, pp=pp, zb=zb: g.matmul(
                    pp[:, 0:512], w1s[:, k * DC * 128 + dc * 128:k * DC * 128 + (dc + 1) * 128],
                    zq_nt[:, zb * 4096 + dc * 512:zb * 4096 + (dc + 1) * 512],
                    start=(dc == 0), stop=(dc == DC - 1)),
                  reads=[B_w1s[k], B_zqx[zb]], writes=[bpp])
            if hcc + W1PF - 1 < HC:
                kpf = (hcc + W1PF - 1) % W1PF
                E("sync", lambda g, hcc=hcc, kpf=kpf: g.dma_start(
                    out=w1s[:, kpf * DC * 128:(kpf + 1) * DC * 128],
                    in_=dw1qT.ap()[hcc + W1PF - 1].rearrange("p d c -> p (d c)")),
                  reads=[B_dw1qT[hcc + W1PF - 1]], writes=[B_w1s[kpf]], dma=True, stream="wgt")
            bh = B_hsb[hcc % 2]
            E("scalar", lambda g, hcc=hcc, pp=pp: g.activation(
                out=hsb[:, 1024 + (hcc % 2) * 512:1024 + (hcc % 2 + 1) * 512], in_=pp[:, 0:512],
                func=ACT.Gelu, scale=sc_bc[:, 3:4], bias=b1s_c[:, hcc:hcc + 1]),
              reads=[bpp, B_scbc, B_b1s], writes=[bh])
            E("gpsimd", lambda g, hcc=hcc, nt=nt: g.dma_start(
                out=dh.ap()[nt, hcc], in_=hsb[:, 1024 + (hcc % 2) * 512:1024 + (hcc % 2 + 1) * 512]),
              reads=[bh], writes=[B_dh[nt][hcc]], dma=True, stream="hst")
            E("vector", lambda g, hcc=hcc: g.reduce_max(
                small[:, 7:8], hsb[:, 1024 + (hcc % 2) * 512:1024 + (hcc % 2 + 1) * 512], axis=AX.X,
                apply_absolute_value=True),
              reads=[bh], writes=[B_small])
            pph = (nt * HC + hcc) % 2
            E("vector", lambda g, pph=pph: g.tensor_max(maxh_r[:, 1 - pph:2 - pph],
                                                        maxh_r[:, pph:pph + 1], small[:, 7:8]),
              reads=[B_small, B_maxh], writes=[B_maxh])
            if nt + 1 < 8 and hcc % 8 == 0:
                pc_quant_prep(nt + 1, hcc // 8)
            if nt + 1 < 8 and hcc % 8 == 3:
                pc_quant_emit(nt + 1, hcc // 8)

    finalize_max(maxh_r[:, 0:1], B_maxh, 11)
    E("vector", lambda g: g.tensor_copy(cc_sb[:], scal[0:1, 11:12].broadcast_to([1, 8])),
      reads=[B_scal], writes=[B_ccsb])
    E("gpsimd", lambda g: g.dma_start(out=cc2_in.ap(), in_=cc_sb[:]),
      reads=[B_ccsb], writes=[B_cc2i], dma=True)
    E("gpsimd", lambda g: g.collective_compute(
        "AllReduce", AL.max, replica_groups=[list(range(8))],
        ins=[cc2_in.ap().opt()], outs=[cc2_out.ap().opt()]),
      reads=[B_cc2i], writes=[B_cc2o])
    E("sync", lambda g: g.dma_start(out=cc_sb[:], in_=cc2_out.ap()),
      reads=[B_cc2o], writes=[B_ccsb], dma=True)
    E("vector", lambda g: g.tensor_copy(scal[0:1, 12:13], cc_sb[0:1, 0:1]),
      reads=[B_ccsb], writes=[B_scal])
    E("scalar", _load_combined_table)
    make_scale(12, 13, 14)
    bcast_scalar(14, 4)
    E("vector", lambda g: g.tensor_mul(scal[0:1, 15:16], scal[0:1, 13:14], scal[0:1, 3:4]),
      reads=[B_scal], writes=[B_scal])
    bcast_scalar(15, 5)
    E("vector", lambda g: g.tensor_scalar(b2s_c[:], b2_c[:], sc_bc[:, 5:6], None, AL.mult),
      reads=[B_b2, B_scbc], writes=[B_b2s])
    if DEBUG:
        E("vector", lambda g: g.tensor_copy(scrB[0:1, 0:16], scal[0:1, 0:16]),
          reads=[B_scal], writes=[B_scrB])
        E("gpsimd", lambda g: g.dma_start(out=dbg["d_scal"], in_=scrB[0:1, 0:16]),
          reads=[B_scrB], dma=True)

    P.barrier()

    B_hraw = nb("hraw")

    def pd_readback(nt):
        E("sync", lambda g, nt=nt: g.dma_start(
            out=yb[:, 0:8192].rearrange("p (k c) -> p k c", k=16),
            in_=dh.ap()[nt, 0:16].rearrange("k p c -> p k c")),
          reads=[B_dh[nt][k_] for k_ in range(16)], writes=[B_hraw], dma=True, stream="hrw")
        E("sync", lambda g, nt=nt: g.dma_start(
            out=dtx_t[:, 0:8192].rearrange("p (k c) -> p k c", k=16),
            in_=dh.ap()[nt, 16:32].rearrange("k p c -> p k c")),
          reads=[B_dh[nt][k_] for k_ in range(16, 32)], writes=[B_hraw], dma=True, stream="hrw")

    def pd_quant_chunk(nt, chunk):
        kb = nt % 2
        hqt = (hq_nt_a, hq_nt_b)[kb]
        for kc in range(chunk * 4, chunk * 4 + 4):
            src = yb if kc < 16 else dtx_t
            off = (kc if kc < 16 else kc - 16) * 512
            bsc = (B_scrA, B_scrB)[kc % 2]
            tsc = (scrA, scrB)[kc % 2]
            E("scalar", lambda g, src=src, off=off, tsc=tsc: g.activation(
                out=tsc[:, 0:512], in_=src[:, off:off + 512],
                func=ACT.Identity, scale=sc_bc[:, 4:5]),
              reads=[B_hraw, B_scbc], writes=[bsc])
            E("vector", lambda g, kc=kc, hqt=hqt, tsc=tsc: g.tensor_scalar(
                hqt[:, kc * 512:(kc + 1) * 512],
                tsc[:, 0:512], RND, RND, AL.add, AL.subtract),
              reads=[bsc], writes=[B_hqnt[kb][kc // 4]])

    def pd_quant(nt):
        for chunk in range(8):
            pd_quant_chunk(nt, chunk)

    pd_readback(0)
    pd_quant(0)
    for nt in range(8):
        kb = nt % 2
        if nt + 1 < 8:
            pd_readback(nt + 1)
        E("sync", lambda g: g.dma_start(
            out=w2s[:, 0:HC * 128].rearrange("p (k c) -> p k c", k=HC),
            in_=dw2qT.ap()[0].rearrange("k p c -> p k c")),
          reads=[B_dw2qT[0]], writes=[B_w2s[0]], dma=True, stream="wgt")
        for dout in range(DC):
            k2 = dout % 2
            pp = (pt_c, pt_d)[dout % 2]
            bpp = (B_pc, B_pd)[dout % 2]
            hqt = (hq_nt_a, hq_nt_b)[kb]
            for kc in range(HC):
                E("tensor", lambda g, kc=kc, k2=k2, hqt=hqt, pp=pp: g.matmul(
                    pp[:, 0:512], w2s[:, k2 * HC * 128 + kc * 128:k2 * HC * 128 + (kc + 1) * 128],
                    hqt[:, kc * 512:(kc + 1) * 512],
                    start=(kc == 0), stop=(kc == HC - 1)),
                  reads=[B_w2s[k2], B_hqnt[kb][kc // 4]], writes=[bpp])
            if dout + 1 < DC:
                E("sync", lambda g, dout=dout, k2=k2: g.dma_start(
                    out=w2s[:, (1 - k2) * HC * 128:(2 - k2) * HC * 128].rearrange("p (k c) -> p k c", k=HC),
                    in_=dw2qT.ap()[dout + 1].rearrange("k p c -> p k c")),
                  reads=[B_dw2qT[dout + 1]], writes=[B_w2s[1 - k2]], dma=True, stream="wgt")
            bx2 = B_x2f[dout % 2]
            E("scalar", lambda g, dout=dout, pp=pp: g.activation(
                out=x2f[:, (dout % 2) * 512:(dout % 2) * 512 + 512], in_=pp[:, 0:512],
                func=ACT.Identity, scale=sc_bc[:, 5:6], bias=b2s_c[:, dout:dout + 1]),
              reads=[bpp, B_scbc, B_b2s], writes=[bx2])
            ppt = (pt_a, pt_b)[dout % 2]
            bppt = (B_pa, B_pb)[dout % 2]
            for sub in range(4):
                E("tensor", lambda g, sub=sub, dout=dout, ppt=ppt: g.transpose(
                    ppt[:, sub * 128:(sub + 1) * 128],
                    x2f[:, (dout % 2) * 512 + sub * 128:(dout % 2) * 512 + (sub + 1) * 128],
                    identf[:]),
                  reads=[bx2, B_identf], writes=[bppt])
            bq1 = B_x1q[dout % 2]
            E("sync", lambda g, nt=nt, dout=dout: g.dma_start(
                out=x1q[:, 1024 + (dout % 2) * 512:1024 + (dout % 2) * 512 + 512].rearrange("p (s c) -> p s c", s=4),
                in_=dx1.ap().rearrange("(n s p) d -> n p s d", s=4, p=128)[nt, :, :,
                                                                          dout * 128:(dout + 1) * 128]),
              reads=[B_dx1[nt * 4], B_dx1[nt * 4 + 1], B_dx1[nt * 4 + 2], B_dx1[nt * 4 + 3]],
              writes=[bq1], dma=True, stream="act")
            bo = B_outt[dout % 2]
            E("vector", lambda g, dout=dout, ppt=ppt: g.tensor_add(
                outt[:, (dout % 2) * 512:(dout % 2) * 512 + 512], ppt[:, 0:512],
                x1q[:, 1024 + (dout % 2) * 512:1024 + (dout % 2) * 512 + 512]),
              reads=[bppt, bq1], writes=[bo])
            E("gpsimd", lambda g, nt=nt, dout=dout: g.dma_start(
                out=dout_view[nt, :, :, dout * 128:(dout + 1) * 128],
                in_=outt[:, (dout % 2) * 512:(dout % 2) * 512 + 512].rearrange("p (s c) -> p s c", s=4)),
              reads=[bo], dma=True, stream="out")
            if nt + 1 < 8:
                pd_quant_chunk(nt + 1, dout)

    if DEBUG:
        for ti in range(NTI):
            E("sync", lambda g, ti=ti: g.dma_start(out=xt_b[:, 0:DM],
                                                   in_=dx1.ap()[ti * 128:(ti + 1) * 128, :]),
              reads=[B_dx1[ti]], writes=[B_xt[0]], dma=True)
            E("gpsimd", lambda g, ti=ti: g.dma_start(out=dbg["d_x1"][ti * 128:(ti + 1) * 128, :],
                                                     in_=xt_b[:, 0:DM]),
              reads=[B_xt[0]], dma=True)
            E("sync", lambda g, ti=ti: g.dma_start(out=xt_b[:, DM:2 * DM],
                                                   in_=dxa.ap()[ti * 128:(ti + 1) * 128, :]),
              reads=[B_dxa[ti]], writes=[B_xt[1]], dma=True)
            E("gpsimd", lambda g, ti=ti: g.dma_start(out=dbg["d_xa"][ti * 128:(ti + 1) * 128, :],
                                                     in_=xt_b[:, DM:2 * DM]),
              reads=[B_xt[1]], dma=True)

    P.barrier()

    P.finalize()

    with es, nc.Block() as block:
        @block.sync
        def _(eng):
            P.replay("sync", eng, esem, dsem, ssem)
            for e in ENGINES:
                n = sum(1 for it in P.items[e] if it.is_dma and it.stream is None)
                if n:
                    eng.wait_ge(dsem[e], 16 * n)
            for k in Prog.STREAMS:
                n = sum(1 for e in ENGINES for it in P.items[e]
                        if it.is_dma and it.stream == k)
                if n:
                    eng.wait_ge(ssem[k], 16 * n)

        @block.scalar
        def _(eng):
            P.replay("scalar", eng, esem, dsem, ssem)

        @block.vector
        def _(eng):
            P.replay("vector", eng, esem, dsem, ssem)

        @block.tensor
        def _(eng):
            P.replay("tensor", eng, esem, dsem, ssem)

        @block.gpsimd
        def _(eng):
            P.replay("gpsimd", eng, esem, dsem, ssem)
            n = sum(1 for it in P.items["gpsimd"] if it.is_dma and it.stream is None)
            eng.wait_ge(dsem["gpsimd"], 16 * n)

    return nc, P


def kernel(L0_attn_x, L0_attn_A_log, L0_attn_D, L0_attn_Wx, L0_attn_dt_W,
           L0_attn_dt_b, L0_ffn_w1, L0_ffn_b1, L0_ffn_w2, L0_ffn_b2):
    bf = ml_dtypes.bfloat16
    x = np.ascontiguousarray(np.asarray(L0_attn_x, dtype=np.float32))
    ident_f = np.eye(128, dtype=np.float32)
    ident_b = np.eye(128).astype(bf)
    ones_b = np.ones((1, 128), dtype=np.float32).astype(bf)
    ones_f = np.ones((1, 128), dtype=np.float32)
    onehot32 = np.zeros((32, 32 * 128), dtype=np.float32)
    for s_ in range(32):
        onehot32[s_, s_ * 128:(s_ + 1) * 128] = 1.0
    onehot32 = onehot32.astype(bf)

    nc, P = build_nc()
    common = dict(
        A_log=np.ascontiguousarray(np.asarray(L0_attn_A_log, dtype=np.float32)),
        D_vec=np.ascontiguousarray(np.asarray(L0_attn_D, dtype=np.float32)),
        Wx=np.ascontiguousarray(np.asarray(L0_attn_Wx, dtype=np.float32)),
        dt_W=np.ascontiguousarray(np.asarray(L0_attn_dt_W, dtype=np.float32)),
        dt_b=np.ascontiguousarray(np.asarray(L0_attn_dt_b, dtype=np.float32)),
        w1=np.ascontiguousarray(np.asarray(L0_ffn_w1, dtype=np.float32)),
        b1=np.ascontiguousarray(np.asarray(L0_ffn_b1, dtype=np.float32)),
        w2=np.ascontiguousarray(np.asarray(L0_ffn_w2, dtype=np.float32)),
        b2=np.ascontiguousarray(np.asarray(L0_ffn_b2, dtype=np.float32)),
        ident_f=ident_f, ident_b=ident_b, ones_b=ones_b,
        ones_f=ones_f, onehot32=onehot32,
    )
    in_maps = [dict(x_in=np.ascontiguousarray(x[i]), **common) for i in range(8)]
    res = run_bass_kernel_spmd(nc, in_maps, core_ids=list(range(8)))
    out = np.stack([np.asarray(res.results[i]["out"], dtype=np.float32) for i in range(8)], axis=0)
    if DEBUG:
        kernel._dbg = res
    return out



# version 16
# speedup vs baseline: 1.5566x; 1.1308x over previous
import numpy as np
import ml_dtypes
import concourse.bass as bass
import concourse.mybir as mybir
from concourse.bass_utils import run_bass_kernel_spmd
from contextlib import ExitStack

F32 = mybir.dt.float32
BF16 = mybir.dt.bfloat16
AL = mybir.AluOpType
ACT = mybir.ActivationFunctionType
AX = mybir.AxisListType

L = 4096
DM = 1024
DC = 8
S = 16
R = 64
H = 4096
HC = 32
TC = 1024
NTC = L // TC
NTI = L // 128
TPC = TC // 128
QP = 127.0
EPS = 1e-5
RND = float(1.5 * 2.0 ** 23)

DEBUG = False

ENGINES = ("sync", "scalar", "vector", "tensor", "gpsimd")


class Buf:
    __slots__ = ("writer", "readers", "name")

    def __init__(self, name=""):
        self.writer = None
        self.readers = []
        self.name = name


class Item:
    __slots__ = ("eng", "fn", "deps", "marked", "ord", "is_dma", "idx", "dmacnt", "nodrain",
                 "stream", "stream_ord")

    def __init__(self, eng, fn, deps, is_dma):
        self.eng = eng
        self.fn = fn
        self.deps = deps
        self.marked = False
        self.ord = None
        self.is_dma = is_dma
        self.idx = None
        self.dmacnt = None
        self.nodrain = False
        self.stream = None
        self.stream_ord = None


class Prog:
    def __init__(self):
        self.items = {e: [] for e in ENGINES}
        self.bar = {e: None for e in ENGINES}
        self.dmac = {e: 0 for e in ENGINES}

    STREAMS = ("wgt", "hrw", "act", "hst", "out")

    def emit(self, eng, fn, reads=(), writes=(), deps=(), dma=False, nodrain=False, stream=None):
        dd = []
        for b in reads:
            if b.writer is not None:
                dd.append(b.writer)
        for b in writes:
            if b.writer is not None:
                dd.append(b.writer)
            dd.extend(b.readers)
        dd.extend(deps)
        if self.bar[eng] is not None:
            dd.append(self.bar[eng])
        it = Item(eng, fn, dd, dma)
        it.nodrain = nodrain
        it.idx = len(self.items[eng])
        it.dmacnt = dict(self.dmac)
        if dma:
            if stream is not None:
                it.stream = stream
                cnt = getattr(self, "_scnt", None)
                if cnt is None:
                    cnt = self._scnt = {k: 0 for k in Prog.STREAMS}
                cnt[stream] += 1
                it.stream_ord = cnt[stream]
            else:
                self.dmac[eng] += 1
        self.items[eng].append(it)
        for b in reads:
            b.readers.append(it)
        for b in writes:
            b.writer = it
            b.readers = []
        for d in dd:
            if d.eng != eng or d.is_dma:
                d.marked = True
        return it

    def barrier(self):
        lasts = [self.items[e][-1] for e in ENGINES if self.items[e]]
        newbar = {}
        for e in ENGINES:
            it = Item(e, lambda eng: eng.nop(), list(lasts), False)
            it.idx = len(self.items[e])
            it.dmacnt = dict(self.dmac)
            self.items[e].append(it)
            newbar[e] = it
        for d in lasts:
            d.marked = True
        for e in ENGINES:
            self.bar[e] = newbar[e]

    def finalize(self):
        for e in ENGINES:
            o = 0
            do = 0
            for it in self.items[e]:
                if it.is_dma:
                    do += 1
                    it.ord = do
                    it.marked = True
                elif it.marked:
                    o += 1
                    it.ord = o
        self.ninstr = {e: len(self.items[e]) for e in ENGINES}

    FENCE = 12

    def replay(self, eng_name, eng_obj, esem, dsem, ssem):
        waited = {}
        ndma = 0
        last_drain = -1
        need_drain_engines = eng_name in ("vector", "gpsimd")
        for it in self.items[eng_name]:
            if need_drain_engines and not it.nodrain:
                for d in it.deps:
                    if (not d.is_dma) and d.eng == eng_name and d.idx > last_drain:
                        eng_obj.drain()
                        last_drain = it.idx - 1
                        break
            for d in it.deps:
                if d.is_dma:
                    if d.stream is not None:
                        sem = ssem[d.stream]
                        val = 16 * d.stream_ord
                    else:
                        sem = dsem[d.eng]
                        val = 16 * it.dmacnt[d.eng]
                elif d.eng == eng_name:
                    continue
                else:
                    if d.ord is None:
                        continue
                    sem = esem[d.eng]
                    val = d.ord
                key = id(sem)
                if waited.get(key, -1) < val:
                    eng_obj.wait_ge(sem, val)
                    waited[key] = val
            if it.is_dma and it.stream is None:
                ndma += 1
                if ndma > Prog.FENCE:
                    val = 16 * (ndma - Prog.FENCE)
                    key = id(dsem[eng_name])
                    if waited.get(key, -1) < val:
                        eng_obj.wait_ge(dsem[eng_name], val)
                        waited[key] = val
            inst = it.fn(eng_obj)
            if it.is_dma:
                if it.stream is not None:
                    inst.then_inc(ssem[it.stream], 16)
                else:
                    inst.then_inc(dsem[it.eng], 16)
            elif it.marked:
                inst.then_inc(esem[it.eng], 1)


def build_nc():
    nc = bass.Bass("TRN2", target_bir_lowering=False, debug=False)

    dx = nc.dram_tensor("x_in", [L, DM], F32, kind="ExternalInput").ap()
    dAlog = nc.dram_tensor("A_log", [DM, S], F32, kind="ExternalInput").ap()
    dD = nc.dram_tensor("D_vec", [DM], F32, kind="ExternalInput").ap()
    dWx = nc.dram_tensor("Wx", [96, DM], F32, kind="ExternalInput").ap()
    ddtW = nc.dram_tensor("dt_W", [DM, R], F32, kind="ExternalInput").ap()
    ddtb = nc.dram_tensor("dt_b", [DM], F32, kind="ExternalInput").ap()
    dw1 = nc.dram_tensor("w1", [H, DM], F32, kind="ExternalInput").ap()
    db1 = nc.dram_tensor("b1", [H], F32, kind="ExternalInput").ap()
    dw2 = nc.dram_tensor("w2", [DM, H], F32, kind="ExternalInput").ap()
    db2 = nc.dram_tensor("b2", [DM], F32, kind="ExternalInput").ap()
    didf = nc.dram_tensor("ident_f", [128, 128], F32, kind="ExternalInput").ap()
    didb = nc.dram_tensor("ident_b", [128, 128], BF16, kind="ExternalInput").ap()
    dones = nc.dram_tensor("ones_b", [1, 128], BF16, kind="ExternalInput").ap()
    donesf = nc.dram_tensor("ones_f", [1, 128], F32, kind="ExternalInput").ap()
    doh32 = nc.dram_tensor("onehot32", [32, 32 * 128], BF16, kind="ExternalInput").ap()
    dw1sl = nc.dram_tensor("w1sl", [512, DM], F32, kind="ExternalInput").ap()
    dw2sl = nc.dram_tensor("w2sl", [128, 4 * DM], F32, kind="ExternalInput").ap()

    dout = nc.dram_tensor("out", [L, DM], F32, kind="ExternalOutput").ap()
    dout_view = dout.rearrange("(n s p) d -> n p s d", s=4, p=128)

    dxa = nc.dram_tensor("xa_hbm", [L, DM], F32)
    dx1 = nc.dram_tensor("x1_hbm", [L, DM], F32)
    dh = nc.dram_tensor("h_hbm", [8, HC, 128, 512], BF16)
    dw1qT = nc.dram_tensor("w1qT_hbm", [HC, 128, DC, 128], BF16)
    dw2qT = nc.dram_tensor("w2qT_hbm", [DC, HC, 128, 128], BF16)
    ag1_in = nc.dram_tensor("ag1_in", [4, 128, DC, 128], BF16)
    ag2_in = nc.dram_tensor("ag2_in", [1, HC, 128, 128], BF16)
    ccw_in = nc.dram_tensor("ccw_in", [1, 8], F32)
    ccw_out = nc.dram_tensor("ccw_out", [1, 8], F32)
    cc1_in = nc.dram_tensor("cc1_in", [1, 8], F32)
    cc1_out = nc.dram_tensor("cc1_out", [1, 8], F32)
    cc2_in = nc.dram_tensor("cc2_in", [1, 8], F32)
    cc2_out = nc.dram_tensor("cc2_out", [1, 8], F32)

    dbg = {}
    if DEBUG:
        dbg["d_x1"] = nc.dram_tensor("dbg_x1", [L, DM], F32, kind="ExternalOutput").ap()
        dbg["d_scal"] = nc.dram_tensor("dbg_scal", [1, 16], F32, kind="ExternalOutput").ap()
        dbg["d_zq"] = nc.dram_tensor("dbg_zq", [128, DC * 512], F32, kind="ExternalOutput").ap()
        dbg["d_xa"] = nc.dram_tensor("dbg_xa", [L, DM], F32, kind="ExternalOutput").ap()
        dbg["d_dt"] = nc.dram_tensor("dbg_dt", [128, DC * TC], F32, kind="ExternalOutput").ap()
        dbg["d_yb"] = nc.dram_tensor("dbg_yb", [128, DC * TC], F32, kind="ExternalOutput").ap()
        dbg["d_Bb"] = nc.dram_tensor("dbg_Bb", [128, S * TC], F32, kind="ExternalOutput").ap()
        dbg["d_z0"] = nc.dram_tensor("dbg_z0", [128, DM], F32, kind="ExternalOutput").ap()
        dbg["d_small"] = nc.dram_tensor("dbg_small", [128, 8], F32, kind="ExternalOutput").ap()
        dbg["d_scal2"] = nc.dram_tensor("dbg_scal2", [1, 24], F32, kind="ExternalOutput").ap()

    es = ExitStack()
    sb = lambda n_, s_, d_: es.enter_context(nc.sbuf_tensor(n_, s_, d_))
    pst = lambda n_, s_, d_: es.enter_context(nc.psum_tensor(n_, s_, d_))

    identf = sb("identf", [128, 128], F32)
    identb = sb("identb", [128, 128], BF16)
    ones1 = sb("ones1", [1, 128], BF16)
    ones1f = sb("ones1f", [1, 128], F32)
    oh32 = sb("oh32", [32, 32 * 128], BF16)
    A_neg = sb("A_neg", [128, DC * S], F32)
    dtb_c = sb("dtb_c", [128, DC], F32)
    Dcol_c = sb("Dcol_c", [128, DC], F32)
    b1_c = sb("b1_c", [128, HC], F32)
    b2_c = sb("b2_c", [128, DC], F32)
    b1s_c = sb("b1s_c", [128, HC], F32)
    b2s_c = sb("b2s_c", [128, DC], F32)
    state = sb("state", [128, DC * S], F32)
    negm2_a = sb("negm2_a", [128, NTI], F32)
    r2_a = sb("r2_a", [128, NTI], F32)
    rq_a = sb("rq_a", [128, NTI], F32)
    bq_a = sb("bq_a", [128, NTI], F32)
    maxz_r = sb("maxz_r", [128, 2], F32)
    maxh_r = sb("maxh_r", [128, 2], F32)
    scal = sb("scal", [1, 24], F32)
    cc_sb = sb("cc_sb", [1, 8], F32)
    sc_bc = sb("sc_bc", [128, 8], F32)
    WxT = sb("WxT", [128, DC * 96], BF16)
    dtWT = sb("dtWT", [64, DC * 128], BF16)

    z0Tb = sb("z0Tb", [128, DC * TC], BF16)
    dt_t = sb("dt_t", [128, DC * TC], BF16)
    dtx_t = sb("dtx_t", [128, DC * TC], BF16)
    Bb = sb("Bb", [128, S * TC], BF16)
    Cb = sb("Cb", [128, S * TC], BF16)
    dtr_b = sb("dtr_b", [64, TC], BF16)
    BCb = sb("BCb", [32, TC], BF16)
    yb = sb("yb", [128, DC * TC], BF16)
    dA_b = sb("dA_b", [128, 2 * TC], F32)
    u_b = sb("u_b", [128, 2 * TC], BF16)
    h_b = sb("h_b", [128, 2 * TC], F32)
    hc_b = sb("hc_b", [128, 2 * TC], BF16)
    xt_b = sb("xt_b", [128, 2 * DM], F32)
    z0_b = sb("z0_b", [128, 2 * DM], F32)
    scrA = sb("scrA", [128, DM], F32)
    scrB = sb("scrB", [128, DM], F32)
    x1t = sb("x1t", [128, DM], F32)
    spt = scrA
    spa = scrB
    spe_ = scrB
    spl = x1t
    small = sb("small", [128, 8], F32)

    wchunk = z0_b
    wq_t = dt_t
    zq_nt = z0Tb
    w1s = dtx_t
    hq_nt_a = Bb
    hq_nt_b = Cb
    hin_b = dtx_t
    w2s = dt_t
    hsb = dt_t
    x2f = z0_b
    x1q = z0_b
    outt = x1t

    pt_a = pst("pt_a", [128, 512], F32)
    pt_b = pst("pt_b", [128, 512], F32)
    pt_c = pst("pt_c", [128, 512], F32)
    pt_d = pst("pt_d", [128, 512], F32)
    y_ps = pst("y_ps", [128, 1024], F32)
    pt_ab = pst("pt_ab", [128, 1024], BF16)
    pt_bb = pst("pt_bb", [128, 1024], BF16)

    esem = {}
    dsem = {}
    for e in ENGINES:
        esem[e] = es.enter_context(nc.semaphore("sem_" + e))
        dsem[e] = es.enter_context(nc.semaphore("dma_" + e))
    ssem = {k: es.enter_context(nc.semaphore("sdma_" + k)) for k in Prog.STREAMS}

    P = Prog()
    nb = lambda n: Buf(n)

    B_identf = nb("identf"); B_identb = nb("identb"); B_ones = nb("ones")
    B_Aneg = nb("A_neg"); B_dtb = nb("dtb"); B_b1 = nb("b1"); B_b2 = nb("b2")
    B_b1s = nb("b1s"); B_b2s = nb("b2s")
    B_Dbc = nb("Dbc"); B_Dcol = nb("Dcol"); B_Ddiag = nb("Ddiag")
    B_state = [nb(f"st{i}") for i in range(DC * S)]
    B_negm2 = nb("negm2"); B_r2 = nb("r2"); B_rq = nb("rq"); B_bq = nb("bq")
    B_maxz = nb("maxz"); B_maxh = nb("maxh")
    B_scal = nb("scal"); B_ccsb = nb("ccsb"); B_scbc = nb("scbc")
    B_ag1 = nb("ag1"); B_ag2 = nb("ag2")
    B_ccwi = nb("ccwi"); B_ccwo = nb("ccwo")
    B_WxT = nb("WxT"); B_dtWT = nb("dtWT")
    B_z0Tb = [nb(f"z0Tb{d}") for d in range(DC)]
    B_dt = [nb(f"dt{d}") for d in range(DC)]
    B_dtx = [nb(f"dtx{d}") for d in range(DC)]
    B_Bb = [nb(f"Bb{s}") for s in range(S)]
    B_Cb = [nb(f"Cb{s}") for s in range(S)]
    B_dtr = nb("dtr"); B_BCb = nb("BCb")
    B_yb = [nb(f"yb{d}") for d in range(DC)]
    B_dA = [nb("dA0"), nb("dA1")]
    B_u = [nb("u0"), nb("u1")]
    B_h = [nb("h0"), nb("h1")]
    B_hc = [nb("hc0"), nb("hc1")]
    B_xt = [nb("xt0"), nb("xt1")]
    B_z0 = [nb(f"z0_{i}") for i in range(2)]
    B_scrA = nb("scrA"); B_scrB = nb("scrB")
    B_x1t = nb("x1t")
    B_spt = [B_scrA, B_scrA]; B_spa = B_scrB; B_spa2 = B_scrB
    B_spe = nb('spe0'); B_spe2 = nb('spe1'); B_spl2 = nb('spl0'); B_spl3 = nb('spl1')
    B_small = nb("small")
    B_pa = nb("pt_a"); B_pb = nb("pt_b"); B_pc = nb("pt_c"); B_pd = nb("pt_d")
    B_yps = nb("yps"); B_pab = nb("pt_ab"); B_pbb = nb("pt_bb")
    B_wch = [nb("wch0"), nb("wch1")]; B_wqt = nb("wq_t")
    B_zqnt = nb("zq_nt"); B_w1s = [nb(f"w1s{i}") for i in range(4)]
    B_hqnt = [[nb(f"hq{b_}_{c_}") for c_ in range(8)] for b_ in range(2)]
    B_hin = [nb("hin0"), nb("hin1")]
    B_w2s = [nb("w2s0"), nb("w2s1")]; B_hsb = [nb("hsb0"), nb("hsb1")]
    B_x2f = [nb("x2f0"), nb("x2f1")]; B_x1q = [nb("x1q0"), nb("x1q1")]
    B_outt = [nb("outt0"), nb("outt1")]
    B_dxa = [nb(f"dxa{i}") for i in range(NTI)]
    B_dx1 = [nb(f"dx1{i}") for i in range(NTI)]
    B_dh = [[nb(f"dh{a}_{b}") for b in range(HC)] for a in range(8)]
    B_dw1qT = [nb(f"dw1qT{a}") for a in range(HC)]
    B_dw2qT = [nb(f"dw2qT{a}") for a in range(DC)]
    B_cc1i = nb("cc1i"); B_cc1o = nb("cc1o"); B_cc2i = nb("cc2i"); B_cc2o = nb("cc2o")

    E = P.emit

    def _load_combined_table(g):
        from concourse.hw_specs import get_activation_tables
        for idx, (nm, funcs) in enumerate(get_activation_tables(nc.m.arch).items()):
            if ACT.Exp in funcs and ACT.Ln in funcs:
                return g.add_instruction(mybir.InstLoadActFuncSet(
                    name=nc.get_next_instruction_name(), act_func_set_id=idx, ins=[], outs=[]))
        raise ValueError("no combined exp+ln table")
    E("scalar", _load_combined_table)
    E("sync", lambda g: g.dma_start(out=identf[:], in_=didf), writes=[B_identf], dma=True)
    E("sync", lambda g: g.dma_start(out=identb[:], in_=didb), writes=[B_identb], dma=True)
    E("sync", lambda g: g.dma_start(out=ones1[:], in_=dones), writes=[B_ones], dma=True)
    E("sync", lambda g: g.dma_start(out=ones1f[:], in_=donesf), writes=[B_ones], dma=True)
    E("sync", lambda g: g.dma_start(out=oh32[:], in_=doh32), writes=[B_ones], dma=True)
    for dc in range(DC):
        E("sync", lambda g, dc=dc: g.dma_start(out=scrB[:, dc * S:(dc + 1) * S],
                                               in_=dAlog[dc * 128:(dc + 1) * 128, :]),
          writes=[B_scrB], dma=True)
    E("scalar", lambda g: g.activation(out=scrB[:, 128:256], in_=scrB[:, 0:128], func=ACT.Exp),
      reads=[B_scrB], writes=[B_scrB])
    E("vector", lambda g: g.tensor_scalar_mul(A_neg[:], scrB[:, 128:256], -1.0),
      reads=[B_scrB], writes=[B_Aneg])
    for dc in range(DC):
        E("sync", lambda g, dc=dc: g.dma_start(out=dtb_c[:, dc:dc + 1],
                                               in_=ddtb[dc * 128:(dc + 1) * 128].rearrange("(p o) -> p o", o=1)),
          writes=[B_dtb], dma=True)
        E("sync", lambda g, dc=dc: g.dma_start(out=b2_c[:, dc:dc + 1],
                                               in_=db2[dc * 128:(dc + 1) * 128].rearrange("(p o) -> p o", o=1)),
          writes=[B_b2], dma=True)
    for hcc in range(HC):
        E("sync", lambda g, hcc=hcc: g.dma_start(out=b1_c[:, hcc:hcc + 1],
                                                 in_=db1[hcc * 128:(hcc + 1) * 128].rearrange("(p o) -> p o", o=1)),
          writes=[B_b1], dma=True)
    for dc in range(DC):
        E("sync", lambda g, dc=dc: g.dma_start(out=Dcol_c[:, dc:dc + 1],
                                               in_=dD[dc * 128:(dc + 1) * 128].rearrange("(p o) -> p o", o=1)),
          writes=[B_Dcol], dma=True)

    E("vector", lambda g: g.memset(state[:], 0.0), writes=B_state)
    E("vector", lambda g: g.memset(maxz_r[:], 0.0), writes=[B_maxz])
    E("vector", lambda g: g.memset(maxh_r[:], 0.0), writes=[B_maxh])
    E("sync", lambda g: g.dma_start(out=wchunk[0:96, 0:DM], in_=dWx), writes=[B_wch[0]], dma=True)
    E("vector", lambda g: g.tensor_copy(wq_t[0:96, 0:DM], wchunk[0:96, 0:DM]),
      reads=[B_wch[0]], writes=[B_wqt])
    for dc in range(DC):
        E("tensor", lambda g, dc=dc: g.transpose(pt_ab[0:128, 0:96],
                                                 wq_t[0:96, dc * 128:(dc + 1) * 128],
                                                 identb[0:96, 0:96]),
          reads=[B_wqt, B_identb], writes=[B_pab])
        E("scalar", lambda g, dc=dc: g.activation(out=WxT[:, dc * 96:(dc + 1) * 96],
                                                  in_=pt_ab[0:128, 0:96], func=ACT.Copy),
          reads=[B_pab], writes=[B_WxT])
    for dc in range(DC):
        E("sync", lambda g, dc=dc: g.dma_start(out=wchunk[:, 0:R],
                                               in_=ddtW[dc * 128:(dc + 1) * 128, :]),
          writes=[B_wch[0]], dma=True)
        E("vector", lambda g, dc=dc: g.tensor_copy(wq_t[:, 0:R], wchunk[:, 0:R]),
          reads=[B_wch[0]], writes=[B_wqt])
        E("tensor", lambda g, dc=dc: g.transpose(pt_ab[0:R, 0:128], wq_t[:, 0:R], identb[:]),
          reads=[B_wqt, B_identb], writes=[B_pab])
        E("scalar", lambda g, dc=dc: g.activation(out=dtWT[:, dc * 128:(dc + 1) * 128],
                                                  in_=pt_ab[0:R, 0:128], func=ACT.Copy),
          reads=[B_pab], writes=[B_dtWT])

    def finalize_max(run_tile, run_buf, idx):
        E("tensor", lambda g: g.transpose(pt_a[0:1, 0:128], run_tile, identf[:]),
          reads=[run_buf, B_identf], writes=[B_pa])
        E("vector", lambda g: g.tensor_copy(scrB[0:1, 0:128], pt_a[0:1, 0:128]),
          reads=[B_pa], writes=[B_scrB])
        E("vector", lambda g, idx=idx: g.reduce_max(scal[0:1, idx:idx + 1], scrB[0:1, 0:128], axis=AX.X),
          reads=[B_scrB], writes=[B_scal])

    E("vector", lambda g: g.memset(small[:, 0:2], 0.0), writes=[B_small])
    E("vector", lambda g: g.memset(small[:, 2:4], 0.0), writes=[B_small])
    for ch in range(4):
        bw = B_wch[ch % 2]
        E("sync", lambda g, ch=ch: g.dma_start(
            out=wchunk[:, (ch % 2) * DM:(ch % 2 + 1) * DM],
            in_=dw1sl[ch * 128:(ch + 1) * 128, :]),
          writes=[bw], dma=True)
        E("vector", lambda g, ch=ch: g.reduce_max(small[:, 4:5],
                                                  wchunk[:, (ch % 2) * DM:(ch % 2 + 1) * DM],
                                                  axis=AX.X, apply_absolute_value=True),
          reads=[bw], writes=[B_small])
        pp_ = ch % 2
        E("vector", lambda g, pp_=pp_: g.tensor_max(small[:, 1 - pp_:2 - pp_],
                                                    small[:, pp_:pp_ + 1], small[:, 4:5]),
          reads=[B_small], writes=[B_small])
    for q in range(4):
        bw = B_wch[q % 2]
        E("sync", lambda g, q=q: g.dma_start(
            out=wchunk[:, (q % 2) * DM:(q % 2 + 1) * DM],
            in_=dw2sl[:, q * DM:(q + 1) * DM]),
          writes=[bw], dma=True)
        E("vector", lambda g, q=q: g.reduce_max(small[:, 4:5],
                                                wchunk[:, (q % 2) * DM:(q % 2 + 1) * DM],
                                                axis=AX.X, apply_absolute_value=True),
          reads=[bw], writes=[B_small])
        pp_ = q % 2
        E("vector", lambda g, pp_=pp_: g.tensor_max(small[:, 3 - pp_:4 - pp_],
                                                    small[:, 2 + pp_:3 + pp_], small[:, 4:5]),
          reads=[B_small], writes=[B_small])
    finalize_max(small[:, 0:1], B_small, 0)
    finalize_max(small[:, 2:3], B_small, 1)
    E("vector", lambda g: g.memset(cc_sb[:], 0.0), writes=[B_ccsb])
    E("vector", lambda g: g.tensor_copy(cc_sb[0:1, 0:2], scal[0:1, 0:2]),
      reads=[B_scal], writes=[B_ccsb])
    E("gpsimd", lambda g: g.dma_start(out=ccw_in.ap(), in_=cc_sb[:]),
      reads=[B_ccsb], writes=[B_ccwi], dma=True)
    E("gpsimd", lambda g: g.collective_compute(
        "AllReduce", AL.max, replica_groups=[list(range(8))],
        ins=[ccw_in.ap().opt()], outs=[ccw_out.ap().opt()]),
      reads=[B_ccwi], writes=[B_ccwo])
    E("sync", lambda g: g.dma_start(out=cc_sb[:], in_=ccw_out.ap()),
      reads=[B_ccwo], writes=[B_ccsb], dma=True)
    E("vector", lambda g: g.tensor_copy(scal[0:1, 0:2], cc_sb[0:1, 0:2]),
      reads=[B_ccsb], writes=[B_scal])

    def make_scale(src_idx, s_idx, inv_idx):
        E("vector", lambda g, src_idx=src_idx, s_idx=s_idx: g.tensor_scalar(
            scal[0:1, s_idx:s_idx + 1], scal[0:1, src_idx:src_idx + 1],
            1e-6, 1.0 / QP, AL.max, AL.mult),
          reads=[B_scal], writes=[B_scal])
        E("scalar", lambda g, s_idx=s_idx: g.activation(out=scal[0:1, 16:17],
                                                        in_=scal[0:1, s_idx:s_idx + 1], func=ACT.Ln),
          reads=[B_scal], writes=[B_scal])
        E("scalar", lambda g, inv_idx=inv_idx: g.activation(out=scal[0:1, inv_idx:inv_idx + 1],
                                                            in_=scal[0:1, 16:17], func=ACT.Exp,
                                                            scale=-1.0),
          reads=[B_scal], writes=[B_scal])

    make_scale(0, 2, 4)
    make_scale(1, 3, 5)

    def bcast_scalar(idx, dst_col):
        E("tensor", lambda g, idx=idx: g.matmul(pt_a[:, 0:1], ones1f[:], scal[0:1, idx:idx + 1],
                                                start=True, stop=True),
          reads=[B_scal, B_ones], writes=[B_pa])
        E("scalar", lambda g, dst_col=dst_col: g.activation(out=sc_bc[:, dst_col:dst_col + 1],
                                                            in_=pt_a[:, 0:1], func=ACT.Copy),
          reads=[B_pa], writes=[B_scbc])

    bcast_scalar(4, 0)
    bcast_scalar(5, 1)

    for ch in range(4):
        bw = B_wch[ch % 2]
        E("sync", lambda g, ch=ch: g.dma_start(
            out=wchunk[:, (ch % 2) * DM:(ch % 2 + 1) * DM],
            in_=dw1sl[ch * 128:(ch + 1) * 128, :]),
          writes=[bw], dma=True)
        E("scalar", lambda g, ch=ch: g.activation(out=scrB[:], in_=wchunk[:, (ch % 2) * DM:(ch % 2 + 1) * DM],
                                                  func=ACT.Identity, scale=sc_bc[:, 0:1]),
          reads=[bw, B_scbc], writes=[B_scrB])
        E("vector", lambda g: g.tensor_scalar(wq_t[:, 0:DM], scrB[:], RND, RND, AL.add, AL.subtract),
          reads=[B_scrB], writes=[B_wqt])
        for dc in range(DC):
            E("tensor", lambda g, dc=dc: g.transpose(pt_ab[:, (dc % 8) * 128:(dc % 8 + 1) * 128],
                                                     wq_t[:, dc * 128:(dc + 1) * 128], identb[:]),
              reads=[B_wqt, B_identb], writes=[B_pab])
            E("scalar", lambda g, dc=dc: g.activation(out=zq_nt[:, dc * 128:(dc + 1) * 128],
                                                      in_=pt_ab[:, (dc % 8) * 128:(dc % 8 + 1) * 128],
                                                      func=ACT.Copy),
              reads=[B_pab], writes=[B_zqnt])
        E("gpsimd", lambda g, ch=ch: g.dma_start(
            out=ag1_in.ap()[ch],
            in_=zq_nt[:, 0:DC * 128].rearrange("p (d c) -> p d c", d=DC)),
          reads=[B_zqnt], writes=[B_ag1], dma=True)
    E("gpsimd", lambda g: g.collective_compute(
        "AllGather", AL.bypass, replica_groups=[list(range(8))],
        ins=[ag1_in.ap().opt()], outs=[dw1qT.ap().opt()]),
      reads=[B_ag1], writes=[B_dw1qT[a] for a in range(HC)])
    for q in range(4):
        bw = B_wch[q % 2]
        E("sync", lambda g, q=q: g.dma_start(
            out=wchunk[:, (q % 2) * DM:(q % 2 + 1) * DM],
            in_=dw2sl[:, q * DM:(q + 1) * DM]),
          writes=[bw], dma=True)
        E("scalar", lambda g, q=q: g.activation(
            out=scrB[:], in_=wchunk[:, (q % 2) * DM:(q % 2 + 1) * DM],
            func=ACT.Identity, scale=sc_bc[:, 1:2]),
          reads=[bw, B_scbc], writes=[B_scrB])
        E("vector", lambda g: g.tensor_scalar(wq_t[:, 0:DM], scrB[:], RND, RND, AL.add, AL.subtract),
          reads=[B_scrB], writes=[B_wqt])
        for k in range(8):
            E("tensor", lambda g, k=k: g.transpose(pt_ab[:, k * 128:(k + 1) * 128],
                                                   wq_t[:, k * 128:(k + 1) * 128], identb[:]),
              reads=[B_wqt, B_identb], writes=[B_pab])
            E("scalar", lambda g, k=k: g.activation(out=zq_nt[:, k * 128:(k + 1) * 128],
                                                    in_=pt_ab[:, k * 128:(k + 1) * 128],
                                                    func=ACT.Copy),
              reads=[B_pab], writes=[B_zqnt])
        E("gpsimd", lambda g, q=q: g.dma_start(
            out=ag2_in.ap()[0, q * 8:(q + 1) * 8].rearrange("k p c -> p k c"),
            in_=zq_nt[:, 0:8 * 128].rearrange("p (k c) -> p k c", k=8)),
          reads=[B_zqnt], writes=[B_ag2], dma=True)
    E("gpsimd", lambda g: g.collective_compute(
        "AllGather", AL.bypass, replica_groups=[list(range(8))],
        ins=[ag2_in.ap().opt()], outs=[dw2qT.ap().opt()]),
      reads=[B_ag2], writes=[B_dw2qT[a] for a in range(DC)])

    P.barrier()

    def emit_A_proj(tc, E_):
            for tib in range(4):
                for tis in range(2):
                    ti = tc * TPC + tib * 2 + tis
                    bx = B_xt[tis % 2]
                    bz = B_z0[tis % 2]
                    E_("sync", lambda g, ti=ti, tis=tis: g.dma_start(
                        out=xt_b[:, (tis % 2) * DM:(tis % 2 + 1) * DM],
                        in_=dx[ti * 128:(ti + 1) * 128, :]),
                      writes=[bx], dma=True)
                    E_("scalar", lambda g, tis=tis: g.activation(
                        out=scrA[:], in_=xt_b[:, (tis % 2) * DM:(tis % 2 + 1) * DM],
                        func=ACT.Copy, accum_out=small[:, 0:1]),
                      reads=[bx], writes=[B_scrA, B_small])
                    E_("scalar", lambda g, tis=tis: g.activation(
                        out=scrA[:], in_=xt_b[:, (tis % 2) * DM:(tis % 2 + 1) * DM],
                        func=ACT.Square, accum_out=small[:, 1:2]),
                      reads=[bx], writes=[B_scrA, B_small])
                    E_("vector", lambda g: g.tensor_scalar_mul(small[:, 2:3], small[:, 0:1], -1.0 / DM),
                      reads=[B_small], writes=[B_small])
                    E_("scalar", lambda g: g.activation(out=small[:, 3:4], in_=small[:, 2:3],
                                                       func=ACT.Square),
                      reads=[B_small], writes=[B_small])
                    E_("vector", lambda g: g.tensor_scalar(small[:, 4:5], small[:, 1:2], 1.0 / DM, EPS,
                                                          AL.mult, AL.add),
                      reads=[B_small], writes=[B_small])
                    E_("vector", lambda g: g.tensor_sub(small[:, 5:6], small[:, 4:5], small[:, 3:4]),
                      reads=[B_small], writes=[B_small])
                    E_("scalar", lambda g: g.activation(out=small[:, 6:7], in_=small[:, 5:6], func=ACT.Ln),
                      reads=[B_small], writes=[B_small])
                    E_("scalar", lambda g: g.activation(out=small[:, 7:8], in_=small[:, 6:7], func=ACT.Exp,
                                                       scale=-0.5),
                      reads=[B_small], writes=[B_small])
                    E_("vector", lambda g, tis=tis: g.tensor_scalar(
                        z0_b[:, tis * DM:(tis + 1) * DM], xt_b[:, (tis % 2) * DM:(tis % 2 + 1) * DM],
                        small[:, 2:3], small[:, 7:8], AL.add, AL.mult),
                      reads=[bx, B_small], writes=[bz])
                    if DEBUG and ti == 0 and tis == 0:
                        E_("gpsimd", lambda g: g.dma_start(out=dbg["d_z0"], in_=z0_b[:, 0:DM]),
                          reads=[bz], dma=True)
                        E_("gpsimd", lambda g: g.dma_start(out=dbg["d_small"], in_=small[:]),
                          reads=[B_small], dma=True)
                for dc in range(DC):
                    for tis in range(2):
                        E_("tensor", lambda g, dc=dc, tis=tis: g.transpose(
                            pt_b[:, tis * 128:(tis + 1) * 128],
                            z0_b[:, tis * DM + dc * 128:tis * DM + (dc + 1) * 128], identf[:]),
                          reads=[B_z0[tis], B_identf], writes=[B_pb])
                    E_("scalar", lambda g, dc=dc, tib=tib: g.activation(
                        out=z0Tb[:, dc * TC + tib * 256:dc * TC + (tib + 1) * 256],
                        in_=pt_b[:, 0:256], func=ACT.Copy),
                      reads=[B_pb], writes=[B_z0Tb[dc]])
            for ntb in range(2):
                for dc in range(DC):
                    E_("tensor", lambda g, ntb=ntb, dc=dc: g.matmul(
                        pt_b[0:96, 0:512], WxT[:, dc * 96:(dc + 1) * 96],
                        z0Tb[:, dc * TC + ntb * 512:dc * TC + (ntb + 1) * 512],
                        start=(dc == 0), stop=(dc == DC - 1)),
                      reads=[B_WxT, B_z0Tb[dc]], writes=[B_pb])
                E_("scalar", lambda g, ntb=ntb: g.activation(out=dtr_b[:, ntb * 512:(ntb + 1) * 512],
                                                            in_=pt_b[0:64, 0:512], func=ACT.Copy),
                  reads=[B_pb], writes=[B_dtr])
                E_("scalar", lambda g, ntb=ntb: g.activation(out=BCb[:, ntb * 512:(ntb + 1) * 512],
                                                            in_=pt_b[64:96, 0:512], func=ACT.Copy),
                  reads=[B_pb], writes=[B_BCb])


    emit_A_proj(0, E)
    for tc in range(NTC):
        for dc in range(DC):
            for ntb in range(2):
                pdt = (pt_c, pt_b)[ntb % 2]
                bpdt = (B_pc, B_pb)[ntb % 2]
                E("tensor", lambda g, dc=dc, ntb=ntb, pdt=pdt: g.matmul(
                    pdt[:, 0:512], dtWT[:, dc * 128:(dc + 1) * 128],
                    dtr_b[:, ntb * 512:(ntb + 1) * 512], start=True, stop=True),
                  reads=[B_dtWT, B_dtr], writes=[bpdt])
                p_ = ntb % 2
                bspt = B_spt[p_]
                bspa = (B_spa, B_spa2)[p_]
                bspe = (B_spe, B_spe2)[p_]
                bspl = (B_spl2, B_spl3)[p_]
                tspa = scrB[:, p_ * 512:(p_ + 1) * 512]
                tspe = z0_b[:, p_ * 512:(p_ + 1) * 512]
                tspl = z0_b[:, 1024 + p_ * 512:1024 + (p_ + 1) * 512]
                pdt = (pt_c, pt_b)[p_]
                bpdt = (B_pc, B_pb)[p_]
                E("scalar", lambda g, dc=dc, p_=p_, pdt=pdt: g.activation(
                    out=spt[:, p_ * 512:(p_ + 1) * 512], in_=pdt[:, 0:512],
                    func=ACT.Identity, bias=dtb_c[:, dc:dc + 1]),
                  reads=[bpdt, B_dtb], writes=[bspt])
                E("vector", lambda g, p_=p_, tspa=tspa: g.scalar_tensor_tensor(
                    tspa, spt[:, p_ * 512:(p_ + 1) * 512], -1.0,
                    spt[:, p_ * 512:(p_ + 1) * 512], AL.mult, AL.max),
                  reads=[bspt], writes=[bspa])
                E("scalar", lambda g, tspa=tspa, tspe=tspe: g.activation(
                    out=tspe, in_=tspa, func=ACT.Exp, scale=-1.0),
                  reads=[bspa], writes=[bspe])
                E("scalar", lambda g, tspe=tspe, tspl=tspl: g.activation(
                    out=tspl, in_=tspe, func=ACT.Ln, bias=1.0),
                  reads=[bspe], writes=[bspl])
                E("vector", lambda g, dc=dc, ntb=ntb, p_=p_, tspl=tspl: g.scalar_tensor_tensor(
                    dt_t[:, dc * TC + ntb * 512:dc * TC + (ntb + 1) * 512],
                    spt[:, p_ * 512:(p_ + 1) * 512], 0.0, tspl, AL.max, AL.add),
                  reads=[bspt, bspl], writes=[B_dt[dc]])
            E("vector", lambda g, dc=dc: g.tensor_mul(dtx_t[:, dc * TC:(dc + 1) * TC],
                                                      dt_t[:, dc * TC:(dc + 1) * TC],
                                                      z0Tb[:, dc * TC:(dc + 1) * TC]),
              reads=[B_dt[dc], B_z0Tb[dc]], writes=[B_dtx[dc]])
        for s in range(S):
            for ntb in range(2):
                pbc = (pt_c, pt_d)[ntb]
                bbc = (B_pc, B_pd)[ntb]
                E("tensor", lambda g, s=s, ntb=ntb, pbc=pbc: g.matmul(
                    pbc[:, 0:512], oh32[:, s * 128:(s + 1) * 128],
                    BCb[:, ntb * 512:(ntb + 1) * 512], start=True, stop=True),
                  reads=[B_ones, B_BCb], writes=[bbc])
                E("scalar", lambda g, s=s, ntb=ntb, pbc=pbc: g.activation(
                    out=Bb[:, s * TC + ntb * 512:s * TC + (ntb + 1) * 512],
                    in_=pbc[:, 0:512], func=ACT.Copy),
                  reads=[bbc], writes=[B_Bb[s]])
            for ntb in range(2):
                pbc = (pt_c, pt_d)[ntb]
                bbc = (B_pc, B_pd)[ntb]
                E("tensor", lambda g, s=s, ntb=ntb, pbc=pbc: g.matmul(
                    pbc[:, 0:512], oh32[:, (16 + s) * 128:(17 + s) * 128],
                    BCb[:, ntb * 512:(ntb + 1) * 512], start=True, stop=True),
                  reads=[B_ones, B_BCb], writes=[bbc])
                E("scalar", lambda g, s=s, ntb=ntb, pbc=pbc: g.activation(
                    out=Cb[:, s * TC + ntb * 512:s * TC + (ntb + 1) * 512],
                    in_=pbc[:, 0:512], func=ACT.Copy),
                  reads=[bbc], writes=[B_Cb[s]])
        thunks = []
        if tc + 1 < NTC:
            emit_A_proj(tc + 1, lambda *a, **k: thunks.append((a, k)))
        tpos = [0]

        def drip(n):
            for _ in range(n):
                if tpos[0] < len(thunks):
                    a_, k_ = thunks[tpos[0]]
                    E(*a_, **k_)
                    tpos[0] += 1

        for dc in range(DC):
            E("vector", lambda g, dc=dc: g.tensor_scalar(
                yb[:, dc * TC:(dc + 1) * TC], z0Tb[:, dc * TC:(dc + 1) * TC],
                Dcol_c[:, dc:dc + 1], None, AL.mult),
              reads=[B_z0Tb[dc], B_Dcol], writes=[B_yb[dc]])
        for dc in range(DC):
            yps = y_ps
            byps = B_yps
            for s in range(S):
                k = s % 2
                st_idx = dc * S + s
                E("scalar", lambda g, dc=dc, s=s, k=k: g.activation(
                    out=dA_b[:, k * TC:(k + 1) * TC], in_=dt_t[:, dc * TC:(dc + 1) * TC],
                    func=ACT.Exp, scale=A_neg[:, dc * S + s:dc * S + s + 1]),
                  reads=[B_dt[dc], B_Aneg], writes=[B_dA[k]])
                E("vector", lambda g, dc=dc, s=s, k=k: g.tensor_mul(
                    u_b[:, k * TC:(k + 1) * TC], dtx_t[:, dc * TC:(dc + 1) * TC],
                    Bb[:, s * TC:(s + 1) * TC]),
                  reads=[B_dtx[dc], B_Bb[s]], writes=[B_u[k]])
                E("vector", lambda g, k=k, st_idx=st_idx: g.tensor_tensor_scan(
                    h_b[:, k * TC:(k + 1) * TC], dA_b[:, k * TC:(k + 1) * TC],
                    u_b[:, k * TC:(k + 1) * TC], state[:, st_idx:st_idx + 1],
                    AL.mult, AL.add),
                  reads=[B_dA[k], B_u[k], B_state[st_idx]], writes=[B_h[k]])
                E("vector", lambda g, k=k, st_idx=st_idx: g.tensor_copy(
                    state[:, st_idx:st_idx + 1], h_b[:, (k + 1) * TC - 1:(k + 1) * TC]),
                  reads=[B_h[k]], writes=[B_state[st_idx]])
                E("vector", lambda g, k=k, s=s: g.tensor_mul(
                    hc_b[:, k * TC:(k + 1) * TC], h_b[:, k * TC:(k + 1) * TC],
                    Cb[:, s * TC:(s + 1) * TC]),
                  reads=[B_h[k], B_Cb[s]], writes=[B_hc[k]])
                for ntb in range(2):
                    E("tensor", lambda g, k=k, ntb=ntb, s=s, yps=yps: g.matmul(
                        yps[:, ntb * 512:(ntb + 1) * 512], identb[:],
                        hc_b[:, k * TC + ntb * 512:k * TC + (ntb + 1) * 512],
                        start=(s == 0), stop=(s == S - 1)),
                      reads=[B_identb, B_hc[k]], writes=[byps])
            drip(2 + len(thunks) // DC)
            for ntb in range(2):
                E("vector", lambda g, dc=dc, ntb=ntb, yps=yps: g.tensor_add(
                    yb[:, dc * TC + ntb * 512:dc * TC + (ntb + 1) * 512],
                    yps[:, ntb * 512:(ntb + 1) * 512],
                    yb[:, dc * TC + ntb * 512:dc * TC + (ntb + 1) * 512]),
                  reads=[byps, B_yb[dc]], writes=[B_yb[dc]])
        drip(len(thunks))
        if DEBUG and tc == 0:
            E("gpsimd", lambda g: g.dma_start(out=dbg["d_dt"], in_=dt_t[:]),
              reads=[B_dt[7]], deps=[P.items["vector"][-1]], dma=True)
            E("gpsimd", lambda g: g.dma_start(out=dbg["d_Bb"], in_=Bb[:]),
              reads=[B_Bb[15]], dma=True)
            E("gpsimd", lambda g: g.dma_start(out=dbg["d_yb"], in_=yb[:]),
              reads=[B_yb[7]], deps=[P.items["scalar"][-1]], dma=True)
        for tloc in range(TPC):
            ti = tc * TPC + tloc
            bx = B_xt[tloc % 2]
            E("sync", lambda g, ti=ti, tloc=tloc: g.dma_start(
                out=xt_b[:, (tloc % 2) * DM:(tloc % 2 + 1) * DM],
                in_=dx[ti * 128:(ti + 1) * 128, :]),
              writes=[bx], dma=True)
            for dc in range(DC):
                pp = (pt_ab, pt_bb)[dc // 4]
                bpp = (B_pab, B_pbb)[dc // 4]
                E("tensor", lambda g, dc=dc, tloc=tloc, pp=pp: g.transpose(
                    pp[:, (dc % 4) * 128:(dc % 4 + 1) * 128],
                    yb[:, dc * TC + tloc * 128:dc * TC + (tloc + 1) * 128], identb[:]),
                  reads=[B_yb[dc], B_identb], writes=[bpp])
            E("vector", lambda g, tloc=tloc: g.tensor_add(
                x1t[:, 0:512], pt_ab[:, 0:512], xt_b[:, (tloc % 2) * DM:(tloc % 2) * DM + 512]),
              reads=[B_pab, bx], writes=[B_x1t])
            E("vector", lambda g, tloc=tloc: g.tensor_add(
                x1t[:, 512:1024], pt_bb[:, 0:512],
                xt_b[:, (tloc % 2) * DM + 512:(tloc % 2 + 1) * DM]),
              reads=[B_pbb, bx], writes=[B_x1t])
            E("scalar", lambda g: g.activation(out=scrA[:], in_=x1t[:], func=ACT.Copy,
                                               accum_out=small[:, 0:1]),
              reads=[B_x1t], writes=[B_scrA, B_small])
            E("scalar", lambda g: g.activation(out=scrA[:], in_=x1t[:], func=ACT.Square,
                                               accum_out=small[:, 1:2]),
              reads=[B_x1t], writes=[B_scrA, B_small])
            E("vector", lambda g, ti=ti: g.tensor_scalar_mul(negm2_a[:, ti:ti + 1], small[:, 0:1],
                                                             -1.0 / DM),
              reads=[B_small], writes=[B_negm2])
            E("scalar", lambda g, ti=ti: g.activation(out=small[:, 3:4], in_=negm2_a[:, ti:ti + 1],
                                                      func=ACT.Square),
              reads=[B_negm2], writes=[B_small])
            E("vector", lambda g: g.tensor_scalar(small[:, 4:5], small[:, 1:2], 1.0 / DM, EPS,
                                                  AL.mult, AL.add),
              reads=[B_small], writes=[B_small])
            E("vector", lambda g: g.tensor_sub(small[:, 5:6], small[:, 4:5], small[:, 3:4]),
              reads=[B_small], writes=[B_small])
            E("scalar", lambda g: g.activation(out=small[:, 6:7], in_=small[:, 5:6], func=ACT.Ln),
              reads=[B_small], writes=[B_small])
            E("scalar", lambda g, ti=ti: g.activation(out=r2_a[:, ti:ti + 1], in_=small[:, 6:7],
                                                      func=ACT.Exp, scale=-0.5),
              reads=[B_small], writes=[B_r2])
            E("scalar", lambda g, ti=ti: g.activation(out=scrA[:], in_=x1t[:], func=ACT.Identity,
                                                      bias=negm2_a[:, ti:ti + 1]),
              reads=[B_x1t, B_negm2], writes=[B_scrA])
            E("vector", lambda g: g.reduce_max(small[:, 3:4], scrA[:], axis=AX.X,
                                               apply_absolute_value=True),
              reads=[B_scrA], writes=[B_small])
            E("vector", lambda g, ti=ti: g.tensor_mul(small[:, 4:5], small[:, 3:4], r2_a[:, ti:ti + 1]),
              reads=[B_small, B_r2], writes=[B_small])
            ppz = ti % 2
            E("vector", lambda g, ppz=ppz: g.tensor_max(maxz_r[:, 1 - ppz:2 - ppz],
                                                        maxz_r[:, ppz:ppz + 1], small[:, 4:5]),
              reads=[B_small, B_maxz], writes=[B_maxz])
            E("gpsimd", lambda g, ti=ti: g.dma_start(out=dx1.ap()[ti * 128:(ti + 1) * 128, :],
                                                     in_=x1t[:]),
              reads=[B_x1t], writes=[B_dx1[ti]], dma=True)

    finalize_max(maxz_r[:, (NTI % 2):(NTI % 2) + 1], B_maxz, 6)
    E("vector", lambda g: g.tensor_copy(cc_sb[:], scal[0:1, 6:7].broadcast_to([1, 8])),
      reads=[B_scal], writes=[B_ccsb])
    E("gpsimd", lambda g: g.dma_start(out=cc1_in.ap(), in_=cc_sb[:]),
      reads=[B_ccsb], writes=[B_cc1i], dma=True)
    E("gpsimd", lambda g: g.collective_compute(
        "AllReduce", AL.max, replica_groups=[list(range(8))],
        ins=[cc1_in.ap().opt()], outs=[cc1_out.ap().opt()]),
      reads=[B_cc1i], writes=[B_cc1o])
    E("sync", lambda g: g.dma_start(out=cc_sb[:], in_=cc1_out.ap()),
      reads=[B_cc1o], writes=[B_ccsb], dma=True)
    E("vector", lambda g: g.tensor_copy(scal[0:1, 7:8], cc_sb[0:1, 0:1]),
      reads=[B_ccsb], writes=[B_scal])
    make_scale(7, 8, 9)
    bcast_scalar(9, 2)
    E("vector", lambda g: g.tensor_mul(scal[0:1, 10:11], scal[0:1, 8:9], scal[0:1, 2:3]),
      reads=[B_scal], writes=[B_scal])
    bcast_scalar(10, 3)
    E("vector", lambda g: g.tensor_scalar(b1s_c[:], b1_c[:], sc_bc[:, 3:4], None, AL.mult),
      reads=[B_b1, B_scbc], writes=[B_b1s])
    E("vector", lambda g: g.tensor_scalar(rq_a[:], r2_a[:], sc_bc[:, 2:3], None, AL.mult),
      reads=[B_r2, B_scbc], writes=[B_rq])
    E("vector", lambda g: g.tensor_mul(bq_a[:], negm2_a[:], rq_a[:]),
      reads=[B_negm2, B_rq], writes=[B_bq])

    P.barrier()

    E("scalar", lambda g: g.preload_act_table(ACT.Gelu))

    B_zqx = [nb("zq0"), nb("zq1")]

    wqs = [dt_t, dt_t]
    wqo = [0, 2048]
    B_wq2 = [nb("wq0"), nb("wq1")]

    def pc_quant_prep(nt, tl):
        ti = nt * 4 + tl
        bx = B_xt[tl % 2]
        E("sync", lambda g, ti=ti, tl=tl: g.dma_start(
            out=xt_b[:, (tl % 2) * DM:(tl % 2 + 1) * DM],
            in_=dx1.ap()[ti * 128:(ti + 1) * 128, :]),
          reads=[B_dx1[ti]], writes=[bx], dma=True, stream="act")
        bsc = (B_scrA, B_scrB)[tl % 2]
        tsc = (scrA, scrB)[tl % 2]
        E("scalar", lambda g, ti=ti, tl=tl, tsc=tsc: g.activation(
            out=tsc[:], in_=xt_b[:, (tl % 2) * DM:(tl % 2 + 1) * DM],
            func=ACT.Identity, scale=rq_a[:, ti:ti + 1], bias=bq_a[:, ti:ti + 1]),
          reads=[bx, B_rq, B_bq], writes=[bsc])
        o = wqo[tl % 2]
        E("vector", lambda g, tsc=tsc, o=o: g.tensor_scalar(dt_t[:, o:o + DM], tsc[:], RND, RND,
                                                            AL.add, AL.subtract),
          reads=[bsc], writes=[B_wq2[tl % 2]])

    def pc_quant_emit(nt, tl):
        zb = nt % 2
        bzq = B_zqx[zb]
        o = wqo[tl % 2]
        for dcb in range(2):
            hb = dcb % 2
            bpab = (B_pab, B_pbb)[hb]
            tpab = (pt_ab, pt_bb)[hb]
            for dcs in range(4):
                dc = dcb * 4 + dcs
                E("tensor", lambda g, dc=dc, dcs=dcs, tpab=tpab, o=o: g.transpose(
                    tpab[:, dcs * 128:(dcs + 1) * 128], dt_t[:, o + dc * 128:o + (dc + 1) * 128],
                    identb[:]),
                  reads=[B_wq2[tl % 2], B_identb], writes=[bpab])
            E("vector", lambda g, dcb=dcb, tl=tl, zb=zb, tpab=tpab: g.tensor_copy(
                zq_nt[:, zb * 4096:(zb + 1) * 4096].rearrange("p (d t) -> p d t", d=DC)[
                    :, dcb * 4:(dcb + 1) * 4, tl * 128:(tl + 1) * 128],
                tpab[:, 0:512].rearrange("p (d t) -> p d t", d=4)),
              reads=[bpab], writes=[bzq])

    def pc_quant_tl(nt, tl):
        pc_quant_prep(nt, tl)
        pc_quant_emit(nt, tl)

    W1PF = 4
    for tl in range(4):
        pc_quant_tl(0, tl)
    if DEBUG:
        E("gpsimd", lambda g: g.dma_start(out=dbg["d_zq"], in_=zq_nt[:, 0:DC * 512]),
          reads=[B_zqx[0]], dma=True)
    mm_banks = [(pt_a, B_pa), (pt_b, B_pb), (pt_c, B_pc), (pt_d, B_pd)]
    for nt in range(8):
        zb = nt % 2
        for pf in range(W1PF - 1):
            E("sync", lambda g, pf=pf: g.dma_start(
                out=w1s[:, pf * DC * 128:(pf + 1) * DC * 128],
                in_=dw1qT.ap()[pf].rearrange("p d c -> p (d c)")),
              reads=[B_dw1qT[pf]], writes=[B_w1s[pf]], dma=True)
        for hcc in range(HC):
            k = hcc % W1PF
            pp, bpp = mm_banks[hcc % 4]
            for dc in range(DC):
                E("tensor", lambda g, hcc=hcc, dc=dc, k=k, pp=pp, zb=zb: g.matmul(
                    pp[:, 0:512], w1s[:, k * DC * 128 + dc * 128:k * DC * 128 + (dc + 1) * 128],
                    zq_nt[:, zb * 4096 + dc * 512:zb * 4096 + (dc + 1) * 512],
                    start=(dc == 0), stop=(dc == DC - 1)),
                  reads=[B_w1s[k], B_zqx[zb]], writes=[bpp])
            if hcc + W1PF - 1 < HC:
                kpf = (hcc + W1PF - 1) % W1PF
                E("sync", lambda g, hcc=hcc, kpf=kpf: g.dma_start(
                    out=w1s[:, kpf * DC * 128:(kpf + 1) * DC * 128],
                    in_=dw1qT.ap()[hcc + W1PF - 1].rearrange("p d c -> p (d c)")),
                  reads=[B_dw1qT[hcc + W1PF - 1]], writes=[B_w1s[kpf]], dma=True, stream="wgt")
            bh = B_hsb[hcc % 2]
            E("scalar", lambda g, hcc=hcc, pp=pp: g.activation(
                out=hsb[:, 1024 + (hcc % 2) * 512:1024 + (hcc % 2 + 1) * 512], in_=pp[:, 0:512],
                func=ACT.Gelu, scale=sc_bc[:, 3:4], bias=b1s_c[:, hcc:hcc + 1]),
              reads=[bpp, B_scbc, B_b1s], writes=[bh])
            E("gpsimd", lambda g, hcc=hcc, nt=nt: g.dma_start(
                out=dh.ap()[nt, hcc], in_=hsb[:, 1024 + (hcc % 2) * 512:1024 + (hcc % 2 + 1) * 512]),
              reads=[bh], writes=[B_dh[nt][hcc]], dma=True, stream="hst")
            E("vector", lambda g, hcc=hcc: g.reduce_max(
                small[:, 7:8], hsb[:, 1024 + (hcc % 2) * 512:1024 + (hcc % 2 + 1) * 512], axis=AX.X,
                apply_absolute_value=True),
              reads=[bh], writes=[B_small])
            pph = (nt * HC + hcc) % 2
            E("vector", lambda g, pph=pph: g.tensor_max(maxh_r[:, 1 - pph:2 - pph],
                                                        maxh_r[:, pph:pph + 1], small[:, 7:8]),
              reads=[B_small, B_maxh], writes=[B_maxh])
            if nt + 1 < 8 and hcc % 8 == 0:
                pc_quant_prep(nt + 1, hcc // 8)
            if nt + 1 < 8 and hcc % 8 == 3:
                pc_quant_emit(nt + 1, hcc // 8)

    finalize_max(maxh_r[:, 0:1], B_maxh, 11)
    E("vector", lambda g: g.tensor_copy(cc_sb[:], scal[0:1, 11:12].broadcast_to([1, 8])),
      reads=[B_scal], writes=[B_ccsb])
    E("gpsimd", lambda g: g.dma_start(out=cc2_in.ap(), in_=cc_sb[:]),
      reads=[B_ccsb], writes=[B_cc2i], dma=True)
    E("gpsimd", lambda g: g.collective_compute(
        "AllReduce", AL.max, replica_groups=[list(range(8))],
        ins=[cc2_in.ap().opt()], outs=[cc2_out.ap().opt()]),
      reads=[B_cc2i], writes=[B_cc2o])
    E("sync", lambda g: g.dma_start(out=cc_sb[:], in_=cc2_out.ap()),
      reads=[B_cc2o], writes=[B_ccsb], dma=True)
    E("vector", lambda g: g.tensor_copy(scal[0:1, 12:13], cc_sb[0:1, 0:1]),
      reads=[B_ccsb], writes=[B_scal])
    E("scalar", _load_combined_table)
    make_scale(12, 13, 14)
    bcast_scalar(14, 4)
    E("vector", lambda g: g.tensor_mul(scal[0:1, 15:16], scal[0:1, 13:14], scal[0:1, 3:4]),
      reads=[B_scal], writes=[B_scal])
    bcast_scalar(15, 5)
    E("vector", lambda g: g.tensor_scalar(b2s_c[:], b2_c[:], sc_bc[:, 5:6], None, AL.mult),
      reads=[B_b2, B_scbc], writes=[B_b2s])
    if DEBUG:
        E("vector", lambda g: g.tensor_copy(scrB[0:1, 0:16], scal[0:1, 0:16]),
          reads=[B_scal], writes=[B_scrB])
        E("gpsimd", lambda g: g.dma_start(out=dbg["d_scal"], in_=scrB[0:1, 0:16]),
          reads=[B_scrB], dma=True)

    P.barrier()

    B_hraw = nb("hraw")

    def pd_readback(nt):
        E("sync", lambda g, nt=nt: g.dma_start(
            out=yb[:, 0:8192].rearrange("p (k c) -> p k c", k=16),
            in_=dh.ap()[nt, 0:16].rearrange("k p c -> p k c")),
          reads=[B_dh[nt][k_] for k_ in range(16)], writes=[B_hraw], dma=True, stream="hrw")
        E("sync", lambda g, nt=nt: g.dma_start(
            out=dtx_t[:, 0:8192].rearrange("p (k c) -> p k c", k=16),
            in_=dh.ap()[nt, 16:32].rearrange("k p c -> p k c")),
          reads=[B_dh[nt][k_] for k_ in range(16, 32)], writes=[B_hraw], dma=True, stream="hrw")

    def pd_quant_chunk(nt, chunk):
        kb = nt % 2
        hqt = (hq_nt_a, hq_nt_b)[kb]
        for kc in range(chunk * 4, chunk * 4 + 4):
            src = yb if kc < 16 else dtx_t
            off = (kc if kc < 16 else kc - 16) * 512
            bsc = (B_scrA, B_scrB)[kc % 2]
            tsc = (scrA, scrB)[kc % 2]
            E("scalar", lambda g, src=src, off=off, tsc=tsc: g.activation(
                out=tsc[:, 0:512], in_=src[:, off:off + 512],
                func=ACT.Identity, scale=sc_bc[:, 4:5]),
              reads=[B_hraw, B_scbc], writes=[bsc])
            E("vector", lambda g, kc=kc, hqt=hqt, tsc=tsc: g.tensor_scalar(
                hqt[:, kc * 512:(kc + 1) * 512],
                tsc[:, 0:512], RND, RND, AL.add, AL.subtract),
              reads=[bsc], writes=[B_hqnt[kb][kc // 4]])

    def pd_quant(nt):
        for chunk in range(8):
            pd_quant_chunk(nt, chunk)

    pd_readback(0)
    pd_quant(0)
    for nt in range(8):
        kb = nt % 2
        if nt + 1 < 8:
            pd_readback(nt + 1)
        E("sync", lambda g: g.dma_start(
            out=w2s[:, 0:HC * 128].rearrange("p (k c) -> p k c", k=HC),
            in_=dw2qT.ap()[0].rearrange("k p c -> p k c")),
          reads=[B_dw2qT[0]], writes=[B_w2s[0]], dma=True, stream="wgt")
        for dout in range(DC):
            k2 = dout % 2
            pp = (pt_c, pt_d)[dout % 2]
            bpp = (B_pc, B_pd)[dout % 2]
            hqt = (hq_nt_a, hq_nt_b)[kb]
            for kc in range(HC):
                E("tensor", lambda g, kc=kc, k2=k2, hqt=hqt, pp=pp: g.matmul(
                    pp[:, 0:512], w2s[:, k2 * HC * 128 + kc * 128:k2 * HC * 128 + (kc + 1) * 128],
                    hqt[:, kc * 512:(kc + 1) * 512],
                    start=(kc == 0), stop=(kc == HC - 1)),
                  reads=[B_w2s[k2], B_hqnt[kb][kc // 4]], writes=[bpp])
            if dout + 1 < DC:
                E("sync", lambda g, dout=dout, k2=k2: g.dma_start(
                    out=w2s[:, (1 - k2) * HC * 128:(2 - k2) * HC * 128].rearrange("p (k c) -> p k c", k=HC),
                    in_=dw2qT.ap()[dout + 1].rearrange("k p c -> p k c")),
                  reads=[B_dw2qT[dout + 1]], writes=[B_w2s[1 - k2]], dma=True, stream="wgt")
            bx2 = B_x2f[dout % 2]
            E("scalar", lambda g, dout=dout, pp=pp: g.activation(
                out=x2f[:, (dout % 2) * 512:(dout % 2) * 512 + 512], in_=pp[:, 0:512],
                func=ACT.Identity, scale=sc_bc[:, 5:6], bias=b2s_c[:, dout:dout + 1]),
              reads=[bpp, B_scbc, B_b2s], writes=[bx2])
            ppt = (pt_a, pt_b)[dout % 2]
            bppt = (B_pa, B_pb)[dout % 2]
            for sub in range(4):
                E("tensor", lambda g, sub=sub, dout=dout, ppt=ppt: g.transpose(
                    ppt[:, sub * 128:(sub + 1) * 128],
                    x2f[:, (dout % 2) * 512 + sub * 128:(dout % 2) * 512 + (sub + 1) * 128],
                    identf[:]),
                  reads=[bx2, B_identf], writes=[bppt])
            bq1 = B_x1q[dout % 2]
            E("sync", lambda g, nt=nt, dout=dout: g.dma_start(
                out=x1q[:, 1024 + (dout % 2) * 512:1024 + (dout % 2) * 512 + 512].rearrange("p (s c) -> p s c", s=4),
                in_=dx1.ap().rearrange("(n s p) d -> n p s d", s=4, p=128)[nt, :, :,
                                                                          dout * 128:(dout + 1) * 128]),
              reads=[B_dx1[nt * 4], B_dx1[nt * 4 + 1], B_dx1[nt * 4 + 2], B_dx1[nt * 4 + 3]],
              writes=[bq1], dma=True, stream="act")
            bo = B_outt[dout % 2]
            E("vector", lambda g, dout=dout, ppt=ppt: g.tensor_add(
                outt[:, (dout % 2) * 512:(dout % 2) * 512 + 512], ppt[:, 0:512],
                x1q[:, 1024 + (dout % 2) * 512:1024 + (dout % 2) * 512 + 512]),
              reads=[bppt, bq1], writes=[bo])
            E("gpsimd", lambda g, nt=nt, dout=dout: g.dma_start(
                out=dout_view[nt, :, :, dout * 128:(dout + 1) * 128],
                in_=outt[:, (dout % 2) * 512:(dout % 2) * 512 + 512].rearrange("p (s c) -> p s c", s=4)),
              reads=[bo], dma=True, stream="out")
            if nt + 1 < 8:
                pd_quant_chunk(nt + 1, dout)

    if DEBUG:
        for ti in range(NTI):
            E("sync", lambda g, ti=ti: g.dma_start(out=xt_b[:, 0:DM],
                                                   in_=dx1.ap()[ti * 128:(ti + 1) * 128, :]),
              reads=[B_dx1[ti]], writes=[B_xt[0]], dma=True)
            E("gpsimd", lambda g, ti=ti: g.dma_start(out=dbg["d_x1"][ti * 128:(ti + 1) * 128, :],
                                                     in_=xt_b[:, 0:DM]),
              reads=[B_xt[0]], dma=True)
            E("sync", lambda g, ti=ti: g.dma_start(out=xt_b[:, DM:2 * DM],
                                                   in_=dxa.ap()[ti * 128:(ti + 1) * 128, :]),
              reads=[B_dxa[ti]], writes=[B_xt[1]], dma=True)
            E("gpsimd", lambda g, ti=ti: g.dma_start(out=dbg["d_xa"][ti * 128:(ti + 1) * 128, :],
                                                     in_=xt_b[:, DM:2 * DM]),
              reads=[B_xt[1]], dma=True)

    P.barrier()

    P.finalize()

    with es, nc.Block() as block:
        @block.sync
        def _(eng):
            P.replay("sync", eng, esem, dsem, ssem)
            for e in ENGINES:
                n = sum(1 for it in P.items[e] if it.is_dma and it.stream is None)
                if n:
                    eng.wait_ge(dsem[e], 16 * n)
            for k in Prog.STREAMS:
                n = sum(1 for e in ENGINES for it in P.items[e]
                        if it.is_dma and it.stream == k)
                if n:
                    eng.wait_ge(ssem[k], 16 * n)

        @block.scalar
        def _(eng):
            P.replay("scalar", eng, esem, dsem, ssem)

        @block.vector
        def _(eng):
            P.replay("vector", eng, esem, dsem, ssem)

        @block.tensor
        def _(eng):
            P.replay("tensor", eng, esem, dsem, ssem)

        @block.gpsimd
        def _(eng):
            P.replay("gpsimd", eng, esem, dsem, ssem)
            n = sum(1 for it in P.items["gpsimd"] if it.is_dma and it.stream is None)
            eng.wait_ge(dsem["gpsimd"], 16 * n)

    return nc, P


def kernel(L0_attn_x, L0_attn_A_log, L0_attn_D, L0_attn_Wx, L0_attn_dt_W,
           L0_attn_dt_b, L0_ffn_w1, L0_ffn_b1, L0_ffn_w2, L0_ffn_b2):
    bf = ml_dtypes.bfloat16
    x = np.ascontiguousarray(np.asarray(L0_attn_x, dtype=np.float32))
    ident_f = np.eye(128, dtype=np.float32)
    ident_b = np.eye(128).astype(bf)
    ones_b = np.ones((1, 128), dtype=np.float32).astype(bf)
    ones_f = np.ones((1, 128), dtype=np.float32)
    onehot32 = np.zeros((32, 32 * 128), dtype=np.float32)
    for s_ in range(32):
        onehot32[s_, s_ * 128:(s_ + 1) * 128] = 1.0
    onehot32 = onehot32.astype(bf)

    nc, P = build_nc()
    common = dict(
        A_log=np.ascontiguousarray(np.asarray(L0_attn_A_log, dtype=np.float32)),
        D_vec=np.ascontiguousarray(np.asarray(L0_attn_D, dtype=np.float32)),
        Wx=np.ascontiguousarray(np.asarray(L0_attn_Wx, dtype=np.float32)),
        dt_W=np.ascontiguousarray(np.asarray(L0_attn_dt_W, dtype=np.float32)),
        dt_b=np.ascontiguousarray(np.asarray(L0_attn_dt_b, dtype=np.float32)),
        w1=np.ascontiguousarray(np.asarray(L0_ffn_w1, dtype=np.float32)),
        b1=np.ascontiguousarray(np.asarray(L0_ffn_b1, dtype=np.float32)),
        w2=np.ascontiguousarray(np.asarray(L0_ffn_w2, dtype=np.float32)),
        b2=np.ascontiguousarray(np.asarray(L0_ffn_b2, dtype=np.float32)),
        ident_f=ident_f, ident_b=ident_b, ones_b=ones_b,
        ones_f=ones_f, onehot32=onehot32,
    )
    w1f = np.ascontiguousarray(np.asarray(L0_ffn_w1, dtype=np.float32))
    w2f = np.ascontiguousarray(np.asarray(L0_ffn_w2, dtype=np.float32))
    in_maps = [dict(x_in=np.ascontiguousarray(x[i]),
                    w1sl=np.ascontiguousarray(w1f[i * 512:(i + 1) * 512]),
                    w2sl=np.ascontiguousarray(w2f[i * 128:(i + 1) * 128]),
                    **common) for i in range(8)]
    res = run_bass_kernel_spmd(nc, in_maps, core_ids=list(range(8)))
    out = np.stack([np.asarray(res.results[i]["out"], dtype=np.float32) for i in range(8)], axis=0)
    if DEBUG:
        kernel._dbg = res
    return out



# version 18
# speedup vs baseline: 1.6158x; 1.0380x over previous
import numpy as np
import ml_dtypes
import concourse.bass as bass
import concourse.mybir as mybir
from concourse.bass_utils import run_bass_kernel_spmd
from contextlib import ExitStack

F32 = mybir.dt.float32
BF16 = mybir.dt.bfloat16
AL = mybir.AluOpType
ACT = mybir.ActivationFunctionType
AX = mybir.AxisListType

L = 4096
DM = 1024
DC = 8
S = 16
R = 64
H = 4096
HC = 32
TC = 1024
NTC = L // TC
NTI = L // 128
TPC = TC // 128
QP = 127.0
EPS = 1e-5
RND = float(1.5 * 2.0 ** 23)

DEBUG = False

ENGINES = ("sync", "scalar", "vector", "tensor", "gpsimd")


class Buf:
    __slots__ = ("writer", "readers", "name")

    def __init__(self, name=""):
        self.writer = None
        self.readers = []
        self.name = name


class Item:
    __slots__ = ("eng", "fn", "deps", "marked", "ord", "is_dma", "idx", "dmacnt", "nodrain",
                 "stream", "stream_ord")

    def __init__(self, eng, fn, deps, is_dma):
        self.eng = eng
        self.fn = fn
        self.deps = deps
        self.marked = False
        self.ord = None
        self.is_dma = is_dma
        self.idx = None
        self.dmacnt = None
        self.nodrain = False
        self.stream = None
        self.stream_ord = None


class Prog:
    def __init__(self):
        self.items = {e: [] for e in ENGINES}
        self.bar = {e: None for e in ENGINES}
        self.dmac = {e: 0 for e in ENGINES}

    STREAMS = ("wgt", "hrw", "act", "hst", "out")

    def emit(self, eng, fn, reads=(), writes=(), deps=(), dma=False, nodrain=False, stream=None):
        dd = []
        for b in reads:
            if b.writer is not None:
                dd.append(b.writer)
        for b in writes:
            if b.writer is not None:
                dd.append(b.writer)
            dd.extend(b.readers)
        dd.extend(deps)
        if self.bar[eng] is not None:
            dd.append(self.bar[eng])
        it = Item(eng, fn, dd, dma)
        it.nodrain = nodrain
        it.idx = len(self.items[eng])
        it.dmacnt = dict(self.dmac)
        if dma:
            if stream is not None:
                it.stream = stream
                cnt = getattr(self, "_scnt", None)
                if cnt is None:
                    cnt = self._scnt = {k: 0 for k in Prog.STREAMS}
                cnt[stream] += 1
                it.stream_ord = cnt[stream]
            else:
                self.dmac[eng] += 1
        self.items[eng].append(it)
        for b in reads:
            b.readers.append(it)
        for b in writes:
            b.writer = it
            b.readers = []
        for d in dd:
            if d.eng != eng or d.is_dma:
                d.marked = True
        return it

    def barrier(self):
        lasts = [self.items[e][-1] for e in ENGINES if self.items[e]]
        newbar = {}
        for e in ENGINES:
            it = Item(e, lambda eng: eng.nop(), list(lasts), False)
            it.idx = len(self.items[e])
            it.dmacnt = dict(self.dmac)
            self.items[e].append(it)
            newbar[e] = it
        for d in lasts:
            d.marked = True
        for e in ENGINES:
            self.bar[e] = newbar[e]

    def finalize(self):
        for e in ENGINES:
            o = 0
            do = 0
            for it in self.items[e]:
                if it.is_dma:
                    do += 1
                    it.ord = do
                    it.marked = True
                elif it.marked:
                    o += 1
                    it.ord = o
        self.ninstr = {e: len(self.items[e]) for e in ENGINES}

    FENCE = 12

    def replay(self, eng_name, eng_obj, esem, dsem, ssem):
        waited = {}
        ndma = 0
        last_drain = -1
        need_drain_engines = eng_name in ("vector", "gpsimd")
        for it in self.items[eng_name]:
            if need_drain_engines and not it.nodrain:
                for d in it.deps:
                    if (not d.is_dma) and d.eng == eng_name and d.idx > last_drain:
                        eng_obj.drain()
                        last_drain = it.idx - 1
                        break
            for d in it.deps:
                if d.is_dma:
                    if d.stream is not None:
                        sem = ssem[d.stream]
                        val = 16 * d.stream_ord
                    else:
                        sem = dsem[d.eng]
                        val = 16 * it.dmacnt[d.eng]
                elif d.eng == eng_name:
                    continue
                else:
                    if d.ord is None:
                        continue
                    sem = esem[d.eng]
                    val = d.ord
                key = id(sem)
                if waited.get(key, -1) < val:
                    eng_obj.wait_ge(sem, val)
                    waited[key] = val
            if it.is_dma and it.stream is None:
                ndma += 1
                if ndma > Prog.FENCE:
                    val = 16 * (ndma - Prog.FENCE)
                    key = id(dsem[eng_name])
                    if waited.get(key, -1) < val:
                        eng_obj.wait_ge(dsem[eng_name], val)
                        waited[key] = val
            inst = it.fn(eng_obj)
            if it.is_dma:
                if it.stream is not None:
                    inst.then_inc(ssem[it.stream], 16)
                else:
                    inst.then_inc(dsem[it.eng], 16)
            elif it.marked:
                inst.then_inc(esem[it.eng], 1)


def build_nc():
    nc = bass.Bass("TRN2", target_bir_lowering=False, debug=False)

    dx = nc.dram_tensor("x_in", [L, DM], F32, kind="ExternalInput").ap()
    dAlog = nc.dram_tensor("A_log", [DM, S], F32, kind="ExternalInput").ap()
    dD = nc.dram_tensor("D_vec", [DM], F32, kind="ExternalInput").ap()
    dWx = nc.dram_tensor("Wx", [96, DM], F32, kind="ExternalInput").ap()
    ddtW = nc.dram_tensor("dt_W", [DM, R], F32, kind="ExternalInput").ap()
    ddtb = nc.dram_tensor("dt_b", [DM], F32, kind="ExternalInput").ap()
    dw1 = nc.dram_tensor("w1", [H, DM], F32, kind="ExternalInput").ap()
    db1 = nc.dram_tensor("b1", [H], F32, kind="ExternalInput").ap()
    dw2 = nc.dram_tensor("w2", [DM, H], F32, kind="ExternalInput").ap()
    db2 = nc.dram_tensor("b2", [DM], F32, kind="ExternalInput").ap()
    didf = nc.dram_tensor("ident_f", [128, 128], F32, kind="ExternalInput").ap()
    didb = nc.dram_tensor("ident_b", [128, 128], BF16, kind="ExternalInput").ap()
    dones = nc.dram_tensor("ones_b", [1, 128], BF16, kind="ExternalInput").ap()
    donesf = nc.dram_tensor("ones_f", [1, 128], F32, kind="ExternalInput").ap()
    doh32 = nc.dram_tensor("onehot32", [32, 32 * 128], BF16, kind="ExternalInput").ap()
    dw1sl = nc.dram_tensor("w1sl", [512, DM], F32, kind="ExternalInput").ap()
    dw2sl = nc.dram_tensor("w2sl", [128, 4 * DM], F32, kind="ExternalInput").ap()

    dout = nc.dram_tensor("out", [L, DM], F32, kind="ExternalOutput").ap()
    dout_view = dout.rearrange("(n s p) d -> n p s d", s=4, p=128)

    dxa = nc.dram_tensor("xa_hbm", [L, DM], F32)
    dx1 = nc.dram_tensor("x1_hbm", [L, DM], F32)
    dh = nc.dram_tensor("h_hbm", [8, HC, 128, 512], BF16)
    dw1qT = nc.dram_tensor("w1qT_hbm", [HC, 128, DC, 128], BF16)
    dw2qT = nc.dram_tensor("w2qT_hbm", [DC, HC, 128, 128], BF16)
    ag1_in = nc.dram_tensor("ag1_in", [4, 128, DC, 128], BF16)
    ag2_in = nc.dram_tensor("ag2_in", [1, HC, 128, 128], BF16)
    ccw_in = nc.dram_tensor("ccw_in", [1, 8], F32)
    ccw_out = nc.dram_tensor("ccw_out", [1, 8], F32)
    cc1_in = nc.dram_tensor("cc1_in", [1, 8], F32)
    cc1_out = nc.dram_tensor("cc1_out", [1, 8], F32)
    cc2_in = nc.dram_tensor("cc2_in", [1, 8], F32)
    cc2_out = nc.dram_tensor("cc2_out", [1, 8], F32)

    dbg = {}
    if DEBUG:
        dbg["d_x1"] = nc.dram_tensor("dbg_x1", [L, DM], F32, kind="ExternalOutput").ap()
        dbg["d_scal"] = nc.dram_tensor("dbg_scal", [1, 16], F32, kind="ExternalOutput").ap()
        dbg["d_zq"] = nc.dram_tensor("dbg_zq", [128, DC * 512], F32, kind="ExternalOutput").ap()
        dbg["d_xa"] = nc.dram_tensor("dbg_xa", [L, DM], F32, kind="ExternalOutput").ap()
        dbg["d_dt"] = nc.dram_tensor("dbg_dt", [128, DC * TC], F32, kind="ExternalOutput").ap()
        dbg["d_yb"] = nc.dram_tensor("dbg_yb", [128, DC * TC], F32, kind="ExternalOutput").ap()
        dbg["d_Bb"] = nc.dram_tensor("dbg_Bb", [128, S * TC], F32, kind="ExternalOutput").ap()
        dbg["d_z0"] = nc.dram_tensor("dbg_z0", [128, DM], F32, kind="ExternalOutput").ap()
        dbg["d_small"] = nc.dram_tensor("dbg_small", [128, 8], F32, kind="ExternalOutput").ap()
        dbg["d_scal2"] = nc.dram_tensor("dbg_scal2", [1, 24], F32, kind="ExternalOutput").ap()

    es = ExitStack()
    sb = lambda n_, s_, d_: es.enter_context(nc.sbuf_tensor(n_, s_, d_))
    pst = lambda n_, s_, d_: es.enter_context(nc.psum_tensor(n_, s_, d_))

    identf = sb("identf", [128, 128], F32)
    identb = sb("identb", [128, 128], BF16)
    ones1 = sb("ones1", [1, 128], BF16)
    ones1f = sb("ones1f", [1, 128], F32)
    oh32 = sb("oh32", [32, 32 * 128], BF16)
    A_neg = sb("A_neg", [128, DC * S], F32)
    dtb_c = sb("dtb_c", [128, DC], F32)
    Dcol_c = sb("Dcol_c", [128, DC], F32)
    b1_c = sb("b1_c", [128, HC], F32)
    b2_c = sb("b2_c", [128, DC], F32)
    b1s_c = sb("b1s_c", [128, HC], F32)
    b2s_c = sb("b2s_c", [128, DC], F32)
    state = sb("state", [128, DC * S], F32)
    negm2_a = sb("negm2_a", [128, NTI], F32)
    r2_a = sb("r2_a", [128, NTI], F32)
    rq_a = sb("rq_a", [128, NTI], F32)
    bq_a = sb("bq_a", [128, NTI], F32)
    maxz_r = sb("maxz_r", [128, 2], F32)
    maxh_r = sb("maxh_r", [128, 2], F32)
    scal = sb("scal", [1, 24], F32)
    cc_sb = sb("cc_sb", [1, 8], F32)
    sc_bc = sb("sc_bc", [128, 8], F32)
    WxT = sb("WxT", [128, DC * 96], BF16)
    dtWT = sb("dtWT", [64, DC * 128], BF16)

    z0Tb = sb("z0Tb", [128, DC * TC], BF16)
    dt_t = sb("dt_t", [128, DC * TC], BF16)
    dtx_t = sb("dtx_t", [128, DC * TC], BF16)
    Bb = sb("Bb", [128, S * TC], BF16)
    Cb = sb("Cb", [128, S * TC], BF16)
    dtr_b = sb("dtr_b", [64, TC], BF16)
    BCb = sb("BCb", [32, TC], BF16)
    yb = sb("yb", [128, DC * TC], BF16)
    dA_b = sb("dA_b", [128, 2 * TC], F32)
    u_b = sb("u_b", [128, 2 * TC], BF16)
    h_b = sb("h_b", [128, 2 * TC], F32)
    hc_b = sb("hc_b", [128, 2 * TC], BF16)
    xt_b = sb("xt_b", [128, 2 * DM], F32)
    z0_b = sb("z0_b", [128, 2 * DM], F32)
    scrA = sb("scrA", [128, DM], F32)
    scrB = sb("scrB", [128, DM], F32)
    x1t = sb("x1t", [128, DM], F32)
    spt = scrA
    spa = scrB
    spe_ = scrB
    spl = x1t
    small = sb("small", [128, 16], F32)

    wchunk = z0_b
    wq_t = dt_t
    zq_nt = z0Tb
    w1s = dtx_t
    hq_nt_a = Bb
    hq_nt_b = Cb
    hin_b = dtx_t
    w2s = dt_t
    hsb = dt_t
    x2f = z0_b
    x1q = z0_b
    outt = x1t

    pt_a = pst("pt_a", [128, 512], F32)
    pt_b = pst("pt_b", [128, 512], F32)
    pt_c = pst("pt_c", [128, 512], F32)
    pt_d = pst("pt_d", [128, 512], F32)
    y_ps = pst("y_ps", [128, 1024], F32)
    pt_ab = pst("pt_ab", [128, 1024], BF16)
    pt_bb = pst("pt_bb", [128, 1024], BF16)

    esem = {}
    dsem = {}
    for e in ENGINES:
        esem[e] = es.enter_context(nc.semaphore("sem_" + e))
        dsem[e] = es.enter_context(nc.semaphore("dma_" + e))
    ssem = {k: es.enter_context(nc.semaphore("sdma_" + k)) for k in Prog.STREAMS}

    P = Prog()
    nb = lambda n: Buf(n)

    B_identf = nb("identf"); B_identb = nb("identb"); B_ones = nb("ones")
    B_Aneg = nb("A_neg"); B_dtb = nb("dtb"); B_b1 = nb("b1"); B_b2 = nb("b2")
    B_b1s = nb("b1s"); B_b2s = nb("b2s")
    B_Dbc = nb("Dbc"); B_Dcol = nb("Dcol"); B_Ddiag = nb("Ddiag")
    B_state = [nb(f"st{i}") for i in range(DC * S)]
    B_negm2 = nb("negm2"); B_r2 = nb("r2"); B_rq = nb("rq"); B_bq = nb("bq")
    B_maxz = nb("maxz"); B_maxh = nb("maxh")
    B_scal = nb("scal"); B_ccsb = nb("ccsb"); B_scbc = nb("scbc")
    B_ag1 = nb("ag1"); B_ag2 = nb("ag2")
    B_ccwi = nb("ccwi"); B_ccwo = nb("ccwo")
    B_WxT = nb("WxT"); B_dtWT = nb("dtWT")
    B_z0Tb = [nb(f"z0Tb{d}") for d in range(DC)]
    B_dt = [nb(f"dt{d}") for d in range(DC)]
    B_dtx = [nb(f"dtx{d}") for d in range(DC)]
    B_Bb = [nb(f"Bb{s}") for s in range(S)]
    B_Cb = [nb(f"Cb{s}") for s in range(S)]
    B_dtr = nb("dtr"); B_BCb = nb("BCb")
    B_yb = [nb(f"yb{d}") for d in range(DC)]
    B_dA = [nb("dA0"), nb("dA1")]
    B_u = [nb("u0"), nb("u1")]
    B_h = [nb("h0"), nb("h1")]
    B_hc = [nb("hc0"), nb("hc1")]
    B_xt = [nb("xt0"), nb("xt1")]
    B_z0 = [nb(f"z0_{i}") for i in range(2)]
    B_scrA = nb("scrA"); B_scrB = nb("scrB")
    B_x1t = nb("x1t")
    B_spt = [B_scrA, B_scrA]; B_spa = B_scrB; B_spa2 = B_scrB
    B_spe = nb('spe0'); B_spe2 = nb('spe1'); B_spl2 = nb('spl0'); B_spl3 = nb('spl1')
    B_small = nb("small"); B_smB = nb("smB")
    B_pa = nb("pt_a"); B_pb = nb("pt_b"); B_pc = nb("pt_c"); B_pd = nb("pt_d")
    B_yps = nb("yps"); B_pab = nb("pt_ab"); B_pbb = nb("pt_bb")
    B_wch = [nb("wch0"), nb("wch1")]; B_wqt = nb("wq_t")
    B_zqnt = nb("zq_nt"); B_w1s = [nb(f"w1s{i}") for i in range(4)]
    B_hqnt = [[nb(f"hq{b_}_{c_}") for c_ in range(8)] for b_ in range(2)]
    B_hin = [nb("hin0"), nb("hin1")]
    B_w2s = [nb("w2s0"), nb("w2s1")]; B_hsb = [nb("hsb0"), nb("hsb1")]
    B_x2f = [nb("x2f0"), nb("x2f1")]; B_x1q = [nb("x1q0"), nb("x1q1")]
    B_outt = [nb("outt0"), nb("outt1")]
    B_dxa = [nb(f"dxa{i}") for i in range(NTI)]
    B_dx1 = [nb(f"dx1{i}") for i in range(NTI)]
    B_dh = [[nb(f"dh{a}_{b}") for b in range(HC)] for a in range(8)]
    B_dw1qT = [nb(f"dw1qT{a}") for a in range(HC)]
    B_dw2qT = [nb(f"dw2qT{a}") for a in range(DC)]
    B_cc1i = nb("cc1i"); B_cc1o = nb("cc1o"); B_cc2i = nb("cc2i"); B_cc2o = nb("cc2o")

    E = P.emit

    def _load_combined_table(g):
        from concourse.hw_specs import get_activation_tables
        for idx, (nm, funcs) in enumerate(get_activation_tables(nc.m.arch).items()):
            if ACT.Exp in funcs and ACT.Ln in funcs:
                return g.add_instruction(mybir.InstLoadActFuncSet(
                    name=nc.get_next_instruction_name(), act_func_set_id=idx, ins=[], outs=[]))
        raise ValueError("no combined exp+ln table")
    E("scalar", _load_combined_table)
    E("sync", lambda g: g.dma_start(out=identf[:], in_=didf), writes=[B_identf], dma=True)
    E("sync", lambda g: g.dma_start(out=identb[:], in_=didb), writes=[B_identb], dma=True)
    E("sync", lambda g: g.dma_start(out=ones1[:], in_=dones), writes=[B_ones], dma=True)
    E("sync", lambda g: g.dma_start(out=ones1f[:], in_=donesf), writes=[B_ones], dma=True)
    E("sync", lambda g: g.dma_start(out=oh32[:], in_=doh32), writes=[B_ones], dma=True)
    for dc in range(DC):
        E("sync", lambda g, dc=dc: g.dma_start(out=scrB[:, dc * S:(dc + 1) * S],
                                               in_=dAlog[dc * 128:(dc + 1) * 128, :]),
          writes=[B_scrB], dma=True)
    E("scalar", lambda g: g.activation(out=scrB[:, 128:256], in_=scrB[:, 0:128], func=ACT.Exp),
      reads=[B_scrB], writes=[B_scrB])
    E("vector", lambda g: g.tensor_scalar_mul(A_neg[:], scrB[:, 128:256], -1.0),
      reads=[B_scrB], writes=[B_Aneg])
    for dc in range(DC):
        E("sync", lambda g, dc=dc: g.dma_start(out=dtb_c[:, dc:dc + 1],
                                               in_=ddtb[dc * 128:(dc + 1) * 128].rearrange("(p o) -> p o", o=1)),
          writes=[B_dtb], dma=True)
        E("sync", lambda g, dc=dc: g.dma_start(out=b2_c[:, dc:dc + 1],
                                               in_=db2[dc * 128:(dc + 1) * 128].rearrange("(p o) -> p o", o=1)),
          writes=[B_b2], dma=True)
    for hcc in range(HC):
        E("sync", lambda g, hcc=hcc: g.dma_start(out=b1_c[:, hcc:hcc + 1],
                                                 in_=db1[hcc * 128:(hcc + 1) * 128].rearrange("(p o) -> p o", o=1)),
          writes=[B_b1], dma=True)
    for dc in range(DC):
        E("sync", lambda g, dc=dc: g.dma_start(out=Dcol_c[:, dc:dc + 1],
                                               in_=dD[dc * 128:(dc + 1) * 128].rearrange("(p o) -> p o", o=1)),
          writes=[B_Dcol], dma=True)

    E("vector", lambda g: g.memset(state[:], 0.0), writes=B_state)
    E("vector", lambda g: g.memset(maxz_r[:], 0.0), writes=[B_maxz])
    E("vector", lambda g: g.memset(maxh_r[:], 0.0), writes=[B_maxh])
    E("sync", lambda g: g.dma_start(out=wchunk[0:96, 0:DM], in_=dWx), writes=[B_wch[0]], dma=True)
    E("vector", lambda g: g.tensor_copy(wq_t[0:96, 0:DM], wchunk[0:96, 0:DM]),
      reads=[B_wch[0]], writes=[B_wqt])
    for dc in range(DC):
        E("tensor", lambda g, dc=dc: g.transpose(pt_ab[0:128, 0:96],
                                                 wq_t[0:96, dc * 128:(dc + 1) * 128],
                                                 identb[0:96, 0:96]),
          reads=[B_wqt, B_identb], writes=[B_pab])
        E("scalar", lambda g, dc=dc: g.activation(out=WxT[:, dc * 96:(dc + 1) * 96],
                                                  in_=pt_ab[0:128, 0:96], func=ACT.Copy),
          reads=[B_pab], writes=[B_WxT])
    for dc in range(DC):
        E("sync", lambda g, dc=dc: g.dma_start(out=wchunk[:, 0:R],
                                               in_=ddtW[dc * 128:(dc + 1) * 128, :]),
          writes=[B_wch[0]], dma=True)
        E("vector", lambda g, dc=dc: g.tensor_copy(wq_t[:, 0:R], wchunk[:, 0:R]),
          reads=[B_wch[0]], writes=[B_wqt])
        E("tensor", lambda g, dc=dc: g.transpose(pt_ab[0:R, 0:128], wq_t[:, 0:R], identb[:]),
          reads=[B_wqt, B_identb], writes=[B_pab])
        E("scalar", lambda g, dc=dc: g.activation(out=dtWT[:, dc * 128:(dc + 1) * 128],
                                                  in_=pt_ab[0:R, 0:128], func=ACT.Copy),
          reads=[B_pab], writes=[B_dtWT])

    def finalize_max(run_tile, run_buf, idx):
        E("tensor", lambda g: g.transpose(pt_a[0:1, 0:128], run_tile, identf[:]),
          reads=[run_buf, B_identf], writes=[B_pa])
        E("vector", lambda g: g.tensor_copy(scrB[0:1, 0:128], pt_a[0:1, 0:128]),
          reads=[B_pa], writes=[B_scrB])
        E("vector", lambda g, idx=idx: g.reduce_max(scal[0:1, idx:idx + 1], scrB[0:1, 0:128], axis=AX.X),
          reads=[B_scrB], writes=[B_scal])

    E("vector", lambda g: g.memset(small[:, 0:2], 0.0), writes=[B_small])
    E("vector", lambda g: g.memset(small[:, 2:4], 0.0), writes=[B_small])
    for ch in range(4):
        bw = B_wch[ch % 2]
        E("scalar", lambda g, ch=ch: g.dma_start(
            out=wchunk[:, (ch % 2) * DM:(ch % 2 + 1) * DM],
            in_=dw1sl[ch * 128:(ch + 1) * 128, :]),
          writes=[bw], dma=True)
        E("vector", lambda g, ch=ch: g.reduce_max(small[:, 4:5],
                                                  wchunk[:, (ch % 2) * DM:(ch % 2 + 1) * DM],
                                                  axis=AX.X, apply_absolute_value=True),
          reads=[bw], writes=[B_small])
        pp_ = ch % 2
        E("vector", lambda g, pp_=pp_: g.tensor_max(small[:, 1 - pp_:2 - pp_],
                                                    small[:, pp_:pp_ + 1], small[:, 4:5]),
          reads=[B_small], writes=[B_small])
    for q in range(4):
        bw = B_wch[q % 2]
        E("scalar", lambda g, q=q: g.dma_start(
            out=wchunk[:, (q % 2) * DM:(q % 2 + 1) * DM],
            in_=dw2sl[:, q * DM:(q + 1) * DM]),
          writes=[bw], dma=True)
        E("vector", lambda g, q=q: g.reduce_max(small[:, 4:5],
                                                wchunk[:, (q % 2) * DM:(q % 2 + 1) * DM],
                                                axis=AX.X, apply_absolute_value=True),
          reads=[bw], writes=[B_small])
        pp_ = q % 2
        E("vector", lambda g, pp_=pp_: g.tensor_max(small[:, 3 - pp_:4 - pp_],
                                                    small[:, 2 + pp_:3 + pp_], small[:, 4:5]),
          reads=[B_small], writes=[B_small])
    finalize_max(small[:, 0:1], B_small, 0)
    finalize_max(small[:, 2:3], B_small, 1)
    E("vector", lambda g: g.memset(cc_sb[:], 0.0), writes=[B_ccsb])
    E("vector", lambda g: g.tensor_copy(cc_sb[0:1, 0:2], scal[0:1, 0:2]),
      reads=[B_scal], writes=[B_ccsb])
    E("gpsimd", lambda g: g.dma_start(out=ccw_in.ap(), in_=cc_sb[:]),
      reads=[B_ccsb], writes=[B_ccwi], dma=True)
    E("gpsimd", lambda g: g.collective_compute(
        "AllReduce", AL.max, replica_groups=[list(range(8))],
        ins=[ccw_in.ap().opt()], outs=[ccw_out.ap().opt()]),
      reads=[B_ccwi], writes=[B_ccwo])

    def make_scale_e(E_, src_idx, s_idx, inv_idx):
        E_("vector", lambda g, src_idx=src_idx, s_idx=s_idx: g.tensor_scalar(
            scal[0:1, s_idx:s_idx + 1], scal[0:1, src_idx:src_idx + 1],
            1e-6, 1.0 / QP, AL.max, AL.mult),
          reads=[B_scal], writes=[B_scal])
        E_("scalar", lambda g, s_idx=s_idx: g.activation(out=scal[0:1, 16:17],
                                                        in_=scal[0:1, s_idx:s_idx + 1], func=ACT.Ln),
          reads=[B_scal], writes=[B_scal])
        E_("scalar", lambda g, inv_idx=inv_idx: g.activation(out=scal[0:1, inv_idx:inv_idx + 1],
                                                            in_=scal[0:1, 16:17], func=ACT.Exp,
                                                            scale=-1.0),
          reads=[B_scal], writes=[B_scal])

    def make_scale(src_idx, s_idx, inv_idx):
        make_scale_e(E, src_idx, s_idx, inv_idx)

    def bcast_scalar_e(E_, idx, dst_col):
        E_("tensor", lambda g, idx=idx: g.matmul(pt_a[:, 0:1], ones1f[:], scal[0:1, idx:idx + 1],
                                                start=True, stop=True),
          reads=[B_scal, B_ones], writes=[B_pa])
        E_("scalar", lambda g, dst_col=dst_col: g.activation(out=sc_bc[:, dst_col:dst_col + 1],
                                                            in_=pt_a[:, 0:1], func=ACT.Copy),
          reads=[B_pa], writes=[B_scbc])

    def bcast_scalar(idx, dst_col):
        bcast_scalar_e(E, idx, dst_col)

    def emit_PW_b(E_):
        E_("sync", lambda g: g.dma_start(out=cc_sb[:], in_=ccw_out.ap()),
           reads=[B_ccwo], writes=[B_ccsb], dma=True)
        E_("vector", lambda g: g.tensor_copy(scal[0:1, 0:2], cc_sb[0:1, 0:2]),
           reads=[B_ccsb], writes=[B_scal])
        make_scale_e(E_, 0, 2, 4)
        make_scale_e(E_, 1, 3, 5)
        bcast_scalar_e(E_, 4, 0)
        bcast_scalar_e(E_, 5, 1)
        for half in range(2):
            for ch in range(4):
                E_("scalar", lambda g, half=half, ch=ch: g.dma_start(
                    out=x1t[:], in_=(dw1sl[ch * 128:(ch + 1) * 128, :] if half == 0
                                     else dw2sl[:, ch * DM:(ch + 1) * DM])),
                   writes=[B_x1t], dma=True)
                E_("scalar", lambda g, half=half: g.activation(
                    out=scrB[:], in_=x1t[:], func=ACT.Identity,
                    scale=sc_bc[:, half:half + 1]),
                   reads=[B_x1t, B_scbc], writes=[B_scrB, B_spa, B_spa2])
                E_("vector", lambda g: g.tensor_scalar(z0Tb[:, 0:DM], scrB[:], RND, RND,
                                                       AL.add, AL.subtract),
                   reads=[B_scrB], writes=[B_z0Tb[0]])
                for k in range(8):
                    E_("tensor", lambda g, k=k: g.transpose(
                        pt_ab[:, k * 128:(k + 1) * 128],
                        z0Tb[:, k * 128:(k + 1) * 128], identb[:]),
                       reads=[B_z0Tb[0], B_identb], writes=[B_pab])
                    E_("scalar", lambda g, k=k: g.activation(
                        out=z0Tb[:, DM + k * 128:DM + (k + 1) * 128],
                        in_=pt_ab[:, k * 128:(k + 1) * 128], func=ACT.Copy),
                       reads=[B_pab], writes=[B_z0Tb[1]])
                if half == 0:
                    E_("gpsimd", lambda g, ch=ch: g.dma_start(
                        out=ag1_in.ap()[ch],
                        in_=z0Tb[:, DM:DM + DC * 128].rearrange("p (d c) -> p d c", d=DC)),
                       reads=[B_z0Tb[1]], writes=[B_ag1], dma=True)
                else:
                    E_("gpsimd", lambda g, ch=ch: g.dma_start(
                        out=ag2_in.ap()[0, ch * 8:(ch + 1) * 8].rearrange("k p c -> p k c"),
                        in_=z0Tb[:, DM:DM + 8 * 128].rearrange("p (k c) -> p k c", k=8)),
                       reads=[B_z0Tb[1]], writes=[B_ag2], dma=True)
            if half == 0:
                E_("gpsimd", lambda g: g.collective_compute(
                    "AllGather", AL.bypass, replica_groups=[list(range(8))],
                    ins=[ag1_in.ap().opt()], outs=[dw1qT.ap().opt()]),
                   reads=[B_ag1], writes=[B_dw1qT[a] for a in range(HC)])
            else:
                E_("gpsimd", lambda g: g.collective_compute(
                    "AllGather", AL.bypass, replica_groups=[list(range(8))],
                    ins=[ag2_in.ap().opt()], outs=[dw2qT.ap().opt()]),
                   reads=[B_ag2], writes=[B_dw2qT[a] for a in range(DC)])

    def emit_A_proj(tc, E_):
            for tib in range(4):
                for tis in range(2):
                    ti = tc * TPC + tib * 2 + tis
                    bx = B_xt[tis % 2]
                    bz = B_z0[tis % 2]
                    o = 8 * (tis % 2)
                    bs = (B_small, B_smB)[tis % 2]
                    sA = (scrA, scrB)[tis % 2]
                    bsA = (B_scrA, B_scrB)[tis % 2]
                    E_("sync", lambda g, ti=ti, tis=tis: g.dma_start(
                        out=xt_b[:, (tis % 2) * DM:(tis % 2 + 1) * DM],
                        in_=dx[ti * 128:(ti + 1) * 128, :]),
                      writes=[bx], dma=True)
                    E_("scalar", lambda g, tis=tis, o=o, sA=sA: g.activation(
                        out=sA[:], in_=xt_b[:, (tis % 2) * DM:(tis % 2 + 1) * DM],
                        func=ACT.Copy, accum_out=small[:, o:o + 1]),
                      reads=[bx], writes=[bsA, bs])
                    E_("scalar", lambda g, tis=tis, o=o, sA=sA: g.activation(
                        out=sA[:], in_=xt_b[:, (tis % 2) * DM:(tis % 2 + 1) * DM],
                        func=ACT.Square, accum_out=small[:, o + 1:o + 2]),
                      reads=[bx], writes=[bsA, bs])
                    E_("vector", lambda g, o=o: g.tensor_scalar_mul(small[:, o + 2:o + 3],
                                                                    small[:, o:o + 1], -1.0 / DM),
                      reads=[bs], writes=[bs])
                    E_("scalar", lambda g, o=o: g.activation(out=small[:, o + 3:o + 4],
                                                             in_=small[:, o + 2:o + 3],
                                                             func=ACT.Square),
                      reads=[bs], writes=[bs])
                    E_("vector", lambda g, o=o: g.tensor_scalar(small[:, o + 4:o + 5],
                                                                small[:, o + 1:o + 2], 1.0 / DM, EPS,
                                                                AL.mult, AL.add),
                      reads=[bs], writes=[bs])
                    E_("vector", lambda g, o=o: g.tensor_sub(small[:, o + 5:o + 6],
                                                             small[:, o + 4:o + 5],
                                                             small[:, o + 3:o + 4]),
                      reads=[bs], writes=[bs])
                    E_("scalar", lambda g, o=o: g.activation(out=small[:, o + 6:o + 7],
                                                             in_=small[:, o + 5:o + 6], func=ACT.Ln),
                      reads=[bs], writes=[bs])
                    E_("scalar", lambda g, o=o: g.activation(out=small[:, o + 7:o + 8],
                                                             in_=small[:, o + 6:o + 7], func=ACT.Exp,
                                                             scale=-0.5),
                      reads=[bs], writes=[bs])
                    E_("vector", lambda g, tis=tis, o=o: g.tensor_scalar(
                        z0_b[:, tis * DM:(tis + 1) * DM], xt_b[:, (tis % 2) * DM:(tis % 2 + 1) * DM],
                        small[:, o + 2:o + 3], small[:, o + 7:o + 8], AL.add, AL.mult),
                      reads=[bx, bs], writes=[bz])
                    if DEBUG and ti == 0 and tis == 0:
                        E_("gpsimd", lambda g: g.dma_start(out=dbg["d_z0"], in_=z0_b[:, 0:DM]),
                          reads=[bz], dma=True)
                        E_("gpsimd", lambda g: g.dma_start(out=dbg["d_small"], in_=small[:]),
                          reads=[B_small], dma=True)
                for dc in range(DC):
                    for tis in range(2):
                        E_("tensor", lambda g, dc=dc, tis=tis: g.transpose(
                            pt_b[:, tis * 128:(tis + 1) * 128],
                            z0_b[:, tis * DM + dc * 128:tis * DM + (dc + 1) * 128], identf[:]),
                          reads=[B_z0[tis], B_identf], writes=[B_pb])
                    E_("scalar", lambda g, dc=dc, tib=tib: g.activation(
                        out=z0Tb[:, dc * TC + tib * 256:dc * TC + (tib + 1) * 256],
                        in_=pt_b[:, 0:256], func=ACT.Copy),
                      reads=[B_pb], writes=[B_z0Tb[dc]])
            for ntb in range(2):
                for dc in range(DC):
                    E_("tensor", lambda g, ntb=ntb, dc=dc: g.matmul(
                        pt_b[0:96, 0:512], WxT[:, dc * 96:(dc + 1) * 96],
                        z0Tb[:, dc * TC + ntb * 512:dc * TC + (ntb + 1) * 512],
                        start=(dc == 0), stop=(dc == DC - 1)),
                      reads=[B_WxT, B_z0Tb[dc]], writes=[B_pb])
                E_("scalar", lambda g, ntb=ntb: g.activation(out=dtr_b[:, ntb * 512:(ntb + 1) * 512],
                                                            in_=pt_b[0:64, 0:512], func=ACT.Copy),
                  reads=[B_pb], writes=[B_dtr])
                E_("scalar", lambda g, ntb=ntb: g.activation(out=BCb[:, ntb * 512:(ntb + 1) * 512],
                                                            in_=pt_b[64:96, 0:512], func=ACT.Copy),
                  reads=[B_pb], writes=[B_BCb])


    emit_A_proj(0, E)
    for tc in range(NTC):
        for dc in range(DC):
            for ntb in range(2):
                pdt = (pt_c, pt_b)[ntb % 2]
                bpdt = (B_pc, B_pb)[ntb % 2]
                E("tensor", lambda g, dc=dc, ntb=ntb, pdt=pdt: g.matmul(
                    pdt[:, 0:512], dtWT[:, dc * 128:(dc + 1) * 128],
                    dtr_b[:, ntb * 512:(ntb + 1) * 512], start=True, stop=True),
                  reads=[B_dtWT, B_dtr], writes=[bpdt])
                p_ = ntb % 2
                bspt = B_spt[p_]
                bspa = (B_spa, B_spa2)[p_]
                bspe = (B_spe, B_spe2)[p_]
                bspl = (B_spl2, B_spl3)[p_]
                tspa = scrB[:, p_ * 512:(p_ + 1) * 512]
                tspe = z0_b[:, p_ * 512:(p_ + 1) * 512]
                tspl = z0_b[:, 1024 + p_ * 512:1024 + (p_ + 1) * 512]
                pdt = (pt_c, pt_b)[p_]
                bpdt = (B_pc, B_pb)[p_]
                E("scalar", lambda g, dc=dc, p_=p_, pdt=pdt: g.activation(
                    out=spt[:, p_ * 512:(p_ + 1) * 512], in_=pdt[:, 0:512],
                    func=ACT.Identity, bias=dtb_c[:, dc:dc + 1]),
                  reads=[bpdt, B_dtb], writes=[bspt])
                E("vector", lambda g, p_=p_, tspa=tspa: g.scalar_tensor_tensor(
                    tspa, spt[:, p_ * 512:(p_ + 1) * 512], -1.0,
                    spt[:, p_ * 512:(p_ + 1) * 512], AL.mult, AL.max),
                  reads=[bspt], writes=[bspa])
                E("scalar", lambda g, tspa=tspa, tspe=tspe: g.activation(
                    out=tspe, in_=tspa, func=ACT.Exp, scale=-1.0),
                  reads=[bspa], writes=[bspe])
                E("scalar", lambda g, tspe=tspe, tspl=tspl: g.activation(
                    out=tspl, in_=tspe, func=ACT.Ln, bias=1.0),
                  reads=[bspe], writes=[bspl])
                E("vector", lambda g, dc=dc, ntb=ntb, p_=p_, tspl=tspl: g.scalar_tensor_tensor(
                    dt_t[:, dc * TC + ntb * 512:dc * TC + (ntb + 1) * 512],
                    spt[:, p_ * 512:(p_ + 1) * 512], 0.0, tspl, AL.max, AL.add),
                  reads=[bspt, bspl], writes=[B_dt[dc]])
            E("vector", lambda g, dc=dc: g.tensor_mul(dtx_t[:, dc * TC:(dc + 1) * TC],
                                                      dt_t[:, dc * TC:(dc + 1) * TC],
                                                      z0Tb[:, dc * TC:(dc + 1) * TC]),
              reads=[B_dt[dc], B_z0Tb[dc]], writes=[B_dtx[dc]])
        for s in range(S):
            for ntb in range(2):
                pbc = (pt_c, pt_d)[ntb]
                bbc = (B_pc, B_pd)[ntb]
                E("tensor", lambda g, s=s, ntb=ntb, pbc=pbc: g.matmul(
                    pbc[:, 0:512], oh32[:, s * 128:(s + 1) * 128],
                    BCb[:, ntb * 512:(ntb + 1) * 512], start=True, stop=True),
                  reads=[B_ones, B_BCb], writes=[bbc])
                E("scalar", lambda g, s=s, ntb=ntb, pbc=pbc: g.activation(
                    out=Bb[:, s * TC + ntb * 512:s * TC + (ntb + 1) * 512],
                    in_=pbc[:, 0:512], func=ACT.Copy),
                  reads=[bbc], writes=[B_Bb[s]])
            for ntb in range(2):
                pbc = (pt_c, pt_d)[ntb]
                bbc = (B_pc, B_pd)[ntb]
                E("tensor", lambda g, s=s, ntb=ntb, pbc=pbc: g.matmul(
                    pbc[:, 0:512], oh32[:, (16 + s) * 128:(17 + s) * 128],
                    BCb[:, ntb * 512:(ntb + 1) * 512], start=True, stop=True),
                  reads=[B_ones, B_BCb], writes=[bbc])
                E("scalar", lambda g, s=s, ntb=ntb, pbc=pbc: g.activation(
                    out=Cb[:, s * TC + ntb * 512:s * TC + (ntb + 1) * 512],
                    in_=pbc[:, 0:512], func=ACT.Copy),
                  reads=[bbc], writes=[B_Cb[s]])
        thunks = []
        if tc + 1 < NTC:
            emit_A_proj(tc + 1, lambda *a, **k: thunks.append((a, k)))
        if tc == NTC - 1:
            emit_PW_b(lambda *a, **k: thunks.append((a, k)))
        tpos = [0]

        def drip(n):
            for _ in range(n):
                if tpos[0] < len(thunks):
                    a_, k_ = thunks[tpos[0]]
                    E(*a_, **k_)
                    tpos[0] += 1

        for dc in range(DC):
            E("vector", lambda g, dc=dc: g.tensor_scalar(
                yb[:, dc * TC:(dc + 1) * TC], z0Tb[:, dc * TC:(dc + 1) * TC],
                Dcol_c[:, dc:dc + 1], None, AL.mult),
              reads=[B_z0Tb[dc], B_Dcol], writes=[B_yb[dc]])
        for dc in range(DC):
            yps = y_ps
            byps = B_yps
            for sp in range(S // 2):
                s0 = 2 * sp
                for s in (s0, s0 + 1):
                    k = s % 2
                    st_idx = dc * S + s
                    E("scalar", lambda g, dc=dc, s=s, k=k: g.activation(
                        out=dA_b[:, k * TC:(k + 1) * TC], in_=dt_t[:, dc * TC:(dc + 1) * TC],
                        func=ACT.Exp, scale=A_neg[:, dc * S + s:dc * S + s + 1]),
                      reads=[B_dt[dc], B_Aneg], writes=[B_dA[k]])
                    E("vector", lambda g, dc=dc, s=s, k=k: g.tensor_mul(
                        u_b[:, k * TC:(k + 1) * TC], dtx_t[:, dc * TC:(dc + 1) * TC],
                        Bb[:, s * TC:(s + 1) * TC]),
                      reads=[B_dtx[dc], B_Bb[s]], writes=[B_u[k]])
                    E("vector", lambda g, k=k, st_idx=st_idx: g.tensor_tensor_scan(
                        h_b[:, k * TC:(k + 1) * TC], dA_b[:, k * TC:(k + 1) * TC],
                        u_b[:, k * TC:(k + 1) * TC], state[:, st_idx:st_idx + 1],
                        AL.mult, AL.add),
                      reads=[B_dA[k], B_u[k], B_state[st_idx]], writes=[B_h[k]])
                st0 = dc * S + s0
                E("vector", lambda g, st0=st0: g.tensor_copy(
                    state[:, st0:st0 + 2], h_b[:, TC - 1:2 * TC:TC]),
                  reads=[B_h[0], B_h[1]], writes=[B_state[st0], B_state[st0 + 1]])
                E("vector", lambda g, s0=s0: g.tensor_mul(
                    hc_b[:, 0:2 * TC], h_b[:, 0:2 * TC], Cb[:, s0 * TC:(s0 + 2) * TC]),
                  reads=[B_h[0], B_h[1], B_Cb[s0], B_Cb[s0 + 1]], writes=[B_hc[0], B_hc[1]])
                for s in (s0, s0 + 1):
                    k = s % 2
                    for ntb in range(2):
                        E("tensor", lambda g, k=k, ntb=ntb, s=s, yps=yps: g.matmul(
                            yps[:, ntb * 512:(ntb + 1) * 512], identb[:],
                            hc_b[:, k * TC + ntb * 512:k * TC + (ntb + 1) * 512],
                            start=(s == 0), stop=(s == S - 1)),
                          reads=[B_identb, B_hc[k]], writes=[byps])
            drip(2 + len(thunks) // DC)
            E("vector", lambda g, dc=dc, yps=yps: g.tensor_add(
                yb[:, dc * TC:(dc + 1) * TC], yps[:, 0:1024],
                yb[:, dc * TC:(dc + 1) * TC]),
              reads=[byps, B_yb[dc]], writes=[B_yb[dc]])
        drip(len(thunks))
        if DEBUG and tc == 0:
            E("gpsimd", lambda g: g.dma_start(out=dbg["d_dt"], in_=dt_t[:]),
              reads=[B_dt[7]], deps=[P.items["vector"][-1]], dma=True)
            E("gpsimd", lambda g: g.dma_start(out=dbg["d_Bb"], in_=Bb[:]),
              reads=[B_Bb[15]], dma=True)
            E("gpsimd", lambda g: g.dma_start(out=dbg["d_yb"], in_=yb[:]),
              reads=[B_yb[7]], deps=[P.items["scalar"][-1]], dma=True)
        for tloc in range(TPC):
            ti = tc * TPC + tloc
            bx = B_xt[tloc % 2]
            E("sync", lambda g, ti=ti, tloc=tloc: g.dma_start(
                out=xt_b[:, (tloc % 2) * DM:(tloc % 2 + 1) * DM],
                in_=dx[ti * 128:(ti + 1) * 128, :]),
              writes=[bx], dma=True)
            for dc in range(DC):
                pp = (pt_ab, pt_bb)[dc // 4]
                bpp = (B_pab, B_pbb)[dc // 4]
                E("tensor", lambda g, dc=dc, tloc=tloc, pp=pp: g.transpose(
                    pp[:, (dc % 4) * 128:(dc % 4 + 1) * 128],
                    yb[:, dc * TC + tloc * 128:dc * TC + (tloc + 1) * 128], identb[:]),
                  reads=[B_yb[dc], B_identb], writes=[bpp])
            E("vector", lambda g, tloc=tloc: g.tensor_add(
                x1t[:, 0:512], pt_ab[:, 0:512], xt_b[:, (tloc % 2) * DM:(tloc % 2) * DM + 512]),
              reads=[B_pab, bx], writes=[B_x1t])
            E("vector", lambda g, tloc=tloc: g.tensor_add(
                x1t[:, 512:1024], pt_bb[:, 0:512],
                xt_b[:, (tloc % 2) * DM + 512:(tloc % 2 + 1) * DM]),
              reads=[B_pbb, bx], writes=[B_x1t])
            E("scalar", lambda g: g.activation(out=scrA[:], in_=x1t[:], func=ACT.Copy,
                                               accum_out=small[:, 0:1]),
              reads=[B_x1t], writes=[B_scrA, B_small])
            E("scalar", lambda g: g.activation(out=scrA[:], in_=x1t[:], func=ACT.Square,
                                               accum_out=small[:, 1:2]),
              reads=[B_x1t], writes=[B_scrA, B_small])
            E("vector", lambda g, ti=ti: g.tensor_scalar_mul(negm2_a[:, ti:ti + 1], small[:, 0:1],
                                                             -1.0 / DM),
              reads=[B_small], writes=[B_negm2])
            E("scalar", lambda g, ti=ti: g.activation(out=small[:, 3:4], in_=negm2_a[:, ti:ti + 1],
                                                      func=ACT.Square),
              reads=[B_negm2], writes=[B_small])
            E("vector", lambda g: g.tensor_scalar(small[:, 4:5], small[:, 1:2], 1.0 / DM, EPS,
                                                  AL.mult, AL.add),
              reads=[B_small], writes=[B_small])
            E("vector", lambda g: g.tensor_sub(small[:, 5:6], small[:, 4:5], small[:, 3:4]),
              reads=[B_small], writes=[B_small])
            E("scalar", lambda g: g.activation(out=small[:, 6:7], in_=small[:, 5:6], func=ACT.Ln),
              reads=[B_small], writes=[B_small])
            E("scalar", lambda g, ti=ti: g.activation(out=r2_a[:, ti:ti + 1], in_=small[:, 6:7],
                                                      func=ACT.Exp, scale=-0.5),
              reads=[B_small], writes=[B_r2])
            E("scalar", lambda g, ti=ti: g.activation(out=scrA[:], in_=x1t[:], func=ACT.Identity,
                                                      bias=negm2_a[:, ti:ti + 1]),
              reads=[B_x1t, B_negm2], writes=[B_scrA])
            E("vector", lambda g: g.reduce_max(small[:, 3:4], scrA[:], axis=AX.X,
                                               apply_absolute_value=True),
              reads=[B_scrA], writes=[B_small])
            E("vector", lambda g, ti=ti: g.tensor_mul(small[:, 4:5], small[:, 3:4], r2_a[:, ti:ti + 1]),
              reads=[B_small, B_r2], writes=[B_small])
            ppz = ti % 2
            E("vector", lambda g, ppz=ppz: g.tensor_max(maxz_r[:, 1 - ppz:2 - ppz],
                                                        maxz_r[:, ppz:ppz + 1], small[:, 4:5]),
              reads=[B_small, B_maxz], writes=[B_maxz])
            E("gpsimd", lambda g, ti=ti: g.dma_start(out=dx1.ap()[ti * 128:(ti + 1) * 128, :],
                                                     in_=x1t[:]),
              reads=[B_x1t], writes=[B_dx1[ti]], dma=True)

    finalize_max(maxz_r[:, (NTI % 2):(NTI % 2) + 1], B_maxz, 6)
    E("vector", lambda g: g.tensor_copy(cc_sb[:], scal[0:1, 6:7].broadcast_to([1, 8])),
      reads=[B_scal], writes=[B_ccsb])
    E("gpsimd", lambda g: g.dma_start(out=cc1_in.ap(), in_=cc_sb[:]),
      reads=[B_ccsb], writes=[B_cc1i], dma=True)
    E("gpsimd", lambda g: g.collective_compute(
        "AllReduce", AL.max, replica_groups=[list(range(8))],
        ins=[cc1_in.ap().opt()], outs=[cc1_out.ap().opt()]),
      reads=[B_cc1i], writes=[B_cc1o])
    E("sync", lambda g: g.dma_start(out=cc_sb[:], in_=cc1_out.ap()),
      reads=[B_cc1o], writes=[B_ccsb], dma=True)
    E("vector", lambda g: g.tensor_copy(scal[0:1, 7:8], cc_sb[0:1, 0:1]),
      reads=[B_ccsb], writes=[B_scal])
    make_scale(7, 8, 9)
    bcast_scalar(9, 2)
    E("vector", lambda g: g.tensor_mul(scal[0:1, 10:11], scal[0:1, 8:9], scal[0:1, 2:3]),
      reads=[B_scal], writes=[B_scal])
    bcast_scalar(10, 3)
    E("vector", lambda g: g.tensor_scalar(b1s_c[:], b1_c[:], sc_bc[:, 3:4], None, AL.mult),
      reads=[B_b1, B_scbc], writes=[B_b1s])
    E("vector", lambda g: g.tensor_scalar(rq_a[:], r2_a[:], sc_bc[:, 2:3], None, AL.mult),
      reads=[B_r2, B_scbc], writes=[B_rq])
    E("vector", lambda g: g.tensor_mul(bq_a[:], negm2_a[:], rq_a[:]),
      reads=[B_negm2, B_rq], writes=[B_bq])

    P.barrier()

    E("scalar", lambda g: g.preload_act_table(ACT.Gelu))

    B_zqx = [nb("zq0"), nb("zq1")]

    wqs = [dt_t, dt_t]
    wqo = [0, 2048]
    B_wq2 = [nb("wq0"), nb("wq1")]

    def pc_quant_prep(nt, tl):
        ti = nt * 4 + tl
        bx = B_xt[tl % 2]
        E("sync", lambda g, ti=ti, tl=tl: g.dma_start(
            out=xt_b[:, (tl % 2) * DM:(tl % 2 + 1) * DM],
            in_=dx1.ap()[ti * 128:(ti + 1) * 128, :]),
          reads=[B_dx1[ti]], writes=[bx], dma=True, stream="act")
        bsc = (B_scrA, B_scrB)[tl % 2]
        tsc = (scrA, scrB)[tl % 2]
        E("scalar", lambda g, ti=ti, tl=tl, tsc=tsc: g.activation(
            out=tsc[:], in_=xt_b[:, (tl % 2) * DM:(tl % 2 + 1) * DM],
            func=ACT.Identity, scale=rq_a[:, ti:ti + 1], bias=bq_a[:, ti:ti + 1]),
          reads=[bx, B_rq, B_bq], writes=[bsc])
        o = wqo[tl % 2]
        E("vector", lambda g, tsc=tsc, o=o: g.tensor_scalar(dt_t[:, o:o + DM], tsc[:], RND, RND,
                                                            AL.add, AL.subtract),
          reads=[bsc], writes=[B_wq2[tl % 2]])

    def pc_quant_emit(nt, tl):
        zb = nt % 2
        bzq = B_zqx[zb]
        o = wqo[tl % 2]
        for dcb in range(2):
            hb = dcb % 2
            bpab = (B_pab, B_pbb)[hb]
            tpab = (pt_ab, pt_bb)[hb]
            for dcs in range(4):
                dc = dcb * 4 + dcs
                E("tensor", lambda g, dc=dc, dcs=dcs, tpab=tpab, o=o: g.transpose(
                    tpab[:, dcs * 128:(dcs + 1) * 128], dt_t[:, o + dc * 128:o + (dc + 1) * 128],
                    identb[:]),
                  reads=[B_wq2[tl % 2], B_identb], writes=[bpab])
            E("vector", lambda g, dcb=dcb, tl=tl, zb=zb, tpab=tpab: g.tensor_copy(
                zq_nt[:, zb * 4096:(zb + 1) * 4096].rearrange("p (d t) -> p d t", d=DC)[
                    :, dcb * 4:(dcb + 1) * 4, tl * 128:(tl + 1) * 128],
                tpab[:, 0:512].rearrange("p (d t) -> p d t", d=4)),
              reads=[bpab], writes=[bzq])

    def pc_quant_tl(nt, tl):
        pc_quant_prep(nt, tl)
        pc_quant_emit(nt, tl)

    W1PF = 4
    for tl in range(4):
        pc_quant_tl(0, tl)
    if DEBUG:
        E("gpsimd", lambda g: g.dma_start(out=dbg["d_zq"], in_=zq_nt[:, 0:DC * 512]),
          reads=[B_zqx[0]], dma=True)
    mm_banks = [(pt_a, B_pa), (pt_b, B_pb), (pt_c, B_pc), (pt_d, B_pd)]
    for nt in range(8):
        zb = nt % 2
        for pf in range(W1PF - 1):
            E("sync", lambda g, pf=pf: g.dma_start(
                out=w1s[:, pf * DC * 128:(pf + 1) * DC * 128],
                in_=dw1qT.ap()[pf].rearrange("p d c -> p (d c)")),
              reads=[B_dw1qT[pf]], writes=[B_w1s[pf]], dma=True)
        for hcc in range(HC):
            k = hcc % W1PF
            pp, bpp = mm_banks[hcc % 4]
            for dc in range(DC):
                E("tensor", lambda g, hcc=hcc, dc=dc, k=k, pp=pp, zb=zb: g.matmul(
                    pp[:, 0:512], w1s[:, k * DC * 128 + dc * 128:k * DC * 128 + (dc + 1) * 128],
                    zq_nt[:, zb * 4096 + dc * 512:zb * 4096 + (dc + 1) * 512],
                    start=(dc == 0), stop=(dc == DC - 1)),
                  reads=[B_w1s[k], B_zqx[zb]], writes=[bpp])
            if hcc + W1PF - 1 < HC:
                kpf = (hcc + W1PF - 1) % W1PF
                E("sync", lambda g, hcc=hcc, kpf=kpf: g.dma_start(
                    out=w1s[:, kpf * DC * 128:(kpf + 1) * DC * 128],
                    in_=dw1qT.ap()[hcc + W1PF - 1].rearrange("p d c -> p (d c)")),
                  reads=[B_dw1qT[hcc + W1PF - 1]], writes=[B_w1s[kpf]], dma=True, stream="wgt")
            bh = B_hsb[hcc % 2]
            E("scalar", lambda g, hcc=hcc, pp=pp: g.activation(
                out=hsb[:, 1024 + (hcc % 2) * 512:1024 + (hcc % 2 + 1) * 512], in_=pp[:, 0:512],
                func=ACT.Gelu, scale=sc_bc[:, 3:4], bias=b1s_c[:, hcc:hcc + 1]),
              reads=[bpp, B_scbc, B_b1s], writes=[bh])
            E("gpsimd", lambda g, hcc=hcc, nt=nt: g.dma_start(
                out=dh.ap()[nt, hcc], in_=hsb[:, 1024 + (hcc % 2) * 512:1024 + (hcc % 2 + 1) * 512]),
              reads=[bh], writes=[B_dh[nt][hcc]], dma=True, stream="hst")
            E("vector", lambda g, hcc=hcc: g.reduce_max(
                small[:, 7:8], hsb[:, 1024 + (hcc % 2) * 512:1024 + (hcc % 2 + 1) * 512], axis=AX.X,
                apply_absolute_value=True),
              reads=[bh], writes=[B_small])
            pph = (nt * HC + hcc) % 2
            E("vector", lambda g, pph=pph: g.tensor_max(maxh_r[:, 1 - pph:2 - pph],
                                                        maxh_r[:, pph:pph + 1], small[:, 7:8]),
              reads=[B_small, B_maxh], writes=[B_maxh])
            if nt + 1 < 8 and hcc % 8 == 0:
                pc_quant_prep(nt + 1, hcc // 8)
            if nt + 1 < 8 and hcc % 8 == 3:
                pc_quant_emit(nt + 1, hcc // 8)

    finalize_max(maxh_r[:, 0:1], B_maxh, 11)
    E("vector", lambda g: g.tensor_copy(cc_sb[:], scal[0:1, 11:12].broadcast_to([1, 8])),
      reads=[B_scal], writes=[B_ccsb])
    E("gpsimd", lambda g: g.dma_start(out=cc2_in.ap(), in_=cc_sb[:]),
      reads=[B_ccsb], writes=[B_cc2i], dma=True)
    E("gpsimd", lambda g: g.collective_compute(
        "AllReduce", AL.max, replica_groups=[list(range(8))],
        ins=[cc2_in.ap().opt()], outs=[cc2_out.ap().opt()]),
      reads=[B_cc2i], writes=[B_cc2o])
    E("sync", lambda g: g.dma_start(out=cc_sb[:], in_=cc2_out.ap()),
      reads=[B_cc2o], writes=[B_ccsb], dma=True)
    E("vector", lambda g: g.tensor_copy(scal[0:1, 12:13], cc_sb[0:1, 0:1]),
      reads=[B_ccsb], writes=[B_scal])
    E("scalar", _load_combined_table)
    make_scale(12, 13, 14)
    bcast_scalar(14, 4)
    E("vector", lambda g: g.tensor_mul(scal[0:1, 15:16], scal[0:1, 13:14], scal[0:1, 3:4]),
      reads=[B_scal], writes=[B_scal])
    bcast_scalar(15, 5)
    E("vector", lambda g: g.tensor_scalar(b2s_c[:], b2_c[:], sc_bc[:, 5:6], None, AL.mult),
      reads=[B_b2, B_scbc], writes=[B_b2s])
    if DEBUG:
        E("vector", lambda g: g.tensor_copy(scrB[0:1, 0:16], scal[0:1, 0:16]),
          reads=[B_scal], writes=[B_scrB])
        E("gpsimd", lambda g: g.dma_start(out=dbg["d_scal"], in_=scrB[0:1, 0:16]),
          reads=[B_scrB], dma=True)

    P.barrier()

    B_hraw = nb("hraw")

    def pd_readback(nt):
        E("sync", lambda g, nt=nt: g.dma_start(
            out=yb[:, 0:8192].rearrange("p (k c) -> p k c", k=16),
            in_=dh.ap()[nt, 0:16].rearrange("k p c -> p k c")),
          reads=[B_dh[nt][k_] for k_ in range(16)], writes=[B_hraw], dma=True, stream="hrw")
        E("sync", lambda g, nt=nt: g.dma_start(
            out=dtx_t[:, 0:8192].rearrange("p (k c) -> p k c", k=16),
            in_=dh.ap()[nt, 16:32].rearrange("k p c -> p k c")),
          reads=[B_dh[nt][k_] for k_ in range(16, 32)], writes=[B_hraw], dma=True, stream="hrw")

    def pd_quant_chunk(nt, chunk):
        kb = nt % 2
        hqt = (hq_nt_a, hq_nt_b)[kb]
        for kc in range(chunk * 4, chunk * 4 + 4):
            src = yb if kc < 16 else dtx_t
            off = (kc if kc < 16 else kc - 16) * 512
            bsc = (B_scrA, B_scrB)[kc % 2]
            tsc = (scrA, scrB)[kc % 2]
            E("scalar", lambda g, src=src, off=off, tsc=tsc: g.activation(
                out=tsc[:, 0:512], in_=src[:, off:off + 512],
                func=ACT.Identity, scale=sc_bc[:, 4:5]),
              reads=[B_hraw, B_scbc], writes=[bsc])
            E("vector", lambda g, kc=kc, hqt=hqt, tsc=tsc: g.tensor_scalar(
                hqt[:, kc * 512:(kc + 1) * 512],
                tsc[:, 0:512], RND, RND, AL.add, AL.subtract),
              reads=[bsc], writes=[B_hqnt[kb][kc // 4]])

    def pd_quant(nt):
        for chunk in range(8):
            pd_quant_chunk(nt, chunk)

    pd_readback(0)
    pd_quant(0)
    for nt in range(8):
        kb = nt % 2
        if nt + 1 < 8:
            pd_readback(nt + 1)
        E("sync", lambda g: g.dma_start(
            out=w2s[:, 0:HC * 128].rearrange("p (k c) -> p k c", k=HC),
            in_=dw2qT.ap()[0].rearrange("k p c -> p k c")),
          reads=[B_dw2qT[0]], writes=[B_w2s[0]], dma=True, stream="wgt")
        for dout in range(DC):
            k2 = dout % 2
            pp = (pt_c, pt_d)[dout % 2]
            bpp = (B_pc, B_pd)[dout % 2]
            hqt = (hq_nt_a, hq_nt_b)[kb]
            for kc in range(HC):
                E("tensor", lambda g, kc=kc, k2=k2, hqt=hqt, pp=pp: g.matmul(
                    pp[:, 0:512], w2s[:, k2 * HC * 128 + kc * 128:k2 * HC * 128 + (kc + 1) * 128],
                    hqt[:, kc * 512:(kc + 1) * 512],
                    start=(kc == 0), stop=(kc == HC - 1)),
                  reads=[B_w2s[k2], B_hqnt[kb][kc // 4]], writes=[bpp])
            if dout + 1 < DC:
                E("sync", lambda g, dout=dout, k2=k2: g.dma_start(
                    out=w2s[:, (1 - k2) * HC * 128:(2 - k2) * HC * 128].rearrange("p (k c) -> p k c", k=HC),
                    in_=dw2qT.ap()[dout + 1].rearrange("k p c -> p k c")),
                  reads=[B_dw2qT[dout + 1]], writes=[B_w2s[1 - k2]], dma=True, stream="wgt")
            bx2 = B_x2f[dout % 2]
            E("scalar", lambda g, dout=dout, pp=pp: g.activation(
                out=x2f[:, (dout % 2) * 512:(dout % 2) * 512 + 512], in_=pp[:, 0:512],
                func=ACT.Identity, scale=sc_bc[:, 5:6], bias=b2s_c[:, dout:dout + 1]),
              reads=[bpp, B_scbc, B_b2s], writes=[bx2])
            ppt = (pt_a, pt_b)[dout % 2]
            bppt = (B_pa, B_pb)[dout % 2]
            for sub in range(4):
                E("tensor", lambda g, sub=sub, dout=dout, ppt=ppt: g.transpose(
                    ppt[:, sub * 128:(sub + 1) * 128],
                    x2f[:, (dout % 2) * 512 + sub * 128:(dout % 2) * 512 + (sub + 1) * 128],
                    identf[:]),
                  reads=[bx2, B_identf], writes=[bppt])
            bq1 = B_x1q[dout % 2]
            E("sync", lambda g, nt=nt, dout=dout: g.dma_start(
                out=x1q[:, 1024 + (dout % 2) * 512:1024 + (dout % 2) * 512 + 512].rearrange("p (s c) -> p s c", s=4),
                in_=dx1.ap().rearrange("(n s p) d -> n p s d", s=4, p=128)[nt, :, :,
                                                                          dout * 128:(dout + 1) * 128]),
              reads=[B_dx1[nt * 4], B_dx1[nt * 4 + 1], B_dx1[nt * 4 + 2], B_dx1[nt * 4 + 3]],
              writes=[bq1], dma=True, stream="act")
            bo = B_outt[dout % 2]
            E("vector", lambda g, dout=dout, ppt=ppt: g.tensor_add(
                outt[:, (dout % 2) * 512:(dout % 2) * 512 + 512], ppt[:, 0:512],
                x1q[:, 1024 + (dout % 2) * 512:1024 + (dout % 2) * 512 + 512]),
              reads=[bppt, bq1], writes=[bo])
            E("gpsimd", lambda g, nt=nt, dout=dout: g.dma_start(
                out=dout_view[nt, :, :, dout * 128:(dout + 1) * 128],
                in_=outt[:, (dout % 2) * 512:(dout % 2) * 512 + 512].rearrange("p (s c) -> p s c", s=4)),
              reads=[bo], dma=True, stream="out")
            if nt + 1 < 8:
                pd_quant_chunk(nt + 1, dout)

    if DEBUG:
        for ti in range(NTI):
            E("sync", lambda g, ti=ti: g.dma_start(out=xt_b[:, 0:DM],
                                                   in_=dx1.ap()[ti * 128:(ti + 1) * 128, :]),
              reads=[B_dx1[ti]], writes=[B_xt[0]], dma=True)
            E("gpsimd", lambda g, ti=ti: g.dma_start(out=dbg["d_x1"][ti * 128:(ti + 1) * 128, :],
                                                     in_=xt_b[:, 0:DM]),
              reads=[B_xt[0]], dma=True)
            E("sync", lambda g, ti=ti: g.dma_start(out=xt_b[:, DM:2 * DM],
                                                   in_=dxa.ap()[ti * 128:(ti + 1) * 128, :]),
              reads=[B_dxa[ti]], writes=[B_xt[1]], dma=True)
            E("gpsimd", lambda g, ti=ti: g.dma_start(out=dbg["d_xa"][ti * 128:(ti + 1) * 128, :],
                                                     in_=xt_b[:, DM:2 * DM]),
              reads=[B_xt[1]], dma=True)

    P.barrier()

    P.finalize()

    with es, nc.Block() as block:
        @block.sync
        def _(eng):
            P.replay("sync", eng, esem, dsem, ssem)
            for e in ENGINES:
                n = sum(1 for it in P.items[e] if it.is_dma and it.stream is None)
                if n:
                    eng.wait_ge(dsem[e], 16 * n)
            for k in Prog.STREAMS:
                n = sum(1 for e in ENGINES for it in P.items[e]
                        if it.is_dma and it.stream == k)
                if n:
                    eng.wait_ge(ssem[k], 16 * n)

        @block.scalar
        def _(eng):
            P.replay("scalar", eng, esem, dsem, ssem)

        @block.vector
        def _(eng):
            P.replay("vector", eng, esem, dsem, ssem)

        @block.tensor
        def _(eng):
            P.replay("tensor", eng, esem, dsem, ssem)

        @block.gpsimd
        def _(eng):
            P.replay("gpsimd", eng, esem, dsem, ssem)
            n = sum(1 for it in P.items["gpsimd"] if it.is_dma and it.stream is None)
            eng.wait_ge(dsem["gpsimd"], 16 * n)

    return nc, P


def kernel(L0_attn_x, L0_attn_A_log, L0_attn_D, L0_attn_Wx, L0_attn_dt_W,
           L0_attn_dt_b, L0_ffn_w1, L0_ffn_b1, L0_ffn_w2, L0_ffn_b2):
    bf = ml_dtypes.bfloat16
    x = np.ascontiguousarray(np.asarray(L0_attn_x, dtype=np.float32))
    ident_f = np.eye(128, dtype=np.float32)
    ident_b = np.eye(128).astype(bf)
    ones_b = np.ones((1, 128), dtype=np.float32).astype(bf)
    ones_f = np.ones((1, 128), dtype=np.float32)
    onehot32 = np.zeros((32, 32 * 128), dtype=np.float32)
    for s_ in range(32):
        onehot32[s_, s_ * 128:(s_ + 1) * 128] = 1.0
    onehot32 = onehot32.astype(bf)

    nc, P = build_nc()
    common = dict(
        A_log=np.ascontiguousarray(np.asarray(L0_attn_A_log, dtype=np.float32)),
        D_vec=np.ascontiguousarray(np.asarray(L0_attn_D, dtype=np.float32)),
        Wx=np.ascontiguousarray(np.asarray(L0_attn_Wx, dtype=np.float32)),
        dt_W=np.ascontiguousarray(np.asarray(L0_attn_dt_W, dtype=np.float32)),
        dt_b=np.ascontiguousarray(np.asarray(L0_attn_dt_b, dtype=np.float32)),
        w1=np.ascontiguousarray(np.asarray(L0_ffn_w1, dtype=np.float32)),
        b1=np.ascontiguousarray(np.asarray(L0_ffn_b1, dtype=np.float32)),
        w2=np.ascontiguousarray(np.asarray(L0_ffn_w2, dtype=np.float32)),
        b2=np.ascontiguousarray(np.asarray(L0_ffn_b2, dtype=np.float32)),
        ident_f=ident_f, ident_b=ident_b, ones_b=ones_b,
        ones_f=ones_f, onehot32=onehot32,
    )
    w1f = np.ascontiguousarray(np.asarray(L0_ffn_w1, dtype=np.float32))
    w2f = np.ascontiguousarray(np.asarray(L0_ffn_w2, dtype=np.float32))
    in_maps = [dict(x_in=np.ascontiguousarray(x[i]),
                    w1sl=np.ascontiguousarray(w1f[i * 512:(i + 1) * 512]),
                    w2sl=np.ascontiguousarray(w2f[i * 128:(i + 1) * 128]),
                    **common) for i in range(8)]
    res = run_bass_kernel_spmd(nc, in_maps, core_ids=list(range(8)))
    out = np.stack([np.asarray(res.results[i]["out"], dtype=np.float32) for i in range(8)], axis=0)
    if DEBUG:
        kernel._dbg = res
    return out

